# version 08
# baseline (speedup 1.0000x reference)
import numpy as np
import ml_dtypes
from contextlib import ExitStack
import concourse.bass as bass
import concourse.mybir as mybir
from concourse.bass_utils import run_bass_kernel_spmd

F32 = mybir.dt.float32
BF16 = mybir.dt.bfloat16
U32 = mybir.dt.uint32
AF = mybir.ActivationFunctionType
ALU = mybir.AluOpType
AX = mybir.AxisListType
IOA = bass.IndirectOffsetOnAxis

P = 128
T = 2048
NT = 16
D = 1024
KC = 8
H = 16
E = 8
CL = 640
SLOTS = E * CL
DISP_ROWS = SLOTS + P
CAP = 4096.0
FF = 4096
EPS = 1e-5
SCALE = 64 ** -0.5
NCORE = 8


def build(stage=3, sub=9):
    nc = bass.Bass("TRN2", target_bir_lowering=False, detect_race_conditions=False)
    x_d = nc.dram_tensor("x", [T, D], F32, kind="ExternalInput").ap()
    wq_d = nc.dram_tensor("wq", [D, D], F32, kind="ExternalInput").ap()
    wk_d = nc.dram_tensor("wk", [D, D], F32, kind="ExternalInput").ap()
    wv_d = nc.dram_tensor("wv", [D, D], F32, kind="ExternalInput").ap()
    wgt_d = nc.dram_tensor("wgt", [D, D], F32, kind="ExternalInput").ap()
    wo_d = nc.dram_tensor("wo", [D, D], F32, kind="ExternalInput").ap()
    bgb_d = nc.dram_tensor("bgb", [P, D], F32, kind="ExternalInput").ap()
    wgr_d = nc.dram_tensor("wgr", [D, E], F32, kind="ExternalInput").ap()
    wge_h = nc.dram_tensor("wge", [E * D, FF], F32, kind="ExternalInput")
    wdn_h = nc.dram_tensor("wdn", [E * FF, D], F32, kind="ExternalInput")
    tri_d = nc.dram_tensor("tri", [P, P], BF16, kind="ExternalInput").ap()
    csts_d = nc.dram_tensor("csts", [P, 16], F32, kind="ExternalInput").ap()
    onesf_d = nc.dram_tensor("onesf", [P, P], F32, kind="ExternalInput").ap()
    prefm_d = nc.dram_tensor("prefm", [E, 3], F32, kind="ExternalInput").ap()
    out_d = nc.dram_tensor("out", [T, D], F32, kind="ExternalOutput").ap()

    xin1_dram = nc.dram_tensor("xin1_dram", [T, D], BF16)
    attn_dram = nc.dram_tensor("attn_dram", [T, D], BF16)
    xin2_dram = nc.dram_tensor("xin2_dram", [T, D], BF16)
    xmid_dram = nc.dram_tensor("xmid_dram", [T, D], F32)
    disp_dram = nc.dram_tensor("disp_dram", [DISP_ROWS, D], BF16)
    eo_dram = nc.dram_tensor("eo_dram", [DISP_ROWS, D], BF16)
    kv_in = nc.dram_tensor("kv_in", [P, 8 * 65], F32)
    kv_out = nc.dram_tensor("kv_out", [P, 8 * 65], F32)
    cnt_in = nc.dram_tensor("cnt_in", [1, 16], F32)
    cnt_out = nc.dram_tensor("cnt_out", [E, 16], F32)

    with ExitStack() as st:
        Ez = st.enter_context
        tri = Ez(nc.sbuf_tensor("tri_s", [P, P], BF16))
        csts = Ez(nc.sbuf_tensor("csts_s", [P, 16], F32))
        onesf = Ez(nc.sbuf_tensor("onesf_s", [P, P], F32))
        prefm = Ez(nc.sbuf_tensor("prefm_s", [E, 3], F32))

        junk = Ez(nc.sbuf_tensor("junk", [P, 8], F32))
        neg1 = Ez(nc.sbuf_tensor("neg1", [P, 1], F32))
        zb = Ez(nc.sbuf_tensor("zb", [P, D], BF16))
        wgs = Ez(nc.sbuf_tensor("wgs", [P, KC, E], BF16))
        acta = Ez(nc.sbuf_tensor("acta", [P, KC, T], BF16))
        bst = Ez(nc.sbuf_tensor("bst", [P, 12], F32))
        mv = Ez(nc.sbuf_tensor("mv", [P, 2, 2], F32))
        lt1 = Ez(nc.sbuf_tensor("lt1", [P, 4], F32))
        lt2 = Ez(nc.sbuf_tensor("lt2", [P, 4], F32))
        lt3 = Ez(nc.sbuf_tensor("lt3", [P, 1], F32))
        sdv = Ez(nc.sbuf_tensor("sdv", [P, 2], F32))
        sds = Ez(nc.sbuf_tensor("sds", [P, 2], F32))
        rvv = Ez(nc.sbuf_tensor("rvv", [P, 2], F32))
        xt = Ez(nc.sbuf_tensor("xt", [P, 2, D], F32))
        sc1 = Ez(nc.sbuf_tensor("sc1", [P, D], F32))
        sc2 = Ez(nc.sbuf_tensor("sc2", [P, D], F32))
        sc3 = Ez(nc.sbuf_tensor("sc3", [P, D], F32))
        gsc = Ez(nc.sbuf_tensor("gsc", [P, D], F32))
        tsc = Ez(nc.sbuf_tensor("tsc", [P, 256], F32))
        zq = Ez(nc.sbuf_tensor("zq", [P, H], F32))
        zr = Ez(nc.sbuf_tensor("zr", [P, H], F32))
        lgf = Ez(nc.sbuf_tensor("lgf", [P, NT, E], F32))
        tmaxs = Ez(nc.sbuf_tensor("tmaxs", [P, NT, E], F32))
        tidxs = Ez(nc.sbuf_tensor("tidxs", [P, NT, E], U32))
        i1fa = Ez(nc.sbuf_tensor("i1fa", [P, NT], F32))
        i2fa = Ez(nc.sbuf_tensor("i2fa", [P, NT], F32))
        lda = Ez(nc.sbuf_tensor("lda", [P, NT], F32))
        g1a = Ez(nc.sbuf_tensor("g1a", [P, NT], F32))
        g2a = Ez(nc.sbuf_tensor("g2a", [P, NT], F32))
        m_all = Ez(nc.sbuf_tensor("m_all", [P, 2, NT, E], BF16))
        incl = Ez(nc.sbuf_tensor("incl", [P, 2 * NT * E], F32))
        offa = Ez(nc.sbuf_tensor("offa", [P, 2 * NT * E], F32))
        offb = Ez(nc.sbuf_tensor("offb", [P, 2 * NT * E], F32))
        rsc1 = Ez(nc.sbuf_tensor("rsc1", [P, 2 * NT * E], F32))
        s_all = Ez(nc.sbuf_tensor("s_all", [P, 2, NT], F32))
        kg = Ez(nc.sbuf_tensor("kg", [P, 2, NT], F32))
        ov1 = Ez(nc.sbuf_tensor("ov1", [P, 2, NT], F32))
        slotf = Ez(nc.sbuf_tensor("slotf", [P, 2, NT], F32))
        slosc = Ez(nc.sbuf_tensor("slosc", [P, 2, NT], F32))
        slotu = Ez(nc.sbuf_tensor("slotu", [P, 2, NT], U32))
        cntp = Ez(nc.sbuf_tensor("cntp", [P, 16], F32))
        cnts = Ez(nc.sbuf_tensor("cnts", [E, 16], F32))
        goffr = Ez(nc.sbuf_tensor("goffr", [1, 16], F32))
        w1a = Ez(nc.sbuf_tensor("w1a", [P, NT], F32))
        w2a = Ez(nc.sbuf_tensor("w2a", [P, NT], F32))

        iot8 = csts[:, 0:8]
        trash = csts[:, 8:9]

        def sep(v):
            v.tensor_scalar(junk[:, 0:1], junk[:, 0:1], 1.0, None, ALU.add)

        def mkdrain(eng, sem):
            state = {"n": 0}

            def D(inst):
                state["n"] += 1
                inst.then_inc(sem, 1)
                eng.wait_ge(sem, state["n"])
                return inst
            return D

        def LNSTATS(v, DR, src_ap, tpar, sqbuf):
                        DR(v.tensor_reduce(lt3[:], src_ap, AX.X, ALU.add))
                        DR(v.tensor_scalar(mv[:, tpar, 0:1], lt3[:], 1.0 / D, None, ALU.mult))
                        DR(v.tensor_tensor(sqbuf, src_ap, src_ap, ALU.mult))
                        DR(v.tensor_reduce(lt3[:], sqbuf, AX.X, ALU.add))
                        DR(v.tensor_scalar(lt3[:], lt3[:], 1.0 / D, EPS, ALU.mult,
                                           ALU.add))
                        DR(v.tensor_tensor(lt2[:, 1:2], mv[:, tpar, 0:1],
                                           mv[:, tpar, 0:1], ALU.mult))
                        DR(v.tensor_tensor(mv[:, tpar, 1:2], lt3[:], lt2[:, 1:2],
                                           ALU.subtract))

        with ExitStack() as sA:
            EA = sA.enter_context
            wa = EA(nc.sbuf_tensor("wa", [P, 4, KC, D], BF16))
            ktl = EA(nc.sbuf_tensor("ktl", [P, 2, D], BF16))
            vxt = EA(nc.sbuf_tensor("vxt", [P, 2, D], BF16))
            onebf = EA(nc.sbuf_tensor("onebf", [P, 1], BF16))
            qfm = EA(nc.sbuf_tensor("qfm", [P, 2, KC, 512], BF16))
            gtl = EA(nc.sbuf_tensor("gtl", [P, 2, D], BF16))
            x1s = EA(nc.sbuf_tensor("x1s", [P, 2, D], BF16))
            ats = EA(nc.sbuf_tensor("ats", [P, 2, D], BF16))
            bgb = EA(nc.sbuf_tensor("bgb_s", [P, D], F32))
            kvpk = EA(nc.sbuf_tensor("kvpk", [P, 8, 65], F32))
            kvsum = EA(nc.sbuf_tensor("kvsum", [P, 8, 65], F32))
            kvx = EA(nc.sbuf_tensor("kvx", [P, 8, 65], BF16))

            with ExitStack() as s1:
                E1 = s1.enter_context
                ps_k = E1(nc.psum_tensor("ps_k", [P, 2, D], F32))
                ps_v = E1(nc.psum_tensor("ps_v", [P, D], F32))
                ps_kvA = E1(nc.psum_tensor("ps_kvA", [P, 4, 128], F32))
                ps_kvB = E1(nc.psum_tensor("ps_kvB", [P, 4, 128], F32))
                xsem0 = E1(nc.semaphore("b1_x0"))
                xsem1 = E1(nc.semaphore("b1_x1p"))
                xsem = [xsem0, xsem1]
                csem = E1(nc.semaphore("b1_c"))
                wkvs = E1(nc.semaphore("b1_wkv"))
                wsem = E1(nc.semaphore("b1_w"))
                x1sem = E1(nc.semaphore("b1_x1"))
                stp0 = E1(nc.semaphore("b1_st0"))
                stp1 = E1(nc.semaphore("b1_st1"))
                stp = [stp0, stp1]
                tsem = E1(nc.semaphore("b1_t"))
                v2s = E1(nc.semaphore("b1_v2s"))
                s2v = E1(nc.semaphore("b1_s2v"))
                ksem = E1(nc.semaphore("b1_k"))
                kvrdy = E1(nc.semaphore("b1_kvr"))
                vesem = E1(nc.semaphore("b1_ve"))
                kelsem = E1(nc.semaphore("b1_kel"))
                kvmmd = E1(nc.semaphore("b1_kvm"))
                kvadd = E1(nc.semaphore("b1_kva"))
                kvev = E1(nc.semaphore("b1_kve"))
                gsem = E1(nc.semaphore("b1_g"))
                vdr = E1(nc.semaphore("b1_vdr"))
                zdsem = E1(nc.semaphore("b1_zd"))
                kvin_s = E1(nc.semaphore("b1_kvi"))
                LAG = 3
                blk = E1(nc.Block())

                @blk.sync
                def _(sy):
                    sy.dma_start(out=tri[:], in_=tri_d).then_inc(csem, 16)
                    sy.dma_start(out=csts[:], in_=csts_d).then_inc(csem, 16)
                    sy.dma_start(out=onesf[:], in_=onesf_d).then_inc(csem, 16)
                    sy.dma_start(out=prefm[:], in_=prefm_d).then_inc(csem, 16)
                    sy.dma_start(out=bgb[:], in_=bgb_d).then_inc(csem, 16)
                    def TD(tl):
                        sy.wait_ge(stp[tl % 2], 16 * (tl // 2 + 1))
                        if tl >= 1:
                            sy.wait_ge(tsem, 16 * KC * tl)
                        for kc in range(KC):
                            sy.dma_start(out=acta[:, kc, tl * P:(tl + 1) * P],
                                         in_=xin1_dram[tl * P:(tl + 1) * P,
                                                       kc * P:(kc + 1) * P],
                                         transpose=True).then_inc(tsem, 16)

                    for t in range(NT):
                        if t >= 2:
                            sy.wait_ge(x1sem, t - 1)
                        sy.dma_start(out=xt[:, t % 2, :],
                                     in_=x_d[t * P:(t + 1) * P, :]).then_inc(xsem[t % 2], 16)
                        if t >= 1:
                            sy.wait_ge(x1sem, t)
                            sy.dma_start(out=xin1_dram[(t - 1) * P:t * P, :],
                                         in_=x1s[:, (t - 1) % 2, :]).then_inc(stp[(t - 1) % 2], 16)
                        if t >= 2:
                            TD(t - 2)
                    sy.wait_ge(x1sem, NT)
                    sy.dma_start(out=xin1_dram[(NT - 1) * P:NT * P, :],
                                 in_=x1s[:, (NT - 1) % 2, :]).then_inc(stp[(NT - 1) % 2], 16)
                    TD(NT - 2)
                    TD(NT - 1)
                    sy.wait_ge(gsem, 1)
                    for i in range(DISP_ROWS // P):
                        sy.dma_start(out=disp_dram[i * P:(i + 1) * P, :],
                                     in_=zb[:]).then_inc(zdsem, 16)

                @blk.gpsimd
                def _(g):
                    g.memset(zb[:], 0.0)
                    g.sem_inc(gsem, 1)
                    for i, wsrc in enumerate([wk_d, wv_d, wq_d, wgt_d]):
                        for kc in range(KC):
                            g.dma_start(out=wa[:, i, kc, :],
                                        in_=wsrc[kc * P:(kc + 1) * P, :]
                                        ).then_inc(wkvs if i < 2 else wsem, 16)
                    for kc in range(KC):
                        g.dma_start(out=wgs[:, kc, :],
                                    in_=wgr_d[kc * P:(kc + 1) * P, :]).then_inc(wsem, 16)
                    g.wait_ge(kvev, 1)
                    g.dma_start(out=kv_in.ap(), in_=kvpk[:, :, :]).then_inc(kvin_s, 16)
                    g.wait_ge(kvin_s, 16)

                @blk.vector
                def _(v):
                    DR = mkdrain(v, vdr)
                    DR(v.memset(junk[:], 0.0))
                    DR(v.memset(neg1[:], -1.0))
                    DR(v.memset(onebf[:], 1.0))
                    DR(v.memset(kvpk[:, :, :], 0.0))
                    v.wait_ge(csem, 16 * 5)
                    vskip = stage == 0 and (sub < 1 or sub == 10)
                    kvskip = stage == 0 and (sub < 2 or sub >= 10)

                    def EPI(u):
                        v.wait_ge(ksem, u + 1)
                        v.wait_ge(kelsem, u + 1)
                        v.tensor_tensor(ktl[:, u % 2, :], sc2[:], sc1[:], ALU.add)
                        v.tensor_copy(vxt[:, u % 2, :],
                                      ps_v[:, :]).then_inc(vesem, 1)
                        v.sem_inc(kvrdy, 1)
                        if u >= 1 and not kvskip:
                            v.wait_ge(kvmmd, u)
                            DR(v.tensor_tensor(kvpk[:, 0:4, :], kvpk[:, 0:4, :],
                                               ps_kvA[:, :, 0:65], ALU.add))
                            v.tensor_tensor(kvpk[:, 4:8, :], kvpk[:, 4:8, :],
                                            ps_kvB[:, :, 0:65], ALU.add).then_inc(kvadd, 1)
                            v.wait_ge(kvadd, u)

                    for t in range(NT):
                        v.wait_ge(xsem[t % 2], 16 * (t // 2 + 1))
                        LNSTATS(v, DR, xt[:, t % 2, :], t % 2, gsc[:])
                        v.tensor_copy(sdv[:, t % 2:t % 2 + 1],
                                      mv[:, t % 2, 1:2]).then_inc(v2s, 1)
                        v.wait_ge(s2v, t + 1)
                        DR(v.reciprocal(rvv[:, t % 2:t % 2 + 1], sds[:, t % 2:t % 2 + 1]))
                        if t >= 2:
                            v.wait_ge(stp[t % 2], 16 * (t // 2))
                        v.tensor_scalar(x1s[:, t % 2, :], xt[:, t % 2, :],
                                        mv[:, t % 2, 0:1], rvv[:, t % 2:t % 2 + 1],
                                        ALU.subtract, ALU.mult).then_inc(x1sem, 1)
                        if t >= LAG and not vskip:
                            EPI(t - LAG)
                    if not vskip:
                        for u in range(NT - LAG, NT):
                            EPI(u)
                    if stage == 0 and (sub < 2 or sub >= 10):
                        v.sem_inc(kvev, 1)
                        return
                    v.wait_ge(kvmmd, NT)
                    DR(v.tensor_tensor(kvpk[:, 0:4, :], kvpk[:, 0:4, :],
                                       ps_kvA[:, :, 0:65], ALU.add))
                    v.tensor_tensor(kvpk[:, 4:8, :], kvpk[:, 4:8, :],
                                    ps_kvB[:, :, 0:65], ALU.add).then_inc(kvev, 1)
                    v.wait_ge(kvev, 1)

                @blk.scalar
                def _(s):
                    kskip = stage == 0 and (sub < 1 or sub == 10 or sub >= 12)

                    def KELU(u):
                        s.wait_ge(ksem, u + 1)
                        if u >= 1:
                            s.wait_ge(kvrdy, u)
                        s.activation(sc2[:], ps_k[:, u % 2, :], AF.Relu)
                        s.activation(sc3[:], ps_k[:, u % 2, :], AF.Relu,
                                     scale=neg1[:, 0:1])
                        s.activation(sc1[:], sc3[:], AF.Exp,
                                     scale=neg1[:, 0:1]).then_inc(kelsem, 1)

                    for t in range(NT):
                        s.wait_ge(v2s, t + 1)
                        s.activation(sds[:, t % 2:t % 2 + 1], sdv[:, t % 2:t % 2 + 1],
                                     AF.Sqrt).then_inc(s2v, 1)
                        if t >= LAG and not kskip:
                            KELU(t - LAG)
                    if not kskip:
                        for u in range(NT - LAG, NT):
                            KELU(u)

                @blk.tensor
                def _(tn):
                    if stage == 0 and sub < 1:
                        return
                    for t in range(3):
                        tn.wait_ge(x1sem, t + 1)
                        for _ in range(4):
                            tn.matmul(ps_k[:, 0, 0:128], tri[:, :], tri[:, :],
                                      start=True, stop=True)
                    tn.wait_ge(wkvs, 16 * 16)
                    for t in range(NT):
                        tn.wait_ge(tsem, 16 * KC * (t + 1))
                        if t >= 2 and sub != 10:
                            tn.wait_ge(kvrdy, t - 1)
                        if t >= 1 and sub != 10:
                            tn.wait_ge(vesem, t)
                        for kc in range(KC):
                            st_, sp_ = kc == 0, kc == KC - 1
                            lhs = acta[:, kc, t * P:(t + 1) * P]
                            tn.matmul(ps_k[:, t % 2, 0:512], lhs, wa[:, 0, kc, 0:512],
                                      start=st_, stop=sp_)
                            tn.matmul(ps_k[:, t % 2, 512:1024], lhs, wa[:, 0, kc, 512:1024],
                                      start=st_, stop=sp_)
                            tn.matmul(ps_v[:, 0:512], lhs, wa[:, 1, kc, 0:512],
                                      start=st_, stop=sp_)
                            mm = tn.matmul(ps_v[:, 512:1024], lhs, wa[:, 1, kc, 512:1024],
                                           start=st_, stop=sp_)
                        mm.then_inc(ksem, 1)
                        if t >= 1 and not (stage == 0 and (sub < 2 or sub >= 10)):
                            tn.wait_ge(kvrdy, t)
                            if t >= 2:
                                tn.wait_ge(kvadd, t - 1)
                            for h in range(H):
                                pst = ps_kvA if (h // 2) < 4 else ps_kvB
                                tp = (0, 64) if (h % 2) else None
                                po = (h % 2) * 64
                                blk_i = (h // 2) % 4
                                tn.matmul(pst[po:po + 64, blk_i, 0:64],
                                          ktl[:, (t - 1) % 2, h * 64:(h + 1) * 64],
                                          vxt[:, (t - 1) % 2, h * 64:(h + 1) * 64],
                                          start=True, stop=True, tile_position=tp)
                                mm = tn.matmul(pst[po:po + 64, blk_i, 64:65],
                                               ktl[:, (t - 1) % 2, h * 64:(h + 1) * 64],
                                               onebf[:, :],
                                               start=True, stop=True, tile_position=tp)
                            mm.then_inc(kvmmd, 1)
                    if stage == 0 and (sub < 2 or sub >= 10):
                        return
                    tn.wait_ge(kvrdy, NT)
                    tn.wait_ge(kvadd, NT - 1)
                    for h in range(H):
                        pst = ps_kvA if (h // 2) < 4 else ps_kvB
                        tp = (0, 64) if (h % 2) else None
                        po = (h % 2) * 64
                        blk_i = (h // 2) % 4
                        tn.matmul(pst[po:po + 64, blk_i, 0:64],
                                  ktl[:, (NT - 1) % 2, h * 64:(h + 1) * 64],
                                  vxt[:, (NT - 1) % 2, h * 64:(h + 1) * 64],
                                  start=True, stop=True, tile_position=tp)
                        mm = tn.matmul(pst[po:po + 64, blk_i, 64:65],
                                       ktl[:, (NT - 1) % 2, h * 64:(h + 1) * 64],
                                       onebf[:, :],
                                       start=True, stop=True, tile_position=tp)
                    mm.then_inc(kvmmd, 1)

            if stage == 5:
                pass
            if stage == 0:
                with ExitStack() as s0:
                    E0 = s0.enter_context
                    x1f = E0(nc.sbuf_tensor("x1f", [P, D], F32))
                    ktf = E0(nc.sbuf_tensor("ktf", [P, D], F32))
                    d0 = E0(nc.semaphore("s0_d"))
                    g0 = E0(nc.semaphore("s0_g"))
                    blk0 = E0(nc.Block())

                    @blk0.gpsimd
                    def _(g):
                        g.dma_start(out=ktf[:], in_=xin1_dram[0:P, :]).then_inc(g0, 16)

                    @blk0.vector
                    def _(v):
                        v.tensor_copy(ktf[:], ktl[:, (NT - 1) % 2, :])
                        v.tensor_copy(x1f[:, 0:12], bst[:])
                        v.tensor_copy(x1f[:, 16:20], bass.AP(mv, 0, [[4, P], [1, 4]]))
                        v.tensor_copy(x1f[:, 20:22], sdv[:])
                        v.tensor_copy(x1f[:, 22:24], rvv[:]).then_inc(g0, 1)

                    @blk0.sync
                    def _(sy):
                        sy.wait_ge(g0, 17)
                        sy.dma_start(out=out_d[P:2 * P, :], in_=x1f[:]).then_inc(d0, 16)
                        sy.dma_start(out=out_d[2 * P:3 * P, :], in_=ktf[:]).then_inc(d0, 16)
                        sy.wait_ge(d0, 32)
                return nc

            with ExitStack() as s2:
                E2 = s2.enter_context
                ps_q = E2(nc.psum_tensor("ps_q", [P, 2, 512], F32))
                ps_g = E2(nc.psum_tensor("ps_g", [P, 2, D], F32))
                ps_oA = E2(nc.psum_tensor("ps_oA", [P, 4, 128], F32))
                ps_oB = E2(nc.psum_tensor("ps_oB", [P, 4, 128], F32))
                osb = E2(nc.sbuf_tensor("osb", [P, 8, 128], F32))
                ccsem = E2(nc.semaphore("b2_cc"))
                kvls = E2(nc.semaphore("b2_kvl"))
                kvxs = E2(nc.semaphore("b2_kvx"))
                qp = E2(nc.semaphore("b2_qp"))
                qe = E2(nc.semaphore("b2_qe"))
                gp = E2(nc.semaphore("b2_gp"))
                ge = E2(nc.semaphore("b2_ge"))
                oms = E2(nc.semaphore("b2_om"))
                gat = E2(nc.semaphore("b2_gat"))
                asem0 = E2(nc.semaphore("b2_as0"))
                asem1 = E2(nc.semaphore("b2_as1"))
                asem = [asem0, asem1]
                v2s2 = E2(nc.semaphore("b2_v2s"))
                s2v2 = E2(nc.semaphore("b2_s2v"))
                g2s2 = E2(nc.semaphore("b2_g2s"))
                vdr2 = E2(nc.semaphore("b2_vdr"))
                blk2 = E2(nc.Block())

                @blk2.gpsimd
                def _(g):
                    g.collective_compute(
                        "AllReduce", ALU.add,
                        replica_groups=[[0, 1], [2, 3], [4, 5], [6, 7]],
                        ins=[kv_in.ap().opt()], outs=[kv_out.ap().opt()],
                    ).then_inc(ccsem, 1)
                    g.wait_ge(ccsem, 1)
                    g.dma_start(out=kvsum[:, :, :], in_=kv_out.ap()).then_inc(kvls, 16)

                @blk2.vector
                def _(v):
                    DR = mkdrain(v, vdr2)
                    if stage == 5 and sub == 20:
                        v.wait_ge(kvls, 16)
                        v.tensor_copy(kvx[:, :, :], kvsum[:, :, :]).then_inc(kvxs, 1)
                        return
                    for sl in range(4):
                        if sl == 1:
                            pass
                        for m in range(KC):
                            v.wait_ge(s2v2, sl * 8 + m + 1)
                            v.tensor_tensor(qfm[:, sl % 2, m, :], sc2[:, 0:512],
                                            sc1[:, 0:512], ALU.add).then_inc(qe, 1)
                        if sl == 0:
                            v.wait_ge(kvls, 16)
                            v.tensor_copy(kvx[:, :, :], kvsum[:, :, :]).then_inc(kvxs, 1)
                        if stage == 5 and sub == 21:
                            continue
                        for tt in range(4):
                            t = sl * 4 + tt
                            v.wait_ge(gp, t + 1)
                            if t >= 2 and not (stage == 5 and sub == 22):
                                v.wait_ge(asem[t % 2], 16 * (t // 2))
                            v.tensor_tensor(gsc[:], ps_g[:, t % 2, :], bgb[:],
                                            ALU.add).then_inc(v2s2, 1)
                            if stage == 5 and sub == 22:
                                continue
                            for part in range(2):
                                v.wait_ge(oms, 2 * t + part + 1)
                                if part == 0:
                                    v.wait_ge(g2s2, t + 1)
                                DR(v.tensor_copy(osb[:, 0:4, 0:65], ps_oA[:, :, 0:65]))
                                v.tensor_copy(osb[:, 4:8, 0:65],
                                              ps_oB[:, :, 0:65]).then_inc(gat, 1)
                                v.wait_ge(gat, 4 * t + 2 * part + 1)
                                DR(v.tensor_scalar(zq[:, part * 8:part * 8 + 8],
                                                   bass.AP(osb, 64, [[1024, P], [128, 8]]),
                                                   1e-6, None, ALU.max))
                                DR(v.reciprocal(zr[:, part * 8:part * 8 + 8],
                                                zq[:, part * 8:part * 8 + 8]))
                                DR(v.tensor_scalar(zr[:, part * 8:part * 8 + 8],
                                                   zr[:, part * 8:part * 8 + 8],
                                                   SCALE, None, ALU.mult))
                                for par in range(2):
                                    base = (t % 2) * D + part * 512 + par * 64
                                    DR(v.tensor_tensor(
                                        tsc[:, 0:256],
                                        bass.AP(osb, par * 4 * 128, [[1024, P], [128, 4], [1, 64]]),
                                        bass.AP(zr, part * 8 + par * 4,
                                                [[16, P], [1, 4], [0, 64]]),
                                        ALU.mult))
                                    mmv = v.tensor_tensor(
                                        bass.AP(ats, base, [[2 * D, P], [128, 4], [1, 64]]),
                                        tsc[:, 0:256],
                                        bass.AP(gtl, base, [[2 * D, P], [128, 4], [1, 64]]),
                                        ALU.mult)
                                mmv.then_inc(gat, 1)

                @blk2.scalar
                def _(s):
                    if stage == 5 and sub == 20:
                        return
                    for sl in range(4):
                        for m in range(KC):
                            i = sl * 8 + m
                            s.wait_ge(qp, i + 1)
                            if i >= 1:
                                s.wait_ge(qe, i)
                            s.activation(sc2[:, 0:512], ps_q[:, m % 2, :], AF.Relu)
                            s.activation(sc3[:, 0:512], ps_q[:, m % 2, :], AF.Relu,
                                         scale=neg1[:, 0:1])
                            s.activation(sc1[:, 0:512], sc3[:, 0:512], AF.Exp,
                                         scale=neg1[:, 0:1]).then_inc(s2v2, 1)
                        for tt in range(4):
                            if stage == 5 and sub == 21:
                                break
                            t = sl * 4 + tt
                            s.wait_ge(v2s2, t + 1)
                            s.activation(gtl[:, t % 2, :], gsc[:],
                                         AF.Sigmoid).then_inc(g2s2, 1)

                @blk2.tensor
                def _(tn):
                    if stage == 5 and sub == 20:
                        return
                    for sl in range(4):
                        for m in range(KC):
                            if sl * 8 + m >= 2:
                                tn.wait_ge(qe, sl * 8 + m - 1)
                            for kc in range(KC):
                                st_, sp_ = kc == 0, kc == KC - 1
                                mm = tn.matmul(ps_q[:, m % 2, :],
                                               wa[:, 2, kc, m * P:(m + 1) * P],
                                               acta[:, kc, sl * 512:(sl + 1) * 512],
                                               start=st_, stop=sp_)
                            mm.then_inc(qp, 1)
                        if stage == 5 and sub == 21:
                            continue
                        for tt in range(4):
                            t = sl * 4 + tt
                            if t >= 2:
                                tn.wait_ge(g2s2, t - 1)
                            for kc in range(KC):
                                st_, sp_ = kc == 0, kc == KC - 1
                                lhs = acta[:, kc, t * P:(t + 1) * P]
                                tn.matmul(ps_g[:, t % 2, 0:512], lhs,
                                          wa[:, 3, kc, 0:512], start=st_, stop=sp_)
                                mm = tn.matmul(ps_g[:, t % 2, 512:1024], lhs,
                                               wa[:, 3, kc, 512:1024], start=st_, stop=sp_)
                            mm.then_inc(gp, 1)
                            if stage == 5 and sub == 22:
                                continue
                            if t == 0:
                                tn.wait_ge(kvxs, 1)
                            tn.wait_ge(qe, sl * 8 + 8)
                            for part in range(2):
                                if t > 0 or part > 0:
                                    tn.wait_ge(gat, 4 * t + 2 * part - 1)
                                for par in range(2):
                                    pst = ps_oA if par == 0 else ps_oB
                                    for j in range(4):
                                        h = part * 8 + 2 * j + par
                                        mm = tn.matmul(pst[:, j, 0:65],
                                                       qfm[par * 64:par * 64 + 64,
                                                           sl % 2, h // 2, tt * P:(tt + 1) * P],
                                                       kvx[par * 64:par * 64 + 64, h // 2, :],
                                                       start=True, stop=True)
                                mm.then_inc(oms, 1)

                @blk2.sync
                def _(sy):
                    if stage == 5 and sub in (20, 21, 22):
                        return
                    for t in range(NT):
                        sy.wait_ge(gat, 4 * t + 4)
                        sy.dma_start(out=attn_dram[t * P:(t + 1) * P, :],
                                     in_=ats[:, t % 2, :]).then_inc(asem[t % 2], 16)
                    sy.wait_ge(asem[0], 16 * (NT // 2))
                    sy.wait_ge(asem[1], 16 * (NT // 2))

        if stage == 5:
            with ExitStack() as s5:
                E5 = s5.enter_context
                d5 = E5(nc.semaphore("s5_d"))
                blk5d = E5(nc.Block())

                @blk5d.sync
                def _(sy):
                    sy.dma_start(out=out_d[0:P, 0:520],
                                 in_=bass.AP(kvsum, 0, [[520, P], [1, 520]])).then_inc(d5, 16)
                    sy.wait_ge(d5, 16)
            return nc

        sPre = ExitStack()
        wga = sPre.enter_context(nc.sbuf_tensor("wga", [P, 2, KC, D], BF16))
        wgq = [sPre.enter_context(nc.semaphore(f"wgq{i}")) for i in range(2)]

        def wge_quarter_ap(e, q):
            return bass.AP(wge_h, e * D * FF + q * D,
                           [[FF, P], [P * FF, KC], [1, D]])

        with ExitStack() as sB:
            EB = sB.enter_context
            wo_sb = EB(nc.sbuf_tensor("wo_sb", [P, KC, D], BF16))
            xin2 = EB(nc.sbuf_tensor("xin2", [P, NT, D], BF16))
            x2fm = acta
            xms = EB(nc.sbuf_tensor("xms", [P, 2, D], F32))
            ps_wo = EB(nc.psum_tensor("ps_wo", [P, 2, D], F32))
            ps_lg = EB(nc.psum_tensor("ps_lg", [P, 2, 512], F32))
            ps_pre = EB(nc.psum_tensor("ps_pre", [P, 256], F32))
            ps_off = EB(nc.psum_tensor("ps_off", [P, 256], F32))
            t3sem = EB(nc.semaphore("b3_t"))
            wosem = EB(nc.semaphore("b3_wo"))
            xr0 = EB(nc.semaphore("b3_xr0"))
            xr1 = EB(nc.semaphore("b3_xr1"))
            xrsem = [xr0, xr1]
            wops = EB(nc.semaphore("b3_wop"))
            xmr = EB(nc.semaphore("b3_xmr"))
            xms0 = EB(nc.semaphore("b3_xms0"))
            xms1 = EB(nc.semaphore("b3_xms1"))
            xmst = [xms0, xms1]
            v2s3 = EB(nc.semaphore("b3_v2s"))
            s2v3 = EB(nc.semaphore("b3_s2v"))
            x2r = EB(nc.semaphore("b3_x2r"))
            x2st = EB(nc.semaphore("b3_x2s"))
            t2sem = EB(nc.semaphore("b3_t2"))
            lgp = EB(nc.semaphore("b3_lgp"))
            lgc = EB(nc.semaphore("b3_lgc"))
            mdone = EB(nc.semaphore("b3_md"))
            prep = EB(nc.semaphore("b3_pre"))
            cpd = EB(nc.semaphore("b3_cpd"))
            offr = EB(nc.semaphore("b3_offr"))
            offbs = EB(nc.semaphore("b3_offb"))
            sltr = EB(nc.semaphore("b3_slt"))
            cntr = EB(nc.semaphore("b3_cnt"))
            scat = EB(nc.semaphore("b3_sca"))
            cis = EB(nc.semaphore("b3_cis"))
            ostg1 = EB(nc.semaphore("b3_o1"))
            vdr3 = EB(nc.semaphore("b3_vdr"))
            blk3 = EB(nc.Block())

            @blk3.sync
            def _(sy):
                def TD3(tl):
                    if tl >= 1:
                        sy.wait_ge(t3sem, 16 * KC * tl)
                    for kc in range(KC):
                        sy.dma_start(out=acta[:, kc, tl * P:(tl + 1) * P],
                                     in_=attn_dram[tl * P:(tl + 1) * P,
                                                   kc * P:(kc + 1) * P],
                                     transpose=True).then_inc(t3sem, 16)

                def TD2(tl):
                    sy.wait_ge(x2st, 16 * (tl + 1))
                    if tl >= 1:
                        sy.wait_ge(t2sem, 16 * KC * tl)
                    for kc in range(KC):
                        sy.dma_start(out=x2fm[:, kc, tl * P:(tl + 1) * P],
                                     in_=xin2_dram[tl * P:(tl + 1) * P,
                                                   kc * P:(kc + 1) * P],
                                     transpose=True).then_inc(t2sem, 16)

                for t in range(NT):
                    TD3(t)
                    if t >= 1:
                        sy.wait_ge(xmr, t)
                        sy.dma_start(out=xmid_dram[(t - 1) * P:t * P, :],
                                     in_=xms[:, (t - 1) % 2, :]).then_inc(xmst[(t - 1) % 2], 16)
                        if stage == 1:
                            sy.dma_start(out=out_d[(t - 1) * P:t * P, :],
                                         in_=xms[:, (t - 1) % 2, :]).then_inc(xmst[(t - 1) % 2], 16)
                    if stage != 1 and t >= 2:
                        sy.wait_ge(x2r, t - 1)
                        sy.dma_start(out=xin2_dram[(t - 2) * P:(t - 1) * P, :],
                                     in_=xin2[:, t - 2, :]).then_inc(x2st, 16)
                        TD2(t - 2)
                sy.wait_ge(xmr, NT)
                sy.dma_start(out=xmid_dram[(NT - 1) * P:NT * P, :],
                             in_=xms[:, (NT - 1) % 2, :]).then_inc(xmst[(NT - 1) % 2], 16)
                if stage == 1:
                    sy.dma_start(out=out_d[(NT - 1) * P:NT * P, :],
                                 in_=xms[:, (NT - 1) % 2, :]).then_inc(xmst[(NT - 1) % 2], 16)
                    sy.wait_ge(xmst[0], 32 * (NT // 2))
                    sy.wait_ge(xmst[1], 32 * (NT // 2))
                    return
                for tl in range(NT - 2, NT):
                    sy.wait_ge(x2r, tl + 1)
                    sy.dma_start(out=xin2_dram[tl * P:(tl + 1) * P, :],
                                 in_=xin2[:, tl, :]).then_inc(x2st, 16)
                    TD2(tl)

            @blk3.gpsimd
            def _(g):
                for kc in range(KC):
                    g.dma_start(out=wo_sb[:, kc, :],
                                in_=wo_d[kc * P:(kc + 1) * P, :]).then_inc(wosem, 16)
                for t in range(NT):
                    if t >= 2:
                        g.wait_ge(xmr, t - 1)
                    g.dma_start(out=xt[:, t % 2, :],
                                in_=x_d[t * P:(t + 1) * P, :]).then_inc(xrsem[t % 2], 16)
                if stage == 1:
                    return
                g.dma_start(out=eo_dram[SLOTS:DISP_ROWS, :], in_=zb[:]).then_inc(cis, 16)
                for i in range(2):
                    g.dma_start(out=wga[:, i, :, :],
                                in_=wge_quarter_ap(0, i)).then_inc(wgq[i], 16)
                for j in range(2):
                    g.wait_ge(sltr, j + 1)
                    for t in range(NT):
                        g.indirect_dma_start(
                            out=disp_dram.ap(),
                            out_offset=IOA(slotu[:, j, t:t + 1], 0),
                            in_=xin2[:, t, :], in_offset=None,
                        ).then_inc(scat, 16)
                g.wait_ge(cntr, 1)
                g.dma_start(out=cnt_in.ap(), in_=cntp[0:1, :]).then_inc(cis, 16)
                g.wait_ge(cis, 32)
                g.wait_ge(scat, 16 * 32)

            @blk3.vector
            def _(v):
                DR = mkdrain(v, vdr3)
                xmul = 32 if stage == 1 else 16
                for t in range(NT):
                    v.wait_ge(wops, t + 1)
                    v.wait_ge(xrsem[t % 2], 16 * (t // 2 + 1))
                    if t >= 2:
                        v.wait_ge(xmst[t % 2], xmul * (t // 2))
                    v.tensor_tensor(xms[:, t % 2, :], xt[:, t % 2, :],
                                    ps_wo[:, t % 2, :], ALU.add).then_inc(xmr, 1)
                    v.wait_ge(xmr, t + 1)
                    if stage == 1:
                        continue
                    LNSTATS(v, DR, xms[:, t % 2, :], t % 2, sc1[:])
                    v.tensor_copy(sdv[:, t % 2:t % 2 + 1],
                                  mv[:, t % 2, 1:2]).then_inc(v2s3, 1)
                    v.wait_ge(s2v3, t + 1)
                    DR(v.reciprocal(rvv[:, t % 2:t % 2 + 1], sds[:, t % 2:t % 2 + 1]))
                    v.tensor_scalar(xin2[:, t, :], xms[:, t % 2, :],
                                    mv[:, t % 2, 0:1], rvv[:, t % 2:t % 2 + 1],
                                    ALU.subtract, ALU.mult).then_inc(x2r, 1)
                    if t >= 2:
                        v.wait_ge(lgp, t - 1)
                        v.tensor_copy(lgf[:, t - 2, :],
                                      ps_lg[:, t % 2, 0:E]).then_inc(lgc, 1)
                if stage == 1:
                    return
                for u in range(NT - 2, NT):
                    v.wait_ge(lgp, u + 1)
                    v.tensor_copy(lgf[:, u, :],
                                  ps_lg[:, u % 2, 0:E]).then_inc(lgc, 1)
                v.wait_ge(lgc, NT)
                for t in range(NT):
                    DR(v.max(tmaxs[:, t, :], lgf[:, t, :]))
                for t in range(NT):
                    DR(v.max_index(tidxs[:, t, :], tmaxs[:, t, :], lgf[:, t, :]))
                DR(v.tensor_scalar(i1fa[:], bass.AP(tidxs, 0, [[NT * E, P], [E, NT]]),
                                   0.0, None, ALU.add))
                DR(v.tensor_scalar(i2fa[:], bass.AP(tidxs, 1, [[NT * E, P], [E, NT]]),
                                   0.0, None, ALU.add))
                v.tensor_tensor(lda[:], bass.AP(tmaxs, 0, [[NT * E, P], [E, NT]]),
                                bass.AP(tmaxs, 1, [[NT * E, P], [E, NT]]),
                                ALU.subtract).then_inc(v2s3, 1)
                v.wait_ge(s2v3, NT + 1)
                v.tensor_scalar(g2a[:], g1a[:], -1.0, 1.0, ALU.mult, ALU.add)
                for t in range(NT):
                    v.tensor_scalar(m_all[:, 0, t, :], iot8, i1fa[:, t:t + 1],
                                    None, ALU.is_equal)
                    mi = v.tensor_scalar(m_all[:, 1, t, :], iot8, i2fa[:, t:t + 1],
                                         None, ALU.is_equal)
                    if t == NT - 1:
                        DR(mi)
                v.sem_inc(mdone, 1)
                v.wait_ge(prep, 1)
                v.tensor_copy(incl[:], ps_pre[:]).then_inc(cpd, 1)
                v.wait_ge(prep, 2)
                DR(v.tensor_copy(offa[0:1, :], ps_pre[0:1, :]))
                cur, nxt = offa, offb
                for sh in [8, 16, 32, 64]:
                    for j in range(2):
                        b = j * 128
                        v.tensor_copy(nxt[0:1, b:b + sh], cur[0:1, b:b + sh])
                        DR(v.tensor_tensor(nxt[0:1, b + sh:b + 128],
                                           cur[0:1, b + sh:b + 128],
                                           cur[0:1, b:b + 128 - sh], ALU.add))
                    cur, nxt = nxt, cur
                assert cur is offa
                v.tensor_copy(cntp[0:1, 0:8], cur[0:1, 120:128])
                v.tensor_copy(cntp[0:1, 8:16], cur[0:1, 248:256]).then_inc(cntr, 1)
                v.wait_ge(cntr, 1)
                DR(v.tensor_tensor(cur[0:1, :], cur[0:1, :], ps_pre[0:1, 0:256],
                                   ALU.subtract))
                v.tensor_tensor(bass.AP(cur, 128, [[2 * NT * E, 1], [8, NT], [1, 8]]),
                                bass.AP(cur, 128, [[2 * NT * E, 1], [8, NT], [1, 8]]),
                                bass.AP(cntp, 0, [[16, 1], [0, NT], [1, 8]]),
                                ALU.add).then_inc(offr, 1)
                v.wait_ge(offr, 1)
                v.wait_ge(offbs, 1)
                mflat = bass.AP(m_all, 0, [[256, P], [1, 256]])
                DR(v.tensor_tensor(rsc1[:], incl[:], mflat, ALU.subtract))
                DR(v.tensor_tensor(rsc1[:], rsc1[:], ps_off[:], ALU.add))
                DR(v.tensor_tensor(rsc1[:], rsc1[:], mflat, ALU.mult))
                DR(v.tensor_reduce(bass.AP(s_all, 0, [[2 * NT, P], [1, 2 * NT]]),
                                   bass.AP(rsc1, 0, [[256, P], [8, 2 * NT], [1, 8]]),
                                   AX.X, ALU.add))
                for j in range(2):
                    ifa = i1fa if j == 0 else i2fa
                    DR(v.tensor_scalar(slotf[:, j, :], ifa[:], float(CL), None, ALU.mult))
                    DR(v.tensor_tensor(slotf[:, j, :], slotf[:, j, :], s_all[:, j, :], ALU.add))
                    DR(v.tensor_scalar(ov1[:, j, :], s_all[:, j, :], CL - 0.5, None, ALU.is_gt))
                    DR(v.tensor_scalar(slosc[:, j, :], ov1[:, j, :], -1.0, 1.0,
                                       ALU.mult, ALU.add))
                    DR(v.tensor_tensor(slotf[:, j, :], slotf[:, j, :], slosc[:, j, :], ALU.mult))
                    DR(v.scalar_tensor_tensor(slotf[:, j, :], ov1[:, j, :], trash,
                                              slotf[:, j, :], ALU.mult, ALU.add))
                    DR(v.tensor_scalar(slotu[:, j, :], slotf[:, j, :], 0.0, None, ALU.add))
                    v.sem_inc(sltr, 1)

            @blk3.scalar
            def _(s):
                for t in range(NT):
                    if stage == 1:
                        break
                    s.wait_ge(v2s3, t + 1)
                    s.activation(sds[:, t % 2:t % 2 + 1], sdv[:, t % 2:t % 2 + 1],
                                 AF.Sqrt).then_inc(s2v3, 1)
                if stage >= 2:
                    s.wait_ge(v2s3, NT + 1)
                    s.activation(g1a[:], lda[:], AF.Sigmoid).then_inc(s2v3, 1)

            @blk3.tensor
            def _(tn):
                tn.wait_ge(wosem, 16 * KC)

                def LGT(u):
                    tn.wait_ge(t2sem, 16 * KC * (u + 1))
                    if u >= 2:
                        tn.wait_ge(lgc, u - 1)
                    for kc in range(KC):
                        mm = tn.matmul(ps_lg[:, u % 2, 0:E], x2fm[:, kc, u * P:(u + 1) * P],
                                       wgs[:, kc, :], start=(kc == 0), stop=(kc == KC - 1))
                    mm.then_inc(lgp, 1)

                for t in range(NT):
                    tn.wait_ge(t3sem, 16 * KC * (t + 1))
                    if t >= 2:
                        tn.wait_ge(xmr, t - 1)
                    for kc in range(KC):
                        st_, sp_ = kc == 0, kc == KC - 1
                        lhs = acta[:, kc, t * P:(t + 1) * P]
                        tn.matmul(ps_wo[:, t % 2, 0:512], lhs, wo_sb[:, kc, 0:512],
                                  start=st_, stop=sp_)
                        mm = tn.matmul(ps_wo[:, t % 2, 512:1024], lhs, wo_sb[:, kc, 512:1024],
                                       start=st_, stop=sp_)
                    mm.then_inc(wops, 1)
                    if stage != 1 and t >= 2:
                        LGT(t - 2)
                if stage == 1:
                    return
                for u in range(NT - 2, NT):
                    LGT(u)
                tn.wait_ge(mdone, 1)
                tn.matmul(ps_pre[:], tri[:], bass.AP(m_all, 0, [[256, P], [1, 256]]),
                          start=True, stop=True).then_inc(prep, 1)
                tn.wait_ge(cpd, 1)
                tn.matmul(ps_pre[0:1, :], tri[:, 127:128],
                          bass.AP(m_all, 0, [[256, P], [1, 256]]),
                          start=True, stop=True).then_inc(prep, 1)
                tn.wait_ge(offr, 1)
                tn.matmul(ps_off[:], onesf[0:1, :], offa[0:1, :],
                          start=True, stop=True).then_inc(offbs, 1)
                for i in range(32):
                    tn.wait_ge(scat, 16 * (i + 1))
                    for _ in range(3):
                        tn.matmul(ps_pre[0:128, 0:128], tri[:, :], tri[:, :],
                                  start=True, stop=True)

        if stage == 1:
            return nc

        with ExitStack() as sC:
            EC = sC.enter_context
            wdb = EC(nc.sbuf_tensor("wdb", [P, 4, KC, D], BF16))
            h_sb = EC(nc.sbuf_tensor("h_sb", [P, 32, CL], BF16))
            eosb = EC(nc.sbuf_tensor("eosb", [P, 2, D], BF16))
            ps_h = EC(nc.psum_tensor("ps_h", [P, 2, D], F32))
            ps_eo = EC(nc.psum_tensor("ps_eo", [P, 4, 512], F32))
            ccs2 = EC(nc.semaphore("b4_cc"))
            cntl = EC(nc.semaphore("b4_cl"))
            dspt0 = EC(nc.semaphore("b4_dt0"))
            dspt1 = EC(nc.semaphore("b4_dt1"))
            dspt = [dspt0, dspt1]
            wdqa = EC(nc.semaphore("b4_wda"))
            wdqb = EC(nc.semaphore("b4_wdb"))
            wdqc = EC(nc.semaphore("b4_wdc"))
            wdqd_ = EC(nc.semaphore("b4_wdd"))
            wdq = [wdqa, wdqb, wdqc, wdqd_]
            hps = EC(nc.semaphore("b4_hp"))
            hev = EC(nc.semaphore("b4_he"))
            eops = EC(nc.semaphore("b4_eop"))
            eoev = EC(nc.semaphore("b4_eoe"))
            eost0 = EC(nc.semaphore("b4_eos0"))
            eost1 = EC(nc.semaphore("b4_eos1"))
            eost = [eost0, eost1]
            blk4 = EC(nc.Block())

            @blk4.sync
            def _(sy):
                def eo_stores(e):
                    nonlocal scount
                    for s in range(5):
                        scount += 1
                        sy.wait_ge(eoev, scount)
                        sy.dma_start(
                            out=eo_dram[e * CL + s * P:e * CL + (s + 1) * P, :],
                            in_=eosb[:, scount % 2, :]).then_inc(eost[scount % 2], 16)
                scount = 0
                for e in range(E):
                    if e >= 2:
                        sy.wait_ge(hps, 32 * (e - 1))
                    for kc in range(KC):
                        sy.dma_start(out=acta[:, kc, (e % 2) * CL:(e % 2) * CL + CL],
                                     in_=disp_dram[e * CL:(e + 1) * CL, kc * P:(kc + 1) * P],
                                     transpose=True).then_inc(dspt[e % 2], 16)
                    if e >= 1:
                        eo_stores(e - 1)
                eo_stores(E - 1)

            @blk4.gpsimd
            def _(g):
                g.collective_compute(
                    "AllGather", ALU.bypass,
                    replica_groups=[list(range(NCORE))],
                    ins=[cnt_in.ap().opt()], outs=[cnt_out.ap().opt()],
                ).then_inc(ccs2, 1)
                g.wait_ge(ccs2, 1)
                g.dma_start(out=cnts[:, :], in_=cnt_out.ap()).then_inc(cntl, 16)
                for e in range(E):
                    for kind, idx in [("wg", 0), ("wg", 1), ("wd", 0), ("wd", 1),
                                      ("wg", 2), ("wg", 3), ("wd", 2), ("wd", 3)]:
                        i = 4 * e + idx
                        if kind == "wg":
                            if i < 2:
                                continue
                            g.wait_ge(hps, 8 * (i - 1))
                            g.dma_start(
                                out=wga[:, i % 2, :, :],
                                in_=wge_quarter_ap(e, idx),
                            ).then_inc(wgq[i % 2], 16)
                        else:
                            if i >= 4:
                                g.wait_ge(eops, 3 * ((i - 4) // 4 + 1))
                            g.dma_start(
                                out=wdb[:, i % 4, :, :],
                                in_=bass.AP(wdn_h, e * FF * D + idx * KC * P * D,
                                            [[D, P], [P * D, KC], [1, D]]),
                            ).then_inc(wdq[i % 4], 16)

            @blk4.scalar
            def _(s):
                for e in range(E):
                    for fq in range(4):
                        for m in range(KC):
                            i = e * 32 + fq * 8 + m
                            s.wait_ge(hps, i + 1)
                            s.activation(h_sb[:, fq * 8 + m, :],
                                         ps_h[:, m % 2, 0:CL], AF.Gelu).then_inc(hev, 1)

            @blk4.vector
            def _(v):
                scount = 0
                for e in range(E):
                    for sgi, group in enumerate([[0, 1], [2, 3], [4]]):
                        v.wait_ge(eops, e * 3 + sgi + 1)
                        for s_ in group:
                            scount += 1
                            if scount >= 3:
                                v.wait_ge(eost[scount % 2], 16 * ((scount - 2 + 1) // 2))
                            v.tensor_copy(eosb[:, scount % 2, :],
                                          bass.AP(ps_eo, (2 * s_) % 4 * 512,
                                                  [[2048, P], [1, 1024]])
                                          ).then_inc(eoev, 1)

            @blk4.tensor
            def _(tn):
                for e in range(E):
                    tn.wait_ge(dspt[e % 2], 16 * KC * (e // 2 + 1))
                    for fq in range(4):
                        i = 4 * e + fq
                        tn.wait_ge(wgq[i % 2], 16 * (i // 2 + 1))
                        for m in range(KC):
                            mi = e * 32 + fq * 8 + m
                            if mi >= 2:
                                tn.wait_ge(hev, mi - 1)
                            for kc in range(KC):
                                st_, sp_ = kc == 0, kc == KC - 1
                                lhs = wga[:, i % 2, kc, m * P:(m + 1) * P]
                                tn.matmul(ps_h[:, m % 2, 0:512], lhs,
                                          acta[:, kc, (e % 2) * CL:(e % 2) * CL + 512],
                                          start=st_, stop=sp_)
                                mm = tn.matmul(ps_h[:, m % 2, 512:640], lhs,
                                               acta[:, kc, (e % 2) * CL + 512:(e % 2) * CL + CL],
                                               start=st_, stop=sp_)
                            mm.then_inc(hps, 1)
                    tn.wait_ge(hev, e * 32 + 32)
                    for sgi, group in enumerate([[0, 1], [2, 3], [4]]):
                        gidx = e * 3 + sgi
                        if gidx >= 1:
                            prevfull = (gidx - 1) // 3 * 5 + [0, 2, 4][(gidx - 1) % 3]
                            tn.wait_ge(eoev, prevfull + len([[0, 1], [2, 3], [4]][(gidx - 1) % 3]))
                        for kq in range(4):
                            i = 4 * e + kq
                            tn.wait_ge(wdq[i % 4], 16 * (i // 4 + 1))
                            for kc in range(KC):
                                st_ = (kq == 0 and kc == 0)
                                sp_ = (kq == 3 and kc == KC - 1)
                                for s_ in group:
                                    lhs = h_sb[:, kq * KC + kc, s_ * P:(s_ + 1) * P]
                                    tn.matmul(ps_eo[:, (2 * s_) % 4, :], lhs,
                                              wdb[:, i % 4, kc, 0:512],
                                              start=st_, stop=sp_)
                                    mm = tn.matmul(ps_eo[:, (2 * s_ + 1) % 4, :], lhs,
                                                   wdb[:, i % 4, kc, 512:1024],
                                                   start=st_, stop=sp_)
                        mm.then_inc(eops, 1)

        sPre.close()

        with ExitStack() as sD:
            ED = sD.enter_context
            eog = ED(nc.sbuf_tensor("eog", [P, 4, D], BF16))
            outsb = ED(nc.sbuf_tensor("outsb", [P, 2, D], F32))
            ps_go = ED(nc.psum_tensor("ps_go", [1, 16], F32))
            ps_gb = ED(nc.psum_tensor("ps_gb", [P, 16], F32))
            sc2b = ED(nc.sbuf_tensor("sc2b", [P, 2, D], F32))
            s5sem = ED(nc.semaphore("b5_s5"))
            gos = ED(nc.semaphore("b5_go"))
            gbs = ED(nc.semaphore("b5_gb"))
            gev = ED(nc.semaphore("b5_ge"))
            wrd = ED(nc.semaphore("b5_wr"))
            vdr5 = ED(nc.semaphore("b5_vdr"))
            gth0 = ED(nc.semaphore("b5_gt0"))
            gth1 = ED(nc.semaphore("b5_gt1"))
            gth = [gth0, gth1]
            xml0 = ED(nc.semaphore("b5_xm0"))
            xml1 = ED(nc.semaphore("b5_xm1"))
            xml = [xml0, xml1]
            cmb = ED(nc.semaphore("b5_cm"))
            osem0 = ED(nc.semaphore("b5_os0"))
            osem1 = ED(nc.semaphore("b5_os1"))
            osem = [osem0, osem1]
            blk5 = ED(nc.Block())

            @blk5.tensor
            def _(tn):
                tn.matmul(ps_go[0:1, :], prefm[:, 0:1], cnts[:, :], start=True, stop=False)
                tn.matmul(ps_go[0:1, 8:16], prefm[:, 1:2], cnts[:, 0:8],
                          start=False, stop=False)
                tn.matmul(ps_go[0:1, 8:16], prefm[:, 2:3], cnts[:, 0:8],
                          start=False, stop=True).then_inc(gos, 1)
                tn.wait_ge(gev, 1)
                tn.matmul(ps_gb[:, :], onesf[0:1, :], goffr[0:1, :],
                          start=True, stop=True).then_inc(gbs, 1)

            @blk5.vector
            def _(v):
                DR = mkdrain(v, vdr5)
                v.wait_ge(gos, 1)
                v.tensor_copy(goffr[0:1, :], ps_go[0:1, :]).then_inc(gev, 1)
                v.wait_ge(gbs, 1)
                mflat = bass.AP(m_all, 0, [[256, P], [1, 256]])
                DR(v.tensor_tensor(rsc1[:], mflat,
                                   bass.AP(ps_gb, 0, [[16, P], [8, 2], [0, NT], [1, 8]]),
                                   ALU.mult))
                DR(v.tensor_reduce(bass.AP(kg, 0, [[2 * NT, P], [1, 2 * NT]]),
                                   bass.AP(rsc1, 0, [[256, P], [8, 2 * NT], [1, 8]]),
                                   AX.X, ALU.add))
                DR(v.tensor_tensor(kg[:, :, :], kg[:, :, :], s_all[:, :, :], ALU.add))
                DR(v.tensor_scalar(bass.AP(kg, 0, [[2 * NT, P], [1, 2 * NT]]),
                                   bass.AP(kg, 0, [[2 * NT, P], [1, 2 * NT]]),
                                   CAP - 0.5, None, ALU.is_lt))
                DR(v.tensor_tensor(w1a[:], kg[:, 0, :], g1a[:], ALU.mult))
                v.tensor_tensor(w2a[:], kg[:, 1, :], g2a[:], ALU.mult).then_inc(wrd, 1)
                v.wait_ge(wrd, 1)
                for t in range(NT):
                    v.wait_ge(gth[t % 2], 32 * (t // 2 + 1))
                    v.wait_ge(xml[t % 2], 16 * (t // 2 + 1))
                    if t >= 2:
                        v.wait_ge(osem[t % 2], 16 * (t // 2))
                    v.wait_ge(s5sem, t + 1)
                    DR(v.scalar_tensor_tensor(sc1[:], eog[:, (t % 2) * 2 + 1, :],
                                              w2a[:, t:t + 1], sc2b[:, t % 2, :],
                                              ALU.mult, ALU.add))
                    v.tensor_tensor(outsb[:, t % 2, :], sc1[:], xt[:, t % 2, :],
                                    ALU.add).then_inc(cmb, 1)

            @blk5.scalar
            def _(s):
                s.wait_ge(wrd, 1)
                for t in range(NT):
                    s.wait_ge(gth[t % 2], 32 * (t // 2 + 1))
                    if t >= 2:
                        s.wait_ge(cmb, t - 1)
                    s.activation(sc2b[:, t % 2, :], eog[:, (t % 2) * 2, :],
                                 AF.Copy, scale=w1a[:, t:t + 1]).then_inc(s5sem, 1)

            @blk5.gpsimd
            def _(g):
                g.wait_ge(wrd, 1)
                for t in range(NT):
                    if t >= 2:
                        g.wait_ge(cmb, t - 1)
                    g.indirect_dma_start(
                        out=eog[:, (t % 2) * 2, :], out_offset=None,
                        in_=eo_dram.ap(), in_offset=IOA(slotu[:, 0, t:t + 1], 0),
                    ).then_inc(gth[t % 2], 16)
                    g.indirect_dma_start(
                        out=eog[:, (t % 2) * 2 + 1, :], out_offset=None,
                        in_=eo_dram.ap(), in_offset=IOA(slotu[:, 1, t:t + 1], 0),
                    ).then_inc(gth[t % 2], 16)

            @blk5.sync
            def _(sy):
                for t in range(NT):
                    if t >= 2:
                        sy.wait_ge(cmb, t - 1)
                    sy.dma_start(out=xt[:, t % 2, :],
                                 in_=xmid_dram[t * P:(t + 1) * P, :]).then_inc(xml[t % 2], 16)
                    if t >= 1:
                        sy.wait_ge(cmb, t)
                        sy.dma_start(out=out_d[(t - 1) * P:t * P, :],
                                     in_=outsb[:, (t - 1) % 2, :]).then_inc(osem[(t - 1) % 2], 16)
                sy.wait_ge(cmb, NT)
                sy.dma_start(out=out_d[(NT - 1) * P:NT * P, :],
                             in_=outsb[:, (NT - 1) % 2, :]).then_inc(osem[(NT - 1) % 2], 16)
                sy.wait_ge(osem[0], 16 * (NT // 2))
                sy.wait_ge(osem[1], 16 * (NT // 2))

    return nc


_NC_CACHE = {}


def kernel(L0_attn_x, L0_attn_Wq, L0_attn_Wk, L0_attn_Wv, L0_attn_Wg, L0_attn_bg,
           L0_attn_Wo, L0_ffn_wg, L0_ffn_w_gate, L0_ffn_w_down, stage=3, sub=9):
    x = np.ascontiguousarray(np.asarray(L0_attn_x, dtype=np.float32).reshape(4 * 4096, D))
    wq = np.ascontiguousarray(np.asarray(L0_attn_Wq, dtype=np.float32))
    wk = np.ascontiguousarray(np.asarray(L0_attn_Wk, dtype=np.float32))
    wv = np.ascontiguousarray(np.asarray(L0_attn_Wv, dtype=np.float32))
    wgt = np.ascontiguousarray(np.asarray(L0_attn_Wg, dtype=np.float32))
    wo = np.ascontiguousarray(np.asarray(L0_attn_Wo, dtype=np.float32))
    bg = np.asarray(L0_attn_bg, dtype=np.float32)
    wgr = np.ascontiguousarray(np.asarray(L0_ffn_wg, dtype=np.float32))
    wge = np.ascontiguousarray(np.asarray(L0_ffn_w_gate, dtype=np.float32).reshape(E * D, FF))
    wdn = np.ascontiguousarray(np.asarray(L0_ffn_w_down, dtype=np.float32).reshape(E * FF, D))
    bgb = np.ascontiguousarray(np.broadcast_to(bg[None, :], (P, D)))
    tri = np.ascontiguousarray(
        (np.arange(P)[:, None] <= np.arange(P)[None, :]).astype(ml_dtypes.bfloat16))
    csts = np.zeros((P, 16), dtype=np.float32)
    csts[:, 0:8] = np.arange(8, dtype=np.float32)[None, :]
    csts[:, 8] = SLOTS + np.arange(P, dtype=np.float32)
    onesf = np.ones((P, P), dtype=np.float32)

    if (stage, sub) not in _NC_CACHE:
        _NC_CACHE[(stage, sub)] = build(stage, sub)
    nc = _NC_CACHE[(stage, sub)]
    in_maps = []
    for c in range(NCORE):
        prefm = np.zeros((E, 3), dtype=np.float32)
        prefm[:, 0] = (np.arange(E) < c).astype(np.float32)
        prefm[:, 1] = 1.0
        prefm[:, 2] = -((np.arange(E) == c).astype(np.float32))
        in_maps.append(dict(
            x=np.ascontiguousarray(x[c * T:(c + 1) * T, :]),
            wq=wq, wk=wk, wv=wv, wgt=wgt, wo=wo, bgb=bgb, wgr=wgr,
            wge=wge, wdn=wdn, tri=tri, csts=csts, onesf=onesf, prefm=prefm,
        ))
    res = run_bass_kernel_spmd(nc, in_maps, core_ids=list(range(NCORE)))
    out = np.concatenate([res.results[c]["out"] for c in range(NCORE)], axis=0)
    return out.reshape(4, 4096, D)


# version 09
# speedup vs baseline: 1.0181x; 1.0181x over previous
import numpy as np
import ml_dtypes
from contextlib import ExitStack
import concourse.bass as bass
import concourse.mybir as mybir
from concourse.bass_utils import run_bass_kernel_spmd

F32 = mybir.dt.float32
BF16 = mybir.dt.bfloat16
U32 = mybir.dt.uint32
AF = mybir.ActivationFunctionType
ALU = mybir.AluOpType
AX = mybir.AxisListType
IOA = bass.IndirectOffsetOnAxis

P = 128
T = 2048
NT = 16
D = 1024
KC = 8
H = 16
E = 8
CL = 640
SLOTS = E * CL
DISP_ROWS = SLOTS + P
CAP = 4096.0
FF = 4096
EPS = 1e-5
SCALE = 64 ** -0.5
NCORE = 8


def build(stage=3, sub=9):
    nc = bass.Bass("TRN2", target_bir_lowering=False, detect_race_conditions=False)
    x_d = nc.dram_tensor("x", [T, D], F32, kind="ExternalInput").ap()
    wq_d = nc.dram_tensor("wq", [D, D], F32, kind="ExternalInput").ap()
    wk_d = nc.dram_tensor("wk", [D, D], F32, kind="ExternalInput").ap()
    wv_d = nc.dram_tensor("wv", [D, D], F32, kind="ExternalInput").ap()
    wgt_d = nc.dram_tensor("wgt", [D, D], F32, kind="ExternalInput").ap()
    wo_d = nc.dram_tensor("wo", [D, D], F32, kind="ExternalInput").ap()
    bgb_d = nc.dram_tensor("bgb", [P, D], F32, kind="ExternalInput").ap()
    wgr_d = nc.dram_tensor("wgr", [D, E], F32, kind="ExternalInput").ap()
    wge_h = nc.dram_tensor("wge", [E * D, FF], F32, kind="ExternalInput")
    wdn_h = nc.dram_tensor("wdn", [E * FF, D], F32, kind="ExternalInput")
    tri_d = nc.dram_tensor("tri", [P, P], BF16, kind="ExternalInput").ap()
    csts_d = nc.dram_tensor("csts", [P, 16], F32, kind="ExternalInput").ap()
    onesf_d = nc.dram_tensor("onesf", [P, P], F32, kind="ExternalInput").ap()
    prefm_d = nc.dram_tensor("prefm", [E, 3], F32, kind="ExternalInput").ap()
    out_d = nc.dram_tensor("out", [T, D], F32, kind="ExternalOutput").ap()

    xin1_dram = nc.dram_tensor("xin1_dram", [T, D], BF16)
    attn_dram = nc.dram_tensor("attn_dram", [T, D], BF16)
    xin2_dram = nc.dram_tensor("xin2_dram", [T, D], BF16)
    xmid_dram = nc.dram_tensor("xmid_dram", [T, D], F32)
    disp_dram = nc.dram_tensor("disp_dram", [DISP_ROWS, D], BF16)
    eo_dram = nc.dram_tensor("eo_dram", [DISP_ROWS, D], BF16)
    kv_in = nc.dram_tensor("kv_in", [P, 8 * 65], F32)
    kv_out = nc.dram_tensor("kv_out", [P, 8 * 65], F32)
    cnt_in = nc.dram_tensor("cnt_in", [1, 16], F32)
    cnt_out = nc.dram_tensor("cnt_out", [E, 16], F32)

    with ExitStack() as st:
        Ez = st.enter_context
        tri = Ez(nc.sbuf_tensor("tri_s", [P, P], BF16))
        csts = Ez(nc.sbuf_tensor("csts_s", [P, 16], F32))
        onesf = Ez(nc.sbuf_tensor("onesf_s", [P, P], F32))
        prefm = Ez(nc.sbuf_tensor("prefm_s", [E, 3], F32))

        junk = Ez(nc.sbuf_tensor("junk", [P, 8], F32))
        neg1 = Ez(nc.sbuf_tensor("neg1", [P, 1], F32))
        zb = Ez(nc.sbuf_tensor("zb", [P, D], BF16))
        wgs = Ez(nc.sbuf_tensor("wgs", [P, KC, E], BF16))
        acta = Ez(nc.sbuf_tensor("acta", [P, KC, T], BF16))
        bst = Ez(nc.sbuf_tensor("bst", [P, 12], F32))
        mv = Ez(nc.sbuf_tensor("mv", [P, 2, 2], F32))
        lt1 = Ez(nc.sbuf_tensor("lt1", [P, 4], F32))
        lt2 = Ez(nc.sbuf_tensor("lt2", [P, 4], F32))
        lt3 = Ez(nc.sbuf_tensor("lt3", [P, 1], F32))
        sdv = Ez(nc.sbuf_tensor("sdv", [P, 2], F32))
        sds = Ez(nc.sbuf_tensor("sds", [P, 2], F32))
        rvv = Ez(nc.sbuf_tensor("rvv", [P, 2], F32))
        xt = Ez(nc.sbuf_tensor("xt", [P, 2, D], F32))
        sc1 = Ez(nc.sbuf_tensor("sc1", [P, D], F32))
        sc2 = Ez(nc.sbuf_tensor("sc2", [P, D], F32))
        sc3 = Ez(nc.sbuf_tensor("sc3", [P, D], F32))
        gsc = Ez(nc.sbuf_tensor("gsc", [P, D], F32))
        tsc = Ez(nc.sbuf_tensor("tsc", [P, 256], F32))
        zq = Ez(nc.sbuf_tensor("zq", [P, H], F32))
        zr = Ez(nc.sbuf_tensor("zr", [P, H], F32))
        lgf = Ez(nc.sbuf_tensor("lgf", [P, NT, E], F32))
        tmaxs = Ez(nc.sbuf_tensor("tmaxs", [P, NT, E], F32))
        tidxs = Ez(nc.sbuf_tensor("tidxs", [P, NT, E], U32))
        i1fa = Ez(nc.sbuf_tensor("i1fa", [P, NT], F32))
        i2fa = Ez(nc.sbuf_tensor("i2fa", [P, NT], F32))
        lda = Ez(nc.sbuf_tensor("lda", [P, NT], F32))
        g1a = Ez(nc.sbuf_tensor("g1a", [P, NT], F32))
        g2a = Ez(nc.sbuf_tensor("g2a", [P, NT], F32))
        m_all = Ez(nc.sbuf_tensor("m_all", [P, 2, NT, E], BF16))
        incl = Ez(nc.sbuf_tensor("incl", [P, 2 * NT * E], F32))
        offa = Ez(nc.sbuf_tensor("offa", [P, 2 * NT * E], F32))
        offb = Ez(nc.sbuf_tensor("offb", [P, 2 * NT * E], F32))
        rsc1 = Ez(nc.sbuf_tensor("rsc1", [P, 2 * NT * E], F32))
        s_all = Ez(nc.sbuf_tensor("s_all", [P, 2, NT], F32))
        kg = Ez(nc.sbuf_tensor("kg", [P, 2, NT], F32))
        ov1 = Ez(nc.sbuf_tensor("ov1", [P, 2, NT], F32))
        slotf = Ez(nc.sbuf_tensor("slotf", [P, 2, NT], F32))
        slosc = Ez(nc.sbuf_tensor("slosc", [P, 2, NT], F32))
        slotu = Ez(nc.sbuf_tensor("slotu", [P, 2, NT], U32))
        cntp = Ez(nc.sbuf_tensor("cntp", [P, 16], F32))
        cnts = Ez(nc.sbuf_tensor("cnts", [E, 16], F32))
        goffr = Ez(nc.sbuf_tensor("goffr", [1, 16], F32))
        w1a = Ez(nc.sbuf_tensor("w1a", [P, NT], F32))
        w2a = Ez(nc.sbuf_tensor("w2a", [P, NT], F32))

        iot8 = csts[:, 0:8]
        trash = csts[:, 8:9]

        def sep(v):
            v.tensor_scalar(junk[:, 0:1], junk[:, 0:1], 1.0, None, ALU.add)

        def mkdrain(eng, sem):
            state = {"n": 0}

            def D(inst):
                state["n"] += 1
                inst.then_inc(sem, 1)
                eng.wait_ge(sem, state["n"])
                return inst
            return D

        def LNSTATS(v, DR, src_ap, tpar, sqbuf):
                        DR(v.tensor_reduce(lt3[:], src_ap, AX.X, ALU.add))
                        DR(v.tensor_scalar(mv[:, tpar, 0:1], lt3[:], 1.0 / D, None, ALU.mult))
                        DR(v.tensor_tensor(sqbuf, src_ap, src_ap, ALU.mult))
                        DR(v.tensor_reduce(lt3[:], sqbuf, AX.X, ALU.add))
                        DR(v.tensor_scalar(lt3[:], lt3[:], 1.0 / D, EPS, ALU.mult,
                                           ALU.add))
                        DR(v.tensor_tensor(lt2[:, 1:2], mv[:, tpar, 0:1],
                                           mv[:, tpar, 0:1], ALU.mult))
                        DR(v.tensor_tensor(mv[:, tpar, 1:2], lt3[:], lt2[:, 1:2],
                                           ALU.subtract))

        with ExitStack() as sA:
            EA = sA.enter_context
            wa = EA(nc.sbuf_tensor("wa", [P, 4, KC, D], BF16))
            ktl = EA(nc.sbuf_tensor("ktl", [P, 2, D], BF16))
            vxt = EA(nc.sbuf_tensor("vxt", [P, 2, D], BF16))
            onebf = EA(nc.sbuf_tensor("onebf", [P, 1], BF16))
            qfm = EA(nc.sbuf_tensor("qfm", [P, 2, KC, 512], BF16))
            gtl = EA(nc.sbuf_tensor("gtl", [P, 2, D], BF16))
            x1s = EA(nc.sbuf_tensor("x1s", [P, 2, D], BF16))
            ats = EA(nc.sbuf_tensor("ats", [P, 2, D], BF16))
            bgb = EA(nc.sbuf_tensor("bgb_s", [P, D], F32))
            kvpk = EA(nc.sbuf_tensor("kvpk", [P, 8, 65], F32))
            kvsum = EA(nc.sbuf_tensor("kvsum", [P, 8, 65], F32))
            kvx = EA(nc.sbuf_tensor("kvx", [P, 8, 65], BF16))

            with ExitStack() as s1:
                E1 = s1.enter_context
                ps_k = E1(nc.psum_tensor("ps_k", [P, 2, D], F32))
                ps_v = E1(nc.psum_tensor("ps_v", [P, D], F32))
                ps_kvA = E1(nc.psum_tensor("ps_kvA", [P, 4, 128], F32))
                ps_kvB = E1(nc.psum_tensor("ps_kvB", [P, 4, 128], F32))
                xsem0 = E1(nc.semaphore("b1_x0"))
                xsem1 = E1(nc.semaphore("b1_x1p"))
                xsem = [xsem0, xsem1]
                csem = E1(nc.semaphore("b1_c"))
                wkvs = E1(nc.semaphore("b1_wkv"))
                wsem = E1(nc.semaphore("b1_w"))
                x1sem = E1(nc.semaphore("b1_x1"))
                stp0 = E1(nc.semaphore("b1_st0"))
                stp1 = E1(nc.semaphore("b1_st1"))
                stp = [stp0, stp1]
                tsem = E1(nc.semaphore("b1_t"))
                v2s = E1(nc.semaphore("b1_v2s"))
                s2v = E1(nc.semaphore("b1_s2v"))
                ksem = E1(nc.semaphore("b1_k"))
                kvrdy = E1(nc.semaphore("b1_kvr"))
                vesem = E1(nc.semaphore("b1_ve"))
                kelsem = E1(nc.semaphore("b1_kel"))
                kvmmd = E1(nc.semaphore("b1_kvm"))
                kvadd = E1(nc.semaphore("b1_kva"))
                kvev = E1(nc.semaphore("b1_kve"))
                gsem = E1(nc.semaphore("b1_g"))
                vdr = E1(nc.semaphore("b1_vdr"))
                zdsem = E1(nc.semaphore("b1_zd"))
                kvin_s = E1(nc.semaphore("b1_kvi"))
                LAG = 3
                blk = E1(nc.Block())

                @blk.sync
                def _(sy):
                    sy.dma_start(out=tri[:], in_=tri_d).then_inc(csem, 16)
                    sy.dma_start(out=csts[:], in_=csts_d).then_inc(csem, 16)
                    sy.dma_start(out=onesf[:], in_=onesf_d).then_inc(csem, 16)
                    sy.dma_start(out=prefm[:], in_=prefm_d).then_inc(csem, 16)
                    sy.dma_start(out=bgb[:], in_=bgb_d).then_inc(csem, 16)
                    def TD(tl):
                        sy.wait_ge(stp[tl % 2], 16 * (tl // 2 + 1))
                        if tl >= 1:
                            sy.wait_ge(tsem, 16 * KC * tl)
                        for kc in range(KC):
                            sy.dma_start(out=acta[:, kc, tl * P:(tl + 1) * P],
                                         in_=xin1_dram[tl * P:(tl + 1) * P,
                                                       kc * P:(kc + 1) * P],
                                         transpose=True).then_inc(tsem, 16)

                    for t in range(NT):
                        if t >= 2:
                            sy.wait_ge(x1sem, t - 1)
                        sy.dma_start(out=xt[:, t % 2, :],
                                     in_=x_d[t * P:(t + 1) * P, :]).then_inc(xsem[t % 2], 16)
                        if t >= 1:
                            sy.wait_ge(x1sem, t)
                            sy.dma_start(out=xin1_dram[(t - 1) * P:t * P, :],
                                         in_=x1s[:, (t - 1) % 2, :]).then_inc(stp[(t - 1) % 2], 16)
                        if t >= 2:
                            TD(t - 2)
                    sy.wait_ge(x1sem, NT)
                    sy.dma_start(out=xin1_dram[(NT - 1) * P:NT * P, :],
                                 in_=x1s[:, (NT - 1) % 2, :]).then_inc(stp[(NT - 1) % 2], 16)
                    TD(NT - 2)
                    TD(NT - 1)
                    sy.wait_ge(gsem, 1)
                    for i in range(DISP_ROWS // P):
                        sy.dma_start(out=disp_dram[i * P:(i + 1) * P, :],
                                     in_=zb[:]).then_inc(zdsem, 16)

                @blk.gpsimd
                def _(g):
                    g.memset(zb[:], 0.0)
                    g.sem_inc(gsem, 1)
                    for i, wsrc in enumerate([wk_d, wv_d, wq_d, wgt_d]):
                        for kc in range(KC):
                            g.dma_start(out=wa[:, i, kc, :],
                                        in_=wsrc[kc * P:(kc + 1) * P, :]
                                        ).then_inc(wkvs if i < 2 else wsem, 16)
                    for kc in range(KC):
                        g.dma_start(out=wgs[:, kc, :],
                                    in_=wgr_d[kc * P:(kc + 1) * P, :]).then_inc(wsem, 16)
                    g.wait_ge(kvev, 1)
                    g.dma_start(out=kv_in.ap(), in_=kvpk[:, :, :]).then_inc(kvin_s, 16)
                    g.wait_ge(kvin_s, 16)

                @blk.vector
                def _(v):
                    DR = mkdrain(v, vdr)
                    DR(v.memset(junk[:], 0.0))
                    DR(v.memset(neg1[:], -1.0))
                    DR(v.memset(onebf[:], 1.0))
                    DR(v.memset(kvpk[:, :, :], 0.0))
                    v.wait_ge(csem, 16 * 5)
                    vskip = stage == 0 and (sub < 1 or sub == 10)
                    kvskip = stage == 0 and (sub < 2 or sub >= 10)

                    def EPI(u):
                        v.wait_ge(ksem, u + 1)
                        v.wait_ge(kelsem, u + 1)
                        v.tensor_tensor(ktl[:, u % 2, :], sc2[:], sc1[:], ALU.add)
                        v.tensor_copy(vxt[:, u % 2, :],
                                      ps_v[:, :]).then_inc(vesem, 1)
                        v.sem_inc(kvrdy, 1)
                        if u >= 1 and not kvskip:
                            v.wait_ge(kvmmd, u)
                            DR(v.tensor_tensor(kvpk[:, 0:4, :], kvpk[:, 0:4, :],
                                               ps_kvA[:, :, 0:65], ALU.add))
                            v.tensor_tensor(kvpk[:, 4:8, :], kvpk[:, 4:8, :],
                                            ps_kvB[:, :, 0:65], ALU.add).then_inc(kvadd, 1)
                            v.wait_ge(kvadd, u)

                    for t in range(NT):
                        v.wait_ge(xsem[t % 2], 16 * (t // 2 + 1))
                        LNSTATS(v, DR, xt[:, t % 2, :], t % 2, gsc[:])
                        v.tensor_copy(sdv[:, t % 2:t % 2 + 1],
                                      mv[:, t % 2, 1:2]).then_inc(v2s, 1)
                        v.wait_ge(s2v, t + 1)
                        DR(v.reciprocal(rvv[:, t % 2:t % 2 + 1], sds[:, t % 2:t % 2 + 1]))
                        if t >= 2:
                            v.wait_ge(stp[t % 2], 16 * (t // 2))
                        v.tensor_scalar(x1s[:, t % 2, :], xt[:, t % 2, :],
                                        mv[:, t % 2, 0:1], rvv[:, t % 2:t % 2 + 1],
                                        ALU.subtract, ALU.mult).then_inc(x1sem, 1)
                        if t >= LAG and not vskip:
                            EPI(t - LAG)
                    if not vskip:
                        for u in range(NT - LAG, NT):
                            EPI(u)
                    if stage == 0 and (sub < 2 or sub >= 10):
                        v.sem_inc(kvev, 1)
                        return
                    v.wait_ge(kvmmd, NT)
                    DR(v.tensor_tensor(kvpk[:, 0:4, :], kvpk[:, 0:4, :],
                                       ps_kvA[:, :, 0:65], ALU.add))
                    v.tensor_tensor(kvpk[:, 4:8, :], kvpk[:, 4:8, :],
                                    ps_kvB[:, :, 0:65], ALU.add).then_inc(kvev, 1)
                    v.wait_ge(kvev, 1)

                @blk.scalar
                def _(s):
                    kskip = stage == 0 and (sub < 1 or sub == 10 or sub >= 12)

                    def KELU(u):
                        s.wait_ge(ksem, u + 1)
                        if u >= 1:
                            s.wait_ge(kvrdy, u)
                        s.activation(sc2[:], ps_k[:, u % 2, :], AF.Relu)
                        s.activation(sc3[:], ps_k[:, u % 2, :], AF.Relu,
                                     scale=neg1[:, 0:1])
                        s.activation(sc1[:], sc3[:], AF.Exp,
                                     scale=neg1[:, 0:1]).then_inc(kelsem, 1)

                    for t in range(NT):
                        s.wait_ge(v2s, t + 1)
                        s.activation(sds[:, t % 2:t % 2 + 1], sdv[:, t % 2:t % 2 + 1],
                                     AF.Sqrt).then_inc(s2v, 1)
                        if t >= LAG and not kskip:
                            KELU(t - LAG)
                    if not kskip:
                        for u in range(NT - LAG, NT):
                            KELU(u)

                @blk.tensor
                def _(tn):
                    if stage == 0 and sub < 1:
                        return
                    for t in range(3):
                        tn.wait_ge(x1sem, t + 1)
                        for _ in range(4):
                            tn.matmul(ps_k[:, 0, 0:128], tri[:, :], tri[:, :],
                                      start=True, stop=True)
                    tn.wait_ge(wkvs, 16 * 16)
                    for t in range(NT):
                        tn.wait_ge(tsem, 16 * KC * (t + 1))
                        if t >= 2 and sub != 10:
                            tn.wait_ge(kvrdy, t - 1)
                        if t >= 1 and sub != 10:
                            tn.wait_ge(vesem, t)
                        for kc in range(KC):
                            st_, sp_ = kc == 0, kc == KC - 1
                            lhs = acta[:, kc, t * P:(t + 1) * P]
                            tn.matmul(ps_k[:, t % 2, 0:512], lhs, wa[:, 0, kc, 0:512],
                                      start=st_, stop=sp_)
                            tn.matmul(ps_k[:, t % 2, 512:1024], lhs, wa[:, 0, kc, 512:1024],
                                      start=st_, stop=sp_)
                            tn.matmul(ps_v[:, 0:512], lhs, wa[:, 1, kc, 0:512],
                                      start=st_, stop=sp_)
                            mm = tn.matmul(ps_v[:, 512:1024], lhs, wa[:, 1, kc, 512:1024],
                                           start=st_, stop=sp_)
                        mm.then_inc(ksem, 1)
                        if t >= 1 and not (stage == 0 and (sub < 2 or sub >= 10)):
                            tn.wait_ge(kvrdy, t)
                            if t >= 2:
                                tn.wait_ge(kvadd, t - 1)
                            for h in range(H):
                                pst = ps_kvA if (h // 2) < 4 else ps_kvB
                                tp = (0, 64) if (h % 2) else None
                                po = (h % 2) * 64
                                blk_i = (h // 2) % 4
                                tn.matmul(pst[po:po + 64, blk_i, 0:64],
                                          ktl[:, (t - 1) % 2, h * 64:(h + 1) * 64],
                                          vxt[:, (t - 1) % 2, h * 64:(h + 1) * 64],
                                          start=True, stop=True, tile_position=tp)
                                mm = tn.matmul(pst[po:po + 64, blk_i, 64:65],
                                               ktl[:, (t - 1) % 2, h * 64:(h + 1) * 64],
                                               onebf[:, :],
                                               start=True, stop=True, tile_position=tp)
                            mm.then_inc(kvmmd, 1)
                    if stage == 0 and (sub < 2 or sub >= 10):
                        return
                    tn.wait_ge(kvrdy, NT)
                    tn.wait_ge(kvadd, NT - 1)
                    for h in range(H):
                        pst = ps_kvA if (h // 2) < 4 else ps_kvB
                        tp = (0, 64) if (h % 2) else None
                        po = (h % 2) * 64
                        blk_i = (h // 2) % 4
                        tn.matmul(pst[po:po + 64, blk_i, 0:64],
                                  ktl[:, (NT - 1) % 2, h * 64:(h + 1) * 64],
                                  vxt[:, (NT - 1) % 2, h * 64:(h + 1) * 64],
                                  start=True, stop=True, tile_position=tp)
                        mm = tn.matmul(pst[po:po + 64, blk_i, 64:65],
                                       ktl[:, (NT - 1) % 2, h * 64:(h + 1) * 64],
                                       onebf[:, :],
                                       start=True, stop=True, tile_position=tp)
                    mm.then_inc(kvmmd, 1)

            if stage == 5:
                pass
            if stage == 0:
                with ExitStack() as s0:
                    E0 = s0.enter_context
                    x1f = E0(nc.sbuf_tensor("x1f", [P, D], F32))
                    ktf = E0(nc.sbuf_tensor("ktf", [P, D], F32))
                    d0 = E0(nc.semaphore("s0_d"))
                    g0 = E0(nc.semaphore("s0_g"))
                    blk0 = E0(nc.Block())

                    @blk0.gpsimd
                    def _(g):
                        g.dma_start(out=ktf[:], in_=xin1_dram[0:P, :]).then_inc(g0, 16)

                    @blk0.vector
                    def _(v):
                        v.tensor_copy(ktf[:], ktl[:, (NT - 1) % 2, :])
                        v.tensor_copy(x1f[:, 0:12], bst[:])
                        v.tensor_copy(x1f[:, 16:20], bass.AP(mv, 0, [[4, P], [1, 4]]))
                        v.tensor_copy(x1f[:, 20:22], sdv[:])
                        v.tensor_copy(x1f[:, 22:24], rvv[:]).then_inc(g0, 1)

                    @blk0.sync
                    def _(sy):
                        sy.wait_ge(g0, 17)
                        sy.dma_start(out=out_d[P:2 * P, :], in_=x1f[:]).then_inc(d0, 16)
                        sy.dma_start(out=out_d[2 * P:3 * P, :], in_=ktf[:]).then_inc(d0, 16)
                        sy.wait_ge(d0, 32)
                return nc

            with ExitStack() as s2:
                E2 = s2.enter_context
                ps_q = E2(nc.psum_tensor("ps_q", [P, 2, 512], F32))
                ps_g = E2(nc.psum_tensor("ps_g", [P, 2, D], F32))
                ps_oA = E2(nc.psum_tensor("ps_oA", [P, 4, 128], F32))
                ps_oB = E2(nc.psum_tensor("ps_oB", [P, 4, 128], F32))
                osb = E2(nc.sbuf_tensor("osb", [P, 8, 128], F32))
                ccsem = E2(nc.semaphore("b2_cc"))
                kvls = E2(nc.semaphore("b2_kvl"))
                kvxs = E2(nc.semaphore("b2_kvx"))
                qp = E2(nc.semaphore("b2_qp"))
                qe = E2(nc.semaphore("b2_qe"))
                gp = E2(nc.semaphore("b2_gp"))
                ge = E2(nc.semaphore("b2_ge"))
                oms = E2(nc.semaphore("b2_om"))
                gat = E2(nc.semaphore("b2_gat"))
                asem0 = E2(nc.semaphore("b2_as0"))
                asem1 = E2(nc.semaphore("b2_as1"))
                asem = [asem0, asem1]
                v2s2 = E2(nc.semaphore("b2_v2s"))
                s2v2 = E2(nc.semaphore("b2_s2v"))
                g2s2 = E2(nc.semaphore("b2_g2s"))
                vdr2 = E2(nc.semaphore("b2_vdr"))
                blk2 = E2(nc.Block())

                @blk2.gpsimd
                def _(g):
                    g.collective_compute(
                        "AllReduce", ALU.add,
                        replica_groups=[[0, 1], [2, 3], [4, 5], [6, 7]],
                        ins=[kv_in.ap().opt()], outs=[kv_out.ap().opt()],
                    ).then_inc(ccsem, 1)
                    g.wait_ge(ccsem, 1)
                    g.dma_start(out=kvsum[:, :, :], in_=kv_out.ap()).then_inc(kvls, 16)

                @blk2.vector
                def _(v):
                    DR = mkdrain(v, vdr2)
                    if stage == 5 and sub == 20:
                        v.wait_ge(kvls, 16)
                        v.tensor_copy(kvx[:, :, :], kvsum[:, :, :]).then_inc(kvxs, 1)
                        return
                    for sl in range(4):
                        if sl == 1:
                            pass
                        for m in range(KC):
                            v.wait_ge(s2v2, sl * 8 + m + 1)
                            v.tensor_tensor(qfm[:, sl % 2, m, :], sc2[:, 0:512],
                                            sc1[:, 0:512], ALU.add).then_inc(qe, 1)
                        if sl == 0:
                            v.wait_ge(kvls, 16)
                            v.tensor_copy(kvx[:, :, :], kvsum[:, :, :]).then_inc(kvxs, 1)
                        if stage == 5 and sub == 21:
                            continue
                        for tt in range(4):
                            t = sl * 4 + tt
                            v.wait_ge(gp, t + 1)
                            if t >= 2 and not (stage == 5 and sub == 22):
                                v.wait_ge(asem[t % 2], 16 * (t // 2))
                            v.tensor_tensor(gsc[:], ps_g[:, t % 2, :], bgb[:],
                                            ALU.add).then_inc(v2s2, 1)
                            if stage == 5 and sub == 22:
                                continue
                            for part in range(2):
                                v.wait_ge(oms, 2 * t + part + 1)
                                if part == 0:
                                    v.wait_ge(g2s2, t + 1)
                                DR(v.tensor_copy(osb[:, 0:4, 0:65], ps_oA[:, :, 0:65]))
                                v.tensor_copy(osb[:, 4:8, 0:65],
                                              ps_oB[:, :, 0:65]).then_inc(gat, 1)
                                v.wait_ge(gat, 4 * t + 2 * part + 1)
                                DR(v.tensor_scalar(zq[:, part * 8:part * 8 + 8],
                                                   bass.AP(osb, 64, [[1024, P], [128, 8]]),
                                                   1e-6, None, ALU.max))
                                DR(v.reciprocal(zr[:, part * 8:part * 8 + 8],
                                                zq[:, part * 8:part * 8 + 8]))
                                DR(v.tensor_scalar(zr[:, part * 8:part * 8 + 8],
                                                   zr[:, part * 8:part * 8 + 8],
                                                   SCALE, None, ALU.mult))
                                for par in range(2):
                                    base = (t % 2) * D + part * 512 + par * 64
                                    DR(v.tensor_tensor(
                                        tsc[:, 0:256],
                                        bass.AP(osb, par * 4 * 128, [[1024, P], [128, 4], [1, 64]]),
                                        bass.AP(zr, part * 8 + par * 4,
                                                [[16, P], [1, 4], [0, 64]]),
                                        ALU.mult))
                                    mmv = v.tensor_tensor(
                                        bass.AP(ats, base, [[2 * D, P], [128, 4], [1, 64]]),
                                        tsc[:, 0:256],
                                        bass.AP(gtl, base, [[2 * D, P], [128, 4], [1, 64]]),
                                        ALU.mult)
                                mmv.then_inc(gat, 1)

                @blk2.scalar
                def _(s):
                    if stage == 5 and sub == 20:
                        return
                    for sl in range(4):
                        for m in range(KC):
                            i = sl * 8 + m
                            s.wait_ge(qp, i + 1)
                            if i >= 1:
                                s.wait_ge(qe, i)
                            s.activation(sc2[:, 0:512], ps_q[:, m % 2, :], AF.Relu)
                            s.activation(sc3[:, 0:512], ps_q[:, m % 2, :], AF.Relu,
                                         scale=neg1[:, 0:1])
                            s.activation(sc1[:, 0:512], sc3[:, 0:512], AF.Exp,
                                         scale=neg1[:, 0:1]).then_inc(s2v2, 1)
                        for tt in range(4):
                            if stage == 5 and sub == 21:
                                break
                            t = sl * 4 + tt
                            s.wait_ge(v2s2, t + 1)
                            s.activation(gtl[:, t % 2, :], gsc[:],
                                         AF.Sigmoid).then_inc(g2s2, 1)

                @blk2.tensor
                def _(tn):
                    if stage == 5 and sub == 20:
                        return
                    for sl in range(4):
                        for m in range(KC):
                            if sl * 8 + m >= 2:
                                tn.wait_ge(qe, sl * 8 + m - 1)
                            for kc in range(KC):
                                st_, sp_ = kc == 0, kc == KC - 1
                                mm = tn.matmul(ps_q[:, m % 2, :],
                                               wa[:, 2, kc, m * P:(m + 1) * P],
                                               acta[:, kc, sl * 512:(sl + 1) * 512],
                                               start=st_, stop=sp_)
                            mm.then_inc(qp, 1)
                        if stage == 5 and sub == 21:
                            continue
                        for tt in range(4):
                            t = sl * 4 + tt
                            if t >= 2:
                                tn.wait_ge(g2s2, t - 1)
                            for kc in range(KC):
                                st_, sp_ = kc == 0, kc == KC - 1
                                lhs = acta[:, kc, t * P:(t + 1) * P]
                                tn.matmul(ps_g[:, t % 2, 0:512], lhs,
                                          wa[:, 3, kc, 0:512], start=st_, stop=sp_)
                                mm = tn.matmul(ps_g[:, t % 2, 512:1024], lhs,
                                               wa[:, 3, kc, 512:1024], start=st_, stop=sp_)
                            mm.then_inc(gp, 1)
                            if stage == 5 and sub == 22:
                                continue
                            if t == 0:
                                tn.wait_ge(kvxs, 1)
                            tn.wait_ge(qe, sl * 8 + 8)
                            for part in range(2):
                                if t > 0 or part > 0:
                                    tn.wait_ge(gat, 4 * t + 2 * part - 1)
                                for par in range(2):
                                    pst = ps_oA if par == 0 else ps_oB
                                    for j in range(4):
                                        h = part * 8 + 2 * j + par
                                        mm = tn.matmul(pst[:, j, 0:65],
                                                       qfm[par * 64:par * 64 + 64,
                                                           sl % 2, h // 2, tt * P:(tt + 1) * P],
                                                       kvx[par * 64:par * 64 + 64, h // 2, :],
                                                       start=True, stop=True)
                                mm.then_inc(oms, 1)

                @blk2.sync
                def _(sy):
                    if stage == 5 and sub in (20, 21, 22):
                        return
                    for t in range(NT):
                        sy.wait_ge(gat, 4 * t + 4)
                        sy.dma_start(out=attn_dram[t * P:(t + 1) * P, :],
                                     in_=ats[:, t % 2, :]).then_inc(asem[t % 2], 16)
                    sy.wait_ge(asem[0], 16 * (NT // 2))
                    sy.wait_ge(asem[1], 16 * (NT // 2))

        if stage == 5:
            with ExitStack() as s5:
                E5 = s5.enter_context
                d5 = E5(nc.semaphore("s5_d"))
                blk5d = E5(nc.Block())

                @blk5d.sync
                def _(sy):
                    sy.dma_start(out=out_d[0:P, 0:520],
                                 in_=bass.AP(kvsum, 0, [[520, P], [1, 520]])).then_inc(d5, 16)
                    sy.wait_ge(d5, 16)
            return nc

        sPre = ExitStack()
        wga = sPre.enter_context(nc.sbuf_tensor("wga", [P, 2, KC, D], BF16))
        wgq = [sPre.enter_context(nc.semaphore(f"wgq{i}")) for i in range(2)]

        def wge_quarter_ap(e, q):
            return bass.AP(wge_h, e * D * FF + q * D,
                           [[FF, P], [P * FF, KC], [1, D]])

        with ExitStack() as sB:
            EB = sB.enter_context
            wo_sb = EB(nc.sbuf_tensor("wo_sb", [P, KC, D], BF16))
            xin2 = EB(nc.sbuf_tensor("xin2", [P, NT, D], BF16))
            x2fm = acta
            xms = EB(nc.sbuf_tensor("xms", [P, 2, D], F32))
            ps_wo = EB(nc.psum_tensor("ps_wo", [P, 2, D], F32))
            ps_lg = EB(nc.psum_tensor("ps_lg", [P, 2, 512], F32))
            ps_pre = EB(nc.psum_tensor("ps_pre", [P, 256], F32))
            ps_off = EB(nc.psum_tensor("ps_off", [P, 256], F32))
            t3sem = EB(nc.semaphore("b3_t"))
            wosem = EB(nc.semaphore("b3_wo"))
            xr0 = EB(nc.semaphore("b3_xr0"))
            xr1 = EB(nc.semaphore("b3_xr1"))
            xrsem = [xr0, xr1]
            wops = EB(nc.semaphore("b3_wop"))
            xmr = EB(nc.semaphore("b3_xmr"))
            xms0 = EB(nc.semaphore("b3_xms0"))
            xms1 = EB(nc.semaphore("b3_xms1"))
            xmst = [xms0, xms1]
            v2s3 = EB(nc.semaphore("b3_v2s"))
            s2v3 = EB(nc.semaphore("b3_s2v"))
            x2r = EB(nc.semaphore("b3_x2r"))
            x2st = EB(nc.semaphore("b3_x2s"))
            x2sp = [EB(nc.semaphore("b3_x2p0")), EB(nc.semaphore("b3_x2p1"))]
            t2sem = EB(nc.semaphore("b3_t2"))
            lgp = EB(nc.semaphore("b3_lgp"))
            lgc = EB(nc.semaphore("b3_lgc"))
            mdone = EB(nc.semaphore("b3_md"))
            prep = EB(nc.semaphore("b3_pre"))
            cpd = EB(nc.semaphore("b3_cpd"))
            offr = EB(nc.semaphore("b3_offr"))
            offbs = EB(nc.semaphore("b3_offb"))
            sltr = EB(nc.semaphore("b3_slt"))
            cntr = EB(nc.semaphore("b3_cnt"))
            scat = EB(nc.semaphore("b3_sca"))
            cis = EB(nc.semaphore("b3_cis"))
            ostg1 = EB(nc.semaphore("b3_o1"))
            vdr3 = EB(nc.semaphore("b3_vdr"))
            blk3 = EB(nc.Block())

            @blk3.sync
            def _(sy):
                def TD3(tl):
                    if tl >= 1:
                        sy.wait_ge(t3sem, 16 * KC * tl)
                    for kc in range(KC):
                        sy.dma_start(out=acta[:, kc, tl * P:(tl + 1) * P],
                                     in_=attn_dram[tl * P:(tl + 1) * P,
                                                   kc * P:(kc + 1) * P],
                                     transpose=True).then_inc(t3sem, 16)

                def TD2(tl):
                    sy.wait_ge(x2sp[tl % 2], 16 * (tl // 2 + 1))
                    if tl >= 1:
                        sy.wait_ge(t2sem, 16 * KC * tl)
                    for kc in range(KC):
                        sy.dma_start(out=x2fm[:, kc, tl * P:(tl + 1) * P],
                                     in_=xin2_dram[tl * P:(tl + 1) * P,
                                                   kc * P:(kc + 1) * P],
                                     transpose=True).then_inc(t2sem, 16)

                for t in range(NT):
                    TD3(t)
                    if t >= 1:
                        sy.wait_ge(xmr, t)
                        sy.dma_start(out=xmid_dram[(t - 1) * P:t * P, :],
                                     in_=xms[:, (t - 1) % 2, :]).then_inc(xmst[(t - 1) % 2], 16)
                        if stage == 1:
                            sy.dma_start(out=out_d[(t - 1) * P:t * P, :],
                                         in_=xms[:, (t - 1) % 2, :]).then_inc(xmst[(t - 1) % 2], 16)
                    if stage != 1 and t >= 2:
                        sy.wait_ge(x2r, t - 1)
                        sy.dma_start(out=xin2_dram[(t - 2) * P:(t - 1) * P, :],
                                     in_=xin2[:, t - 2, :]).then_inc(x2sp[(t - 2) % 2], 16)
                        if t >= 3:
                            TD2(t - 3)
                sy.wait_ge(xmr, NT)
                sy.dma_start(out=xmid_dram[(NT - 1) * P:NT * P, :],
                             in_=xms[:, (NT - 1) % 2, :]).then_inc(xmst[(NT - 1) % 2], 16)
                if stage == 1:
                    sy.dma_start(out=out_d[(NT - 1) * P:NT * P, :],
                                 in_=xms[:, (NT - 1) % 2, :]).then_inc(xmst[(NT - 1) % 2], 16)
                    sy.wait_ge(xmst[0], 32 * (NT // 2))
                    sy.wait_ge(xmst[1], 32 * (NT // 2))
                    return
                sy.wait_ge(x2r, NT - 1)
                sy.dma_start(out=xin2_dram[(NT - 2) * P:(NT - 1) * P, :],
                             in_=xin2[:, NT - 2, :]).then_inc(x2sp[(NT - 2) % 2], 16)
                TD2(NT - 3)
                sy.wait_ge(x2r, NT)
                sy.dma_start(out=xin2_dram[(NT - 1) * P:NT * P, :],
                             in_=xin2[:, NT - 1, :]).then_inc(x2sp[(NT - 1) % 2], 16)
                TD2(NT - 2)
                TD2(NT - 1)

            @blk3.gpsimd
            def _(g):
                for kc in range(KC):
                    g.dma_start(out=wo_sb[:, kc, :],
                                in_=wo_d[kc * P:(kc + 1) * P, :]).then_inc(wosem, 16)
                for t in range(NT):
                    if t >= 2:
                        g.wait_ge(xmr, t - 1)
                    g.dma_start(out=xt[:, t % 2, :],
                                in_=x_d[t * P:(t + 1) * P, :]).then_inc(xrsem[t % 2], 16)
                if stage == 1:
                    return
                g.dma_start(out=eo_dram[SLOTS:DISP_ROWS, :], in_=zb[:]).then_inc(cis, 16)
                for i in range(2):
                    g.dma_start(out=wga[:, i, :, :],
                                in_=wge_quarter_ap(0, i)).then_inc(wgq[i], 16)
                for j in range(2):
                    g.wait_ge(sltr, j + 1)
                    for t in range(NT):
                        g.indirect_dma_start(
                            out=disp_dram.ap(),
                            out_offset=IOA(slotu[:, j, t:t + 1], 0),
                            in_=xin2[:, t, :], in_offset=None,
                        ).then_inc(scat, 16)
                g.wait_ge(cntr, 1)
                g.dma_start(out=cnt_in.ap(), in_=cntp[0:1, :]).then_inc(cis, 16)
                g.wait_ge(cis, 32)
                g.wait_ge(scat, 16 * 32)

            @blk3.vector
            def _(v):
                DR = mkdrain(v, vdr3)
                xmul = 32 if stage == 1 else 16
                for t in range(NT):
                    v.wait_ge(wops, t + 1)
                    v.wait_ge(xrsem[t % 2], 16 * (t // 2 + 1))
                    if t >= 2:
                        v.wait_ge(xmst[t % 2], xmul * (t // 2))
                    v.tensor_tensor(xms[:, t % 2, :], xt[:, t % 2, :],
                                    ps_wo[:, t % 2, :], ALU.add).then_inc(xmr, 1)
                    v.wait_ge(xmr, t + 1)
                    if stage == 1:
                        continue
                    LNSTATS(v, DR, xms[:, t % 2, :], t % 2, sc1[:])
                    v.tensor_copy(sdv[:, t % 2:t % 2 + 1],
                                  mv[:, t % 2, 1:2]).then_inc(v2s3, 1)
                    v.wait_ge(s2v3, t + 1)
                    DR(v.reciprocal(rvv[:, t % 2:t % 2 + 1], sds[:, t % 2:t % 2 + 1]))
                    v.tensor_scalar(xin2[:, t, :], xms[:, t % 2, :],
                                    mv[:, t % 2, 0:1], rvv[:, t % 2:t % 2 + 1],
                                    ALU.subtract, ALU.mult).then_inc(x2r, 1)
                if stage == 1:
                    return
                for u in range(NT):
                    v.wait_ge(lgp, u + 1)
                    v.tensor_copy(lgf[:, u, :],
                                  ps_lg[:, u % 2, 0:E]).then_inc(lgc, 1)
                v.wait_ge(lgc, NT)
                for t in range(NT):
                    DR(v.max(tmaxs[:, t, :], lgf[:, t, :]))
                for t in range(NT):
                    DR(v.max_index(tidxs[:, t, :], tmaxs[:, t, :], lgf[:, t, :]))
                DR(v.tensor_scalar(i1fa[:], bass.AP(tidxs, 0, [[NT * E, P], [E, NT]]),
                                   0.0, None, ALU.add))
                DR(v.tensor_scalar(i2fa[:], bass.AP(tidxs, 1, [[NT * E, P], [E, NT]]),
                                   0.0, None, ALU.add))
                v.tensor_tensor(lda[:], bass.AP(tmaxs, 0, [[NT * E, P], [E, NT]]),
                                bass.AP(tmaxs, 1, [[NT * E, P], [E, NT]]),
                                ALU.subtract).then_inc(v2s3, 1)
                v.wait_ge(s2v3, NT + 1)
                v.tensor_scalar(g2a[:], g1a[:], -1.0, 1.0, ALU.mult, ALU.add)
                for t in range(NT):
                    v.tensor_scalar(m_all[:, 0, t, :], iot8, i1fa[:, t:t + 1],
                                    None, ALU.is_equal)
                    mi = v.tensor_scalar(m_all[:, 1, t, :], iot8, i2fa[:, t:t + 1],
                                         None, ALU.is_equal)
                    if t == NT - 1:
                        DR(mi)
                v.sem_inc(mdone, 1)
                v.wait_ge(prep, 1)
                v.tensor_copy(incl[:], ps_pre[:]).then_inc(cpd, 1)
                v.wait_ge(prep, 2)
                DR(v.tensor_copy(offa[0:1, :], ps_pre[0:1, :]))
                cur, nxt = offa, offb
                for sh in [8, 16, 32, 64]:
                    for j in range(2):
                        b = j * 128
                        v.tensor_copy(nxt[0:1, b:b + sh], cur[0:1, b:b + sh])
                        DR(v.tensor_tensor(nxt[0:1, b + sh:b + 128],
                                           cur[0:1, b + sh:b + 128],
                                           cur[0:1, b:b + 128 - sh], ALU.add))
                    cur, nxt = nxt, cur
                assert cur is offa
                v.tensor_copy(cntp[0:1, 0:8], cur[0:1, 120:128])
                v.tensor_copy(cntp[0:1, 8:16], cur[0:1, 248:256]).then_inc(cntr, 1)
                v.wait_ge(cntr, 1)
                DR(v.tensor_tensor(cur[0:1, :], cur[0:1, :], ps_pre[0:1, 0:256],
                                   ALU.subtract))
                v.tensor_tensor(bass.AP(cur, 128, [[2 * NT * E, 1], [8, NT], [1, 8]]),
                                bass.AP(cur, 128, [[2 * NT * E, 1], [8, NT], [1, 8]]),
                                bass.AP(cntp, 0, [[16, 1], [0, NT], [1, 8]]),
                                ALU.add).then_inc(offr, 1)
                v.wait_ge(offr, 1)
                v.wait_ge(offbs, 1)
                mflat = bass.AP(m_all, 0, [[256, P], [1, 256]])
                DR(v.tensor_tensor(rsc1[:], incl[:], mflat, ALU.subtract))
                DR(v.tensor_tensor(rsc1[:], rsc1[:], ps_off[:], ALU.add))
                DR(v.tensor_tensor(rsc1[:], rsc1[:], mflat, ALU.mult))
                DR(v.tensor_reduce(bass.AP(s_all, 0, [[2 * NT, P], [1, 2 * NT]]),
                                   bass.AP(rsc1, 0, [[256, P], [8, 2 * NT], [1, 8]]),
                                   AX.X, ALU.add))
                for j in range(2):
                    ifa = i1fa if j == 0 else i2fa
                    DR(v.tensor_scalar(slotf[:, j, :], ifa[:], float(CL), None, ALU.mult))
                    DR(v.tensor_tensor(slotf[:, j, :], slotf[:, j, :], s_all[:, j, :], ALU.add))
                    DR(v.tensor_scalar(ov1[:, j, :], s_all[:, j, :], CL - 0.5, None, ALU.is_gt))
                    DR(v.tensor_scalar(slosc[:, j, :], ov1[:, j, :], -1.0, 1.0,
                                       ALU.mult, ALU.add))
                    DR(v.tensor_tensor(slotf[:, j, :], slotf[:, j, :], slosc[:, j, :], ALU.mult))
                    DR(v.scalar_tensor_tensor(slotf[:, j, :], ov1[:, j, :], trash,
                                              slotf[:, j, :], ALU.mult, ALU.add))
                    DR(v.tensor_scalar(slotu[:, j, :], slotf[:, j, :], 0.0, None, ALU.add))
                    v.sem_inc(sltr, 1)

            @blk3.scalar
            def _(s):
                for t in range(NT):
                    if stage == 1:
                        break
                    s.wait_ge(v2s3, t + 1)
                    s.activation(sds[:, t % 2:t % 2 + 1], sdv[:, t % 2:t % 2 + 1],
                                 AF.Sqrt).then_inc(s2v3, 1)
                if stage >= 2:
                    s.wait_ge(v2s3, NT + 1)
                    s.activation(g1a[:], lda[:], AF.Sigmoid).then_inc(s2v3, 1)

            @blk3.tensor
            def _(tn):
                tn.wait_ge(wosem, 16 * KC)

                def LGT(u):
                    tn.wait_ge(t2sem, 16 * KC * (u + 1))
                    if u >= 2:
                        tn.wait_ge(lgc, u - 1)
                    for kc in range(KC):
                        mm = tn.matmul(ps_lg[:, u % 2, 0:E], x2fm[:, kc, u * P:(u + 1) * P],
                                       wgs[:, kc, :], start=(kc == 0), stop=(kc == KC - 1))
                    mm.then_inc(lgp, 1)

                for t in range(NT):
                    tn.wait_ge(t3sem, 16 * KC * (t + 1))
                    if t >= 2:
                        tn.wait_ge(xmr, t - 1)
                    for kc in range(KC):
                        st_, sp_ = kc == 0, kc == KC - 1
                        lhs = acta[:, kc, t * P:(t + 1) * P]
                        tn.matmul(ps_wo[:, t % 2, 0:512], lhs, wo_sb[:, kc, 0:512],
                                  start=st_, stop=sp_)
                        mm = tn.matmul(ps_wo[:, t % 2, 512:1024], lhs, wo_sb[:, kc, 512:1024],
                                       start=st_, stop=sp_)
                    mm.then_inc(wops, 1)
                if stage == 1:
                    return
                for u in range(NT):
                    LGT(u)
                tn.wait_ge(mdone, 1)
                tn.matmul(ps_pre[:], tri[:], bass.AP(m_all, 0, [[256, P], [1, 256]]),
                          start=True, stop=True).then_inc(prep, 1)
                tn.wait_ge(cpd, 1)
                tn.matmul(ps_pre[0:1, :], tri[:, 127:128],
                          bass.AP(m_all, 0, [[256, P], [1, 256]]),
                          start=True, stop=True).then_inc(prep, 1)
                tn.wait_ge(offr, 1)
                tn.matmul(ps_off[:], onesf[0:1, :], offa[0:1, :],
                          start=True, stop=True).then_inc(offbs, 1)
                for i in range(32):
                    tn.wait_ge(scat, 16 * (i + 1))
                    for _ in range(3):
                        tn.matmul(ps_pre[0:128, 0:128], tri[:, :], tri[:, :],
                                  start=True, stop=True)

        if stage == 1:
            return nc

        with ExitStack() as sC:
            EC = sC.enter_context
            wdb = EC(nc.sbuf_tensor("wdb", [P, 4, KC, D], BF16))
            h_sb = EC(nc.sbuf_tensor("h_sb", [P, 32, CL], BF16))
            eosb = EC(nc.sbuf_tensor("eosb", [P, 2, D], BF16))
            ps_h = EC(nc.psum_tensor("ps_h", [P, 2, D], F32))
            ps_eo = EC(nc.psum_tensor("ps_eo", [P, 4, 512], F32))
            ccs2 = EC(nc.semaphore("b4_cc"))
            cntl = EC(nc.semaphore("b4_cl"))
            dspt0 = EC(nc.semaphore("b4_dt0"))
            dspt1 = EC(nc.semaphore("b4_dt1"))
            dspt = [dspt0, dspt1]
            wdqa = EC(nc.semaphore("b4_wda"))
            wdqb = EC(nc.semaphore("b4_wdb"))
            wdqc = EC(nc.semaphore("b4_wdc"))
            wdqd_ = EC(nc.semaphore("b4_wdd"))
            wdq = [wdqa, wdqb, wdqc, wdqd_]
            hps = EC(nc.semaphore("b4_hp"))
            hev = EC(nc.semaphore("b4_he"))
            eops = EC(nc.semaphore("b4_eop"))
            eoev = EC(nc.semaphore("b4_eoe"))
            eost0 = EC(nc.semaphore("b4_eos0"))
            eost1 = EC(nc.semaphore("b4_eos1"))
            eost = [eost0, eost1]
            blk4 = EC(nc.Block())

            @blk4.sync
            def _(sy):
                def eo_stores(e):
                    nonlocal scount
                    for s in range(5):
                        scount += 1
                        sy.wait_ge(eoev, scount)
                        sy.dma_start(
                            out=eo_dram[e * CL + s * P:e * CL + (s + 1) * P, :],
                            in_=eosb[:, scount % 2, :]).then_inc(eost[scount % 2], 16)
                scount = 0
                for e in range(E):
                    if e >= 2:
                        sy.wait_ge(hps, 32 * (e - 1))
                    for kc in range(KC):
                        sy.dma_start(out=acta[:, kc, (e % 2) * CL:(e % 2) * CL + CL],
                                     in_=disp_dram[e * CL:(e + 1) * CL, kc * P:(kc + 1) * P],
                                     transpose=True).then_inc(dspt[e % 2], 16)
                    if e >= 1:
                        eo_stores(e - 1)
                eo_stores(E - 1)

            @blk4.gpsimd
            def _(g):
                g.collective_compute(
                    "AllGather", ALU.bypass,
                    replica_groups=[list(range(NCORE))],
                    ins=[cnt_in.ap().opt()], outs=[cnt_out.ap().opt()],
                ).then_inc(ccs2, 1)
                g.wait_ge(ccs2, 1)
                g.dma_start(out=cnts[:, :], in_=cnt_out.ap()).then_inc(cntl, 16)
                for e in range(E):
                    for kind, idx in [("wg", 0), ("wg", 1), ("wd", 0), ("wd", 1),
                                      ("wg", 2), ("wg", 3), ("wd", 2), ("wd", 3)]:
                        i = 4 * e + idx
                        if kind == "wg":
                            if i < 2:
                                continue
                            g.wait_ge(hps, 8 * (i - 1))
                            g.dma_start(
                                out=wga[:, i % 2, :, :],
                                in_=wge_quarter_ap(e, idx),
                            ).then_inc(wgq[i % 2], 16)
                        else:
                            if i >= 4:
                                g.wait_ge(eops, 3 * ((i - 4) // 4 + 1))
                            g.dma_start(
                                out=wdb[:, i % 4, :, :],
                                in_=bass.AP(wdn_h, e * FF * D + idx * KC * P * D,
                                            [[D, P], [P * D, KC], [1, D]]),
                            ).then_inc(wdq[i % 4], 16)

            @blk4.scalar
            def _(s):
                for e in range(E):
                    for fq in range(4):
                        for m in range(KC):
                            i = e * 32 + fq * 8 + m
                            s.wait_ge(hps, i + 1)
                            s.activation(h_sb[:, fq * 8 + m, :],
                                         ps_h[:, m % 2, 0:CL], AF.Gelu).then_inc(hev, 1)

            @blk4.vector
            def _(v):
                scount = 0
                for e in range(E):
                    for sgi, group in enumerate([[0, 1], [2, 3], [4]]):
                        v.wait_ge(eops, e * 3 + sgi + 1)
                        for s_ in group:
                            scount += 1
                            if scount >= 3:
                                v.wait_ge(eost[scount % 2], 16 * ((scount - 2 + 1) // 2))
                            v.tensor_copy(eosb[:, scount % 2, :],
                                          bass.AP(ps_eo, (2 * s_) % 4 * 512,
                                                  [[2048, P], [1, 1024]])
                                          ).then_inc(eoev, 1)

            @blk4.tensor
            def _(tn):
                for e in range(E):
                    tn.wait_ge(dspt[e % 2], 16 * KC * (e // 2 + 1))
                    for fq in range(4):
                        i = 4 * e + fq
                        tn.wait_ge(wgq[i % 2], 16 * (i // 2 + 1))
                        for m in range(KC):
                            mi = e * 32 + fq * 8 + m
                            if mi >= 2:
                                tn.wait_ge(hev, mi - 1)
                            for kc in range(KC):
                                st_, sp_ = kc == 0, kc == KC - 1
                                lhs = wga[:, i % 2, kc, m * P:(m + 1) * P]
                                tn.matmul(ps_h[:, m % 2, 0:512], lhs,
                                          acta[:, kc, (e % 2) * CL:(e % 2) * CL + 512],
                                          start=st_, stop=sp_)
                                mm = tn.matmul(ps_h[:, m % 2, 512:640], lhs,
                                               acta[:, kc, (e % 2) * CL + 512:(e % 2) * CL + CL],
                                               start=st_, stop=sp_)
                            mm.then_inc(hps, 1)
                    tn.wait_ge(hev, e * 32 + 32)
                    for sgi, group in enumerate([[0, 1], [2, 3], [4]]):
                        gidx = e * 3 + sgi
                        if gidx >= 1:
                            prevfull = (gidx - 1) // 3 * 5 + [0, 2, 4][(gidx - 1) % 3]
                            tn.wait_ge(eoev, prevfull + len([[0, 1], [2, 3], [4]][(gidx - 1) % 3]))
                        for kq in range(4):
                            i = 4 * e + kq
                            tn.wait_ge(wdq[i % 4], 16 * (i // 4 + 1))
                            for kc in range(KC):
                                st_ = (kq == 0 and kc == 0)
                                sp_ = (kq == 3 and kc == KC - 1)
                                for s_ in group:
                                    lhs = h_sb[:, kq * KC + kc, s_ * P:(s_ + 1) * P]
                                    tn.matmul(ps_eo[:, (2 * s_) % 4, :], lhs,
                                              wdb[:, i % 4, kc, 0:512],
                                              start=st_, stop=sp_)
                                    mm = tn.matmul(ps_eo[:, (2 * s_ + 1) % 4, :], lhs,
                                                   wdb[:, i % 4, kc, 512:1024],
                                                   start=st_, stop=sp_)
                        mm.then_inc(eops, 1)

        sPre.close()

        with ExitStack() as sD:
            ED = sD.enter_context
            eog = ED(nc.sbuf_tensor("eog", [P, 4, D], BF16))
            outsb = ED(nc.sbuf_tensor("outsb", [P, 2, D], F32))
            ps_go = ED(nc.psum_tensor("ps_go", [1, 16], F32))
            ps_gb = ED(nc.psum_tensor("ps_gb", [P, 16], F32))
            sc2b = ED(nc.sbuf_tensor("sc2b", [P, 2, D], F32))
            s5sem = ED(nc.semaphore("b5_s5"))
            gos = ED(nc.semaphore("b5_go"))
            gbs = ED(nc.semaphore("b5_gb"))
            gev = ED(nc.semaphore("b5_ge"))
            wrd = ED(nc.semaphore("b5_wr"))
            vdr5 = ED(nc.semaphore("b5_vdr"))
            gth0 = ED(nc.semaphore("b5_gt0"))
            gth1 = ED(nc.semaphore("b5_gt1"))
            gth = [gth0, gth1]
            xml0 = ED(nc.semaphore("b5_xm0"))
            xml1 = ED(nc.semaphore("b5_xm1"))
            xml = [xml0, xml1]
            cmb = ED(nc.semaphore("b5_cm"))
            osem0 = ED(nc.semaphore("b5_os0"))
            osem1 = ED(nc.semaphore("b5_os1"))
            osem = [osem0, osem1]
            blk5 = ED(nc.Block())

            @blk5.tensor
            def _(tn):
                tn.matmul(ps_go[0:1, :], prefm[:, 0:1], cnts[:, :], start=True, stop=False)
                tn.matmul(ps_go[0:1, 8:16], prefm[:, 1:2], cnts[:, 0:8],
                          start=False, stop=False)
                tn.matmul(ps_go[0:1, 8:16], prefm[:, 2:3], cnts[:, 0:8],
                          start=False, stop=True).then_inc(gos, 1)
                tn.wait_ge(gev, 1)
                tn.matmul(ps_gb[:, :], onesf[0:1, :], goffr[0:1, :],
                          start=True, stop=True).then_inc(gbs, 1)

            @blk5.vector
            def _(v):
                DR = mkdrain(v, vdr5)
                v.wait_ge(gos, 1)
                v.tensor_copy(goffr[0:1, :], ps_go[0:1, :]).then_inc(gev, 1)
                v.wait_ge(gbs, 1)
                mflat = bass.AP(m_all, 0, [[256, P], [1, 256]])
                DR(v.tensor_tensor(rsc1[:], mflat,
                                   bass.AP(ps_gb, 0, [[16, P], [8, 2], [0, NT], [1, 8]]),
                                   ALU.mult))
                DR(v.tensor_reduce(bass.AP(kg, 0, [[2 * NT, P], [1, 2 * NT]]),
                                   bass.AP(rsc1, 0, [[256, P], [8, 2 * NT], [1, 8]]),
                                   AX.X, ALU.add))
                DR(v.tensor_tensor(kg[:, :, :], kg[:, :, :], s_all[:, :, :], ALU.add))
                DR(v.tensor_scalar(bass.AP(kg, 0, [[2 * NT, P], [1, 2 * NT]]),
                                   bass.AP(kg, 0, [[2 * NT, P], [1, 2 * NT]]),
                                   CAP - 0.5, None, ALU.is_lt))
                DR(v.tensor_tensor(w1a[:], kg[:, 0, :], g1a[:], ALU.mult))
                v.tensor_tensor(w2a[:], kg[:, 1, :], g2a[:], ALU.mult).then_inc(wrd, 1)
                v.wait_ge(wrd, 1)
                for t in range(NT):
                    v.wait_ge(gth[t % 2], 32 * (t // 2 + 1))
                    v.wait_ge(xml[t % 2], 16 * (t // 2 + 1))
                    if t >= 2:
                        v.wait_ge(osem[t % 2], 16 * (t // 2))
                    v.wait_ge(s5sem, t + 1)
                    DR(v.scalar_tensor_tensor(sc1[:], eog[:, (t % 2) * 2 + 1, :],
                                              w2a[:, t:t + 1], sc2b[:, t % 2, :],
                                              ALU.mult, ALU.add))
                    v.tensor_tensor(outsb[:, t % 2, :], sc1[:], xt[:, t % 2, :],
                                    ALU.add).then_inc(cmb, 1)

            @blk5.scalar
            def _(s):
                s.wait_ge(wrd, 1)
                for t in range(NT):
                    s.wait_ge(gth[t % 2], 32 * (t // 2 + 1))
                    if t >= 2:
                        s.wait_ge(cmb, t - 1)
                    s.activation(sc2b[:, t % 2, :], eog[:, (t % 2) * 2, :],
                                 AF.Copy, scale=w1a[:, t:t + 1]).then_inc(s5sem, 1)

            @blk5.gpsimd
            def _(g):
                g.wait_ge(wrd, 1)
                for t in range(NT):
                    if t >= 2:
                        g.wait_ge(cmb, t - 1)
                    g.indirect_dma_start(
                        out=eog[:, (t % 2) * 2, :], out_offset=None,
                        in_=eo_dram.ap(), in_offset=IOA(slotu[:, 0, t:t + 1], 0),
                    ).then_inc(gth[t % 2], 16)
                    g.indirect_dma_start(
                        out=eog[:, (t % 2) * 2 + 1, :], out_offset=None,
                        in_=eo_dram.ap(), in_offset=IOA(slotu[:, 1, t:t + 1], 0),
                    ).then_inc(gth[t % 2], 16)

            @blk5.sync
            def _(sy):
                for t in range(NT):
                    if t >= 2:
                        sy.wait_ge(cmb, t - 1)
                    sy.dma_start(out=xt[:, t % 2, :],
                                 in_=xmid_dram[t * P:(t + 1) * P, :]).then_inc(xml[t % 2], 16)
                    if t >= 1:
                        sy.wait_ge(cmb, t)
                        sy.dma_start(out=out_d[(t - 1) * P:t * P, :],
                                     in_=outsb[:, (t - 1) % 2, :]).then_inc(osem[(t - 1) % 2], 16)
                sy.wait_ge(cmb, NT)
                sy.dma_start(out=out_d[(NT - 1) * P:NT * P, :],
                             in_=outsb[:, (NT - 1) % 2, :]).then_inc(osem[(NT - 1) % 2], 16)
                sy.wait_ge(osem[0], 16 * (NT // 2))
                sy.wait_ge(osem[1], 16 * (NT // 2))

    return nc


_NC_CACHE = {}


def kernel(L0_attn_x, L0_attn_Wq, L0_attn_Wk, L0_attn_Wv, L0_attn_Wg, L0_attn_bg,
           L0_attn_Wo, L0_ffn_wg, L0_ffn_w_gate, L0_ffn_w_down, stage=3, sub=9):
    x = np.ascontiguousarray(np.asarray(L0_attn_x, dtype=np.float32).reshape(4 * 4096, D))
    wq = np.ascontiguousarray(np.asarray(L0_attn_Wq, dtype=np.float32))
    wk = np.ascontiguousarray(np.asarray(L0_attn_Wk, dtype=np.float32))
    wv = np.ascontiguousarray(np.asarray(L0_attn_Wv, dtype=np.float32))
    wgt = np.ascontiguousarray(np.asarray(L0_attn_Wg, dtype=np.float32))
    wo = np.ascontiguousarray(np.asarray(L0_attn_Wo, dtype=np.float32))
    bg = np.asarray(L0_attn_bg, dtype=np.float32)
    wgr = np.ascontiguousarray(np.asarray(L0_ffn_wg, dtype=np.float32))
    wge = np.ascontiguousarray(np.asarray(L0_ffn_w_gate, dtype=np.float32).reshape(E * D, FF))
    wdn = np.ascontiguousarray(np.asarray(L0_ffn_w_down, dtype=np.float32).reshape(E * FF, D))
    bgb = np.ascontiguousarray(np.broadcast_to(bg[None, :], (P, D)))
    tri = np.ascontiguousarray(
        (np.arange(P)[:, None] <= np.arange(P)[None, :]).astype(ml_dtypes.bfloat16))
    csts = np.zeros((P, 16), dtype=np.float32)
    csts[:, 0:8] = np.arange(8, dtype=np.float32)[None, :]
    csts[:, 8] = SLOTS + np.arange(P, dtype=np.float32)
    onesf = np.ones((P, P), dtype=np.float32)

    if (stage, sub) not in _NC_CACHE:
        _NC_CACHE[(stage, sub)] = build(stage, sub)
    nc = _NC_CACHE[(stage, sub)]
    in_maps = []
    for c in range(NCORE):
        prefm = np.zeros((E, 3), dtype=np.float32)
        prefm[:, 0] = (np.arange(E) < c).astype(np.float32)
        prefm[:, 1] = 1.0
        prefm[:, 2] = -((np.arange(E) == c).astype(np.float32))
        in_maps.append(dict(
            x=np.ascontiguousarray(x[c * T:(c + 1) * T, :]),
            wq=wq, wk=wk, wv=wv, wgt=wgt, wo=wo, bgb=bgb, wgr=wgr,
            wge=wge, wdn=wdn, tri=tri, csts=csts, onesf=onesf, prefm=prefm,
        ))
    res = run_bass_kernel_spmd(nc, in_maps, core_ids=list(range(NCORE)))
    out = np.concatenate([res.results[c]["out"] for c in range(NCORE)], axis=0)
    return out.reshape(4, 4096, D)


# version 10
# speedup vs baseline: 1.0693x; 1.0503x over previous
import numpy as np
import ml_dtypes
from contextlib import ExitStack
import concourse.bass as bass
import concourse.mybir as mybir
from concourse.bass_utils import run_bass_kernel_spmd

F32 = mybir.dt.float32
BF16 = mybir.dt.bfloat16
U32 = mybir.dt.uint32
AF = mybir.ActivationFunctionType
ALU = mybir.AluOpType
AX = mybir.AxisListType
IOA = bass.IndirectOffsetOnAxis

P = 128
T = 2048
NT = 16
D = 1024
KC = 8
H = 16
E = 8
CL = 640
SLOTS = E * CL
DISP_ROWS = SLOTS + P
CAP = 4096.0
FF = 4096
EPS = 1e-5
SCALE = 64 ** -0.5
NCORE = 8


def build(stage=3, sub=9):
    nc = bass.Bass("TRN2", target_bir_lowering=False, detect_race_conditions=False)
    x_d = nc.dram_tensor("x", [T, D], F32, kind="ExternalInput").ap()
    wq_d = nc.dram_tensor("wq", [D, D], F32, kind="ExternalInput").ap()
    wk_d = nc.dram_tensor("wk", [D, D], F32, kind="ExternalInput").ap()
    wv_d = nc.dram_tensor("wv", [D, D], F32, kind="ExternalInput").ap()
    wgt_d = nc.dram_tensor("wgt", [D, D], F32, kind="ExternalInput").ap()
    wo_d = nc.dram_tensor("wo", [D, D], F32, kind="ExternalInput").ap()
    bgb_d = nc.dram_tensor("bgb", [P, D], F32, kind="ExternalInput").ap()
    wgr_d = nc.dram_tensor("wgr", [D, E], F32, kind="ExternalInput").ap()
    wge_h = nc.dram_tensor("wge", [E * D, FF], F32, kind="ExternalInput")
    wdn_h = nc.dram_tensor("wdn", [E * FF, D], F32, kind="ExternalInput")
    tri_d = nc.dram_tensor("tri", [P, P], BF16, kind="ExternalInput").ap()
    csts_d = nc.dram_tensor("csts", [P, 16], F32, kind="ExternalInput").ap()
    onesf_d = nc.dram_tensor("onesf", [P, P], F32, kind="ExternalInput").ap()
    prefm_d = nc.dram_tensor("prefm", [E, 3], F32, kind="ExternalInput").ap()
    out_d = nc.dram_tensor("out", [T, D], F32, kind="ExternalOutput").ap()

    xin1_dram = nc.dram_tensor("xin1_dram", [T, D], BF16)
    attn_dram = nc.dram_tensor("attn_dram", [T, D], BF16)
    xin2_dram = nc.dram_tensor("xin2_dram", [T, D], BF16)
    xmid_dram = nc.dram_tensor("xmid_dram", [T, D], F32)
    disp_dram = nc.dram_tensor("disp_dram", [DISP_ROWS, D], BF16)
    eo_dram = nc.dram_tensor("eo_dram", [DISP_ROWS, D], BF16)
    kv_in = nc.dram_tensor("kv_in", [P, 8 * 65], F32)
    kv_out = nc.dram_tensor("kv_out", [P, 8 * 65], F32)
    cnt_in = nc.dram_tensor("cnt_in", [1, 16], F32)
    cnt_out = nc.dram_tensor("cnt_out", [E, 16], F32)

    with ExitStack() as st:
        Ez = st.enter_context
        tri = Ez(nc.sbuf_tensor("tri_s", [P, P], BF16))
        csts = Ez(nc.sbuf_tensor("csts_s", [P, 16], F32))
        onesf = Ez(nc.sbuf_tensor("onesf_s", [P, P], F32))
        prefm = Ez(nc.sbuf_tensor("prefm_s", [E, 3], F32))

        junk = Ez(nc.sbuf_tensor("junk", [P, 8], F32))
        neg1 = Ez(nc.sbuf_tensor("neg1", [P, 1], F32))
        zb = Ez(nc.sbuf_tensor("zb", [P, D], BF16))
        wgs = Ez(nc.sbuf_tensor("wgs", [P, KC, E], BF16))
        acta = Ez(nc.sbuf_tensor("acta", [P, KC, T], BF16))
        bst = Ez(nc.sbuf_tensor("bst", [P, 12], F32))
        mv = Ez(nc.sbuf_tensor("mv", [P, 2, 2], F32))
        lt1 = Ez(nc.sbuf_tensor("lt1", [P, 4], F32))
        lt2 = Ez(nc.sbuf_tensor("lt2", [P, 4], F32))
        lt3 = Ez(nc.sbuf_tensor("lt3", [P, 1], F32))
        sdv = Ez(nc.sbuf_tensor("sdv", [P, 2], F32))
        sds = Ez(nc.sbuf_tensor("sds", [P, 2], F32))
        rvv = Ez(nc.sbuf_tensor("rvv", [P, 2], F32))
        xt = Ez(nc.sbuf_tensor("xt", [P, 2, D], F32))
        sc1 = Ez(nc.sbuf_tensor("sc1", [P, D], F32))
        sc2 = Ez(nc.sbuf_tensor("sc2", [P, D], F32))
        sc3 = Ez(nc.sbuf_tensor("sc3", [P, D], F32))
        gsc = Ez(nc.sbuf_tensor("gsc", [P, D], F32))
        tsc = Ez(nc.sbuf_tensor("tsc", [P, 256], F32))
        zq = Ez(nc.sbuf_tensor("zq", [P, H], F32))
        zr = Ez(nc.sbuf_tensor("zr", [P, H], F32))
        lgf = Ez(nc.sbuf_tensor("lgf", [P, NT, E], F32))
        tmaxs = Ez(nc.sbuf_tensor("tmaxs", [P, NT, E], F32))
        tidxs = Ez(nc.sbuf_tensor("tidxs", [P, NT, E], U32))
        i1fa = Ez(nc.sbuf_tensor("i1fa", [P, NT], F32))
        i2fa = Ez(nc.sbuf_tensor("i2fa", [P, NT], F32))
        lda = Ez(nc.sbuf_tensor("lda", [P, NT], F32))
        g1a = Ez(nc.sbuf_tensor("g1a", [P, NT], F32))
        g2a = Ez(nc.sbuf_tensor("g2a", [P, NT], F32))
        m_all = Ez(nc.sbuf_tensor("m_all", [P, 2, NT, E], BF16))
        incl = Ez(nc.sbuf_tensor("incl", [P, 2 * NT * E], F32))
        offa = Ez(nc.sbuf_tensor("offa", [P, 2 * NT * E], F32))
        offb = Ez(nc.sbuf_tensor("offb", [P, 2 * NT * E], F32))
        rsc1 = Ez(nc.sbuf_tensor("rsc1", [P, 2 * NT * E], F32))
        s_all = Ez(nc.sbuf_tensor("s_all", [P, 2, NT], F32))
        kg = Ez(nc.sbuf_tensor("kg", [P, 2, NT], F32))
        ov1 = Ez(nc.sbuf_tensor("ov1", [P, 2, NT], F32))
        slotf = Ez(nc.sbuf_tensor("slotf", [P, 2, NT], F32))
        slosc = Ez(nc.sbuf_tensor("slosc", [P, 2, NT], F32))
        slotu = Ez(nc.sbuf_tensor("slotu", [P, 2, NT], U32))
        cntp = Ez(nc.sbuf_tensor("cntp", [P, 16], F32))
        cnts = Ez(nc.sbuf_tensor("cnts", [E, 16], F32))
        goffr = Ez(nc.sbuf_tensor("goffr", [1, 16], F32))
        w1a = Ez(nc.sbuf_tensor("w1a", [P, NT], F32))
        w2a = Ez(nc.sbuf_tensor("w2a", [P, NT], F32))

        iot8 = csts[:, 0:8]
        trash = csts[:, 8:9]

        def sep(v):
            v.tensor_scalar(junk[:, 0:1], junk[:, 0:1], 1.0, None, ALU.add)

        def mkdrain(eng, sem):
            state = {"n": 0}

            def D(inst):
                state["n"] += 1
                inst.then_inc(sem, 1)
                eng.wait_ge(sem, state["n"])
                return inst
            return D

        def LNSTATS(v, DR, src_ap, tpar, sqbuf):
                        DR(v.tensor_reduce(lt3[:], src_ap, AX.X, ALU.add))
                        DR(v.tensor_scalar(mv[:, tpar, 0:1], lt3[:], 1.0 / D, None, ALU.mult))
                        DR(v.tensor_tensor(sqbuf, src_ap, src_ap, ALU.mult))
                        DR(v.tensor_reduce(lt3[:], sqbuf, AX.X, ALU.add))
                        DR(v.tensor_scalar(lt3[:], lt3[:], 1.0 / D, EPS, ALU.mult,
                                           ALU.add))
                        DR(v.tensor_tensor(lt2[:, 1:2], mv[:, tpar, 0:1],
                                           mv[:, tpar, 0:1], ALU.mult))
                        DR(v.tensor_tensor(mv[:, tpar, 1:2], lt3[:], lt2[:, 1:2],
                                           ALU.subtract))

        with ExitStack() as sA:
            EA = sA.enter_context
            wa = EA(nc.sbuf_tensor("wa", [P, 4, KC, D], BF16))
            ktl = EA(nc.sbuf_tensor("ktl", [P, 2, D], BF16))
            vxt = EA(nc.sbuf_tensor("vxt", [P, 2, D], BF16))
            onebf = EA(nc.sbuf_tensor("onebf", [P, 1], BF16))
            qfm = EA(nc.sbuf_tensor("qfm", [P, 2, KC, 512], BF16))
            gtl = EA(nc.sbuf_tensor("gtl", [P, 2, D], BF16))
            x1s = EA(nc.sbuf_tensor("x1s", [P, 2, D], BF16))
            ats = EA(nc.sbuf_tensor("ats", [P, 2, D], BF16))
            bgb = EA(nc.sbuf_tensor("bgb_s", [P, D], F32))
            kvpk = EA(nc.sbuf_tensor("kvpk", [P, 8, 65], F32))
            kvsum = EA(nc.sbuf_tensor("kvsum", [P, 8, 65], F32))
            kvx = EA(nc.sbuf_tensor("kvx", [P, 8, 65], BF16))

            with ExitStack() as s1:
                E1 = s1.enter_context
                ps_k = E1(nc.psum_tensor("ps_k", [P, 2, D], F32))
                ps_v = E1(nc.psum_tensor("ps_v", [P, D], F32))
                ps_kvA = E1(nc.psum_tensor("ps_kvA", [P, 4, 128], F32))
                ps_kvB = E1(nc.psum_tensor("ps_kvB", [P, 4, 128], F32))
                xsem0 = E1(nc.semaphore("b1_x0"))
                xsem1 = E1(nc.semaphore("b1_x1p"))
                xsem = [xsem0, xsem1]
                csem = E1(nc.semaphore("b1_c"))
                wkvs = E1(nc.semaphore("b1_wkv"))
                wsem = E1(nc.semaphore("b1_w"))
                x1sem = E1(nc.semaphore("b1_x1"))
                stp0 = E1(nc.semaphore("b1_st0"))
                stp1 = E1(nc.semaphore("b1_st1"))
                stp = [stp0, stp1]
                tsem = E1(nc.semaphore("b1_t"))
                v2s = E1(nc.semaphore("b1_v2s"))
                s2v = E1(nc.semaphore("b1_s2v"))
                ksem = E1(nc.semaphore("b1_k"))
                kvrdy = E1(nc.semaphore("b1_kvr"))
                vesem = E1(nc.semaphore("b1_ve"))
                kelsem = E1(nc.semaphore("b1_kel"))
                kvmmd = E1(nc.semaphore("b1_kvm"))
                kvadd = E1(nc.semaphore("b1_kva"))
                kvev = E1(nc.semaphore("b1_kve"))
                gsem = E1(nc.semaphore("b1_g"))
                vdr = E1(nc.semaphore("b1_vdr"))
                zdsem = E1(nc.semaphore("b1_zd"))
                kvin_s = E1(nc.semaphore("b1_kvi"))
                LAG = 4
                blk = E1(nc.Block())

                @blk.sync
                def _(sy):
                    sy.dma_start(out=tri[:], in_=tri_d).then_inc(csem, 16)
                    sy.dma_start(out=csts[:], in_=csts_d).then_inc(csem, 16)
                    sy.dma_start(out=onesf[:], in_=onesf_d).then_inc(csem, 16)
                    sy.dma_start(out=prefm[:], in_=prefm_d).then_inc(csem, 16)
                    sy.dma_start(out=bgb[:], in_=bgb_d).then_inc(csem, 16)
                    def TD(tl):
                        sy.wait_ge(stp[tl % 2], 16 * (tl // 2 + 1))
                        if tl >= 1:
                            sy.wait_ge(tsem, 16 * KC * tl)
                        for kc in range(KC):
                            sy.dma_start(out=acta[:, kc, tl * P:(tl + 1) * P],
                                         in_=xin1_dram[tl * P:(tl + 1) * P,
                                                       kc * P:(kc + 1) * P],
                                         transpose=True).then_inc(tsem, 16)

                    for t in range(NT):
                        if t >= 2:
                            sy.wait_ge(x1sem, t - 1)
                        sy.dma_start(out=xt[:, t % 2, :],
                                     in_=x_d[t * P:(t + 1) * P, :]).then_inc(xsem[t % 2], 16)
                        if t >= 1:
                            sy.wait_ge(x1sem, t)
                            sy.dma_start(out=xin1_dram[(t - 1) * P:t * P, :],
                                         in_=x1s[:, (t - 1) % 2, :]).then_inc(stp[(t - 1) % 2], 16)
                        if t >= 2:
                            TD(t - 2)
                    sy.wait_ge(x1sem, NT)
                    sy.dma_start(out=xin1_dram[(NT - 1) * P:NT * P, :],
                                 in_=x1s[:, (NT - 1) % 2, :]).then_inc(stp[(NT - 1) % 2], 16)
                    TD(NT - 2)
                    TD(NT - 1)
                    sy.wait_ge(gsem, 1)
                    for i in range(DISP_ROWS // P):
                        sy.dma_start(out=disp_dram[i * P:(i + 1) * P, :],
                                     in_=zb[:]).then_inc(zdsem, 16)

                @blk.gpsimd
                def _(g):
                    g.memset(zb[:], 0.0)
                    g.sem_inc(gsem, 1)
                    for i, wsrc in enumerate([wk_d, wv_d, wq_d, wgt_d]):
                        for kc in range(KC):
                            g.dma_start(out=wa[:, i, kc, :],
                                        in_=wsrc[kc * P:(kc + 1) * P, :]
                                        ).then_inc(wkvs if i < 2 else wsem, 16)
                    for kc in range(KC):
                        g.dma_start(out=wgs[:, kc, :],
                                    in_=wgr_d[kc * P:(kc + 1) * P, :]).then_inc(wsem, 16)
                    g.wait_ge(kvev, 1)
                    g.dma_start(out=kv_in.ap(), in_=kvpk[:, :, :]).then_inc(kvin_s, 16)
                    g.wait_ge(kvin_s, 16)

                @blk.vector
                def _(v):
                    DR = mkdrain(v, vdr)
                    DR(v.memset(junk[:], 0.0))
                    DR(v.memset(neg1[:], -1.0))
                    DR(v.memset(onebf[:], 1.0))
                    DR(v.memset(kvpk[:, :, :], 0.0))
                    v.wait_ge(csem, 16 * 5)
                    vskip = stage == 0 and (sub < 1 or sub == 10)
                    kvskip = stage == 0 and (sub < 2 or sub >= 10)

                    def EPI(u):
                        v.wait_ge(ksem, u + 1)
                        v.wait_ge(kelsem, u + 1)
                        v.tensor_tensor(ktl[:, u % 2, :], sc2[:], sc1[:], ALU.add)
                        v.tensor_copy(vxt[:, u % 2, :],
                                      ps_v[:, :]).then_inc(vesem, 1)
                        v.sem_inc(kvrdy, 1)
                        if u >= 1 and not kvskip:
                            v.wait_ge(kvmmd, u)
                            DR(v.tensor_tensor(kvpk[:, 0:4, :], kvpk[:, 0:4, :],
                                               ps_kvA[:, :, 0:65], ALU.add))
                            v.tensor_tensor(kvpk[:, 4:8, :], kvpk[:, 4:8, :],
                                            ps_kvB[:, :, 0:65], ALU.add).then_inc(kvadd, 1)
                            v.wait_ge(kvadd, u)

                    for t in range(NT):
                        v.wait_ge(xsem[t % 2], 16 * (t // 2 + 1))
                        LNSTATS(v, DR, xt[:, t % 2, :], t % 2, gsc[:])
                        v.tensor_copy(sdv[:, t % 2:t % 2 + 1],
                                      mv[:, t % 2, 1:2]).then_inc(v2s, 1)
                        v.wait_ge(s2v, t + 1)
                        DR(v.reciprocal(rvv[:, t % 2:t % 2 + 1], sds[:, t % 2:t % 2 + 1]))
                        if t >= 2:
                            v.wait_ge(stp[t % 2], 16 * (t // 2))
                        v.tensor_scalar(x1s[:, t % 2, :], xt[:, t % 2, :],
                                        mv[:, t % 2, 0:1], rvv[:, t % 2:t % 2 + 1],
                                        ALU.subtract, ALU.mult).then_inc(x1sem, 1)
                        if t >= LAG and not vskip:
                            EPI(t - LAG)
                    if not vskip:
                        for u in range(NT - LAG, NT):
                            EPI(u)
                    if stage == 0 and (sub < 2 or sub >= 10):
                        v.sem_inc(kvev, 1)
                        return
                    v.wait_ge(kvmmd, NT)
                    DR(v.tensor_tensor(kvpk[:, 0:4, :], kvpk[:, 0:4, :],
                                       ps_kvA[:, :, 0:65], ALU.add))
                    v.tensor_tensor(kvpk[:, 4:8, :], kvpk[:, 4:8, :],
                                    ps_kvB[:, :, 0:65], ALU.add).then_inc(kvev, 1)
                    v.wait_ge(kvev, 1)

                @blk.scalar
                def _(s):
                    kskip = stage == 0 and (sub < 1 or sub == 10 or sub >= 12)

                    def KELU(u):
                        s.wait_ge(ksem, u + 1)
                        if u >= 1:
                            s.wait_ge(kvrdy, u)
                        s.activation(sc2[:], ps_k[:, u % 2, :], AF.Relu)
                        s.activation(sc3[:], ps_k[:, u % 2, :], AF.Relu,
                                     scale=neg1[:, 0:1])
                        s.activation(sc1[:], sc3[:], AF.Exp,
                                     scale=neg1[:, 0:1]).then_inc(kelsem, 1)

                    for t in range(NT):
                        s.wait_ge(v2s, t + 1)
                        s.activation(sds[:, t % 2:t % 2 + 1], sdv[:, t % 2:t % 2 + 1],
                                     AF.Sqrt).then_inc(s2v, 1)
                        if t >= LAG and not kskip:
                            KELU(t - LAG)
                    if not kskip:
                        for u in range(NT - LAG, NT):
                            KELU(u)

                @blk.tensor
                def _(tn):
                    if stage == 0 and sub < 1:
                        return
                    for t in range(3):
                        tn.wait_ge(x1sem, t + 1)
                        for _ in range(4):
                            tn.matmul(ps_k[:, 0, 0:128], tri[:, :], tri[:, :],
                                      start=True, stop=True)
                    tn.wait_ge(wkvs, 16 * 16)
                    for t in range(NT):
                        tn.wait_ge(tsem, 16 * KC * (t + 1))
                        if t >= 2 and sub != 10:
                            tn.wait_ge(kvrdy, t - 1)
                        if t >= 1 and sub != 10:
                            tn.wait_ge(vesem, t)
                        for kc in range(KC):
                            st_, sp_ = kc == 0, kc == KC - 1
                            lhs = acta[:, kc, t * P:(t + 1) * P]
                            tn.matmul(ps_k[:, t % 2, 0:512], lhs, wa[:, 0, kc, 0:512],
                                      start=st_, stop=sp_)
                            tn.matmul(ps_k[:, t % 2, 512:1024], lhs, wa[:, 0, kc, 512:1024],
                                      start=st_, stop=sp_)
                            tn.matmul(ps_v[:, 0:512], lhs, wa[:, 1, kc, 0:512],
                                      start=st_, stop=sp_)
                            mm = tn.matmul(ps_v[:, 512:1024], lhs, wa[:, 1, kc, 512:1024],
                                           start=st_, stop=sp_)
                        mm.then_inc(ksem, 1)
                        if t >= 1 and not (stage == 0 and (sub < 2 or sub >= 10)):
                            tn.wait_ge(kvrdy, t)
                            if t >= 2:
                                tn.wait_ge(kvadd, t - 1)
                            for h in range(H):
                                pst = ps_kvA if (h // 2) < 4 else ps_kvB
                                tp = (0, 64) if (h % 2) else None
                                po = (h % 2) * 64
                                blk_i = (h // 2) % 4
                                tn.matmul(pst[po:po + 64, blk_i, 0:64],
                                          ktl[:, (t - 1) % 2, h * 64:(h + 1) * 64],
                                          vxt[:, (t - 1) % 2, h * 64:(h + 1) * 64],
                                          start=True, stop=True, tile_position=tp)
                                mm = tn.matmul(pst[po:po + 64, blk_i, 64:65],
                                               ktl[:, (t - 1) % 2, h * 64:(h + 1) * 64],
                                               onebf[:, :],
                                               start=True, stop=True, tile_position=tp)
                            mm.then_inc(kvmmd, 1)
                    if stage == 0 and (sub < 2 or sub >= 10):
                        return
                    tn.wait_ge(kvrdy, NT)
                    tn.wait_ge(kvadd, NT - 1)
                    for h in range(H):
                        pst = ps_kvA if (h // 2) < 4 else ps_kvB
                        tp = (0, 64) if (h % 2) else None
                        po = (h % 2) * 64
                        blk_i = (h // 2) % 4
                        tn.matmul(pst[po:po + 64, blk_i, 0:64],
                                  ktl[:, (NT - 1) % 2, h * 64:(h + 1) * 64],
                                  vxt[:, (NT - 1) % 2, h * 64:(h + 1) * 64],
                                  start=True, stop=True, tile_position=tp)
                        mm = tn.matmul(pst[po:po + 64, blk_i, 64:65],
                                       ktl[:, (NT - 1) % 2, h * 64:(h + 1) * 64],
                                       onebf[:, :],
                                       start=True, stop=True, tile_position=tp)
                    mm.then_inc(kvmmd, 1)

            if stage == 5:
                pass
            if stage == 0:
                with ExitStack() as s0:
                    E0 = s0.enter_context
                    x1f = E0(nc.sbuf_tensor("x1f", [P, D], F32))
                    ktf = E0(nc.sbuf_tensor("ktf", [P, D], F32))
                    d0 = E0(nc.semaphore("s0_d"))
                    g0 = E0(nc.semaphore("s0_g"))
                    blk0 = E0(nc.Block())

                    @blk0.gpsimd
                    def _(g):
                        g.dma_start(out=ktf[:], in_=xin1_dram[0:P, :]).then_inc(g0, 16)

                    @blk0.vector
                    def _(v):
                        v.tensor_copy(ktf[:], ktl[:, (NT - 1) % 2, :])
                        v.tensor_copy(x1f[:, 0:12], bst[:])
                        v.tensor_copy(x1f[:, 16:20], bass.AP(mv, 0, [[4, P], [1, 4]]))
                        v.tensor_copy(x1f[:, 20:22], sdv[:])
                        v.tensor_copy(x1f[:, 22:24], rvv[:]).then_inc(g0, 1)

                    @blk0.sync
                    def _(sy):
                        sy.wait_ge(g0, 17)
                        sy.dma_start(out=out_d[P:2 * P, :], in_=x1f[:]).then_inc(d0, 16)
                        sy.dma_start(out=out_d[2 * P:3 * P, :], in_=ktf[:]).then_inc(d0, 16)
                        sy.wait_ge(d0, 32)
                return nc

            with ExitStack() as s2:
                E2 = s2.enter_context
                ps_q = E2(nc.psum_tensor("ps_q", [P, 2, 512], F32))
                ps_g = E2(nc.psum_tensor("ps_g", [P, 2, D], F32))
                ps_oA = E2(nc.psum_tensor("ps_oA", [P, 4, 128], F32))
                ps_oB = E2(nc.psum_tensor("ps_oB", [P, 4, 128], F32))
                osb = E2(nc.sbuf_tensor("osb", [P, 8, 128], F32))
                ccsem = E2(nc.semaphore("b2_cc"))
                kvls = E2(nc.semaphore("b2_kvl"))
                kvxs = E2(nc.semaphore("b2_kvx"))
                qp = E2(nc.semaphore("b2_qp"))
                qe = E2(nc.semaphore("b2_qe"))
                gp = E2(nc.semaphore("b2_gp"))
                ge = E2(nc.semaphore("b2_ge"))
                oms = E2(nc.semaphore("b2_om"))
                gat = E2(nc.semaphore("b2_gat"))
                asem0 = E2(nc.semaphore("b2_as0"))
                asem1 = E2(nc.semaphore("b2_as1"))
                asem = [asem0, asem1]
                v2s2 = E2(nc.semaphore("b2_v2s"))
                s2v2 = E2(nc.semaphore("b2_s2v"))
                g2s2 = E2(nc.semaphore("b2_g2s"))
                vdr2 = E2(nc.semaphore("b2_vdr"))
                blk2 = E2(nc.Block())

                @blk2.gpsimd
                def _(g):
                    g.collective_compute(
                        "AllReduce", ALU.add,
                        replica_groups=[[0, 1], [2, 3], [4, 5], [6, 7]],
                        ins=[kv_in.ap().opt()], outs=[kv_out.ap().opt()],
                    ).then_inc(ccsem, 1)
                    g.wait_ge(ccsem, 1)
                    g.dma_start(out=kvsum[:, :, :], in_=kv_out.ap()).then_inc(kvls, 16)

                @blk2.vector
                def _(v):
                    DR = mkdrain(v, vdr2)
                    if stage == 5 and sub == 20:
                        v.wait_ge(kvls, 16)
                        v.tensor_copy(kvx[:, :, :], kvsum[:, :, :]).then_inc(kvxs, 1)
                        return
                    for sl in range(4):
                        if sl == 1:
                            pass
                        for m in range(KC):
                            v.wait_ge(s2v2, sl * 8 + m + 1)
                            v.tensor_tensor(qfm[:, sl % 2, m, :], sc2[:, 0:512],
                                            sc1[:, 0:512], ALU.add).then_inc(qe, 1)
                        if sl == 0:
                            v.wait_ge(kvls, 16)
                            v.tensor_copy(kvx[:, :, :], kvsum[:, :, :]).then_inc(kvxs, 1)
                        if stage == 5 and sub == 21:
                            continue
                        for tt in range(4):
                            t = sl * 4 + tt
                            v.wait_ge(gp, t + 1)
                            if t >= 2 and not (stage == 5 and sub == 22):
                                v.wait_ge(asem[t % 2], 16 * (t // 2))
                            v.tensor_tensor(gsc[:], ps_g[:, t % 2, :], bgb[:],
                                            ALU.add).then_inc(v2s2, 1)
                            if stage == 5 and sub == 22:
                                continue
                            for part in range(2):
                                v.wait_ge(oms, 2 * t + part + 1)
                                if part == 0:
                                    v.wait_ge(g2s2, t + 1)
                                DR(v.tensor_copy(osb[:, 0:4, 0:65], ps_oA[:, :, 0:65]))
                                v.tensor_copy(osb[:, 4:8, 0:65],
                                              ps_oB[:, :, 0:65]).then_inc(gat, 1)
                                v.wait_ge(gat, 4 * t + 2 * part + 1)
                                DR(v.tensor_scalar(zq[:, part * 8:part * 8 + 8],
                                                   bass.AP(osb, 64, [[1024, P], [128, 8]]),
                                                   1e-6, None, ALU.max))
                                DR(v.reciprocal(zr[:, part * 8:part * 8 + 8],
                                                zq[:, part * 8:part * 8 + 8]))
                                DR(v.tensor_scalar(zr[:, part * 8:part * 8 + 8],
                                                   zr[:, part * 8:part * 8 + 8],
                                                   SCALE, None, ALU.mult))
                                for par in range(2):
                                    base = (t % 2) * D + part * 512 + par * 64
                                    DR(v.tensor_tensor(
                                        tsc[:, 0:256],
                                        bass.AP(osb, par * 4 * 128, [[1024, P], [128, 4], [1, 64]]),
                                        bass.AP(zr, part * 8 + par * 4,
                                                [[16, P], [1, 4], [0, 64]]),
                                        ALU.mult))
                                    mmv = v.tensor_tensor(
                                        bass.AP(ats, base, [[2 * D, P], [128, 4], [1, 64]]),
                                        tsc[:, 0:256],
                                        bass.AP(gtl, base, [[2 * D, P], [128, 4], [1, 64]]),
                                        ALU.mult)
                                mmv.then_inc(gat, 1)

                @blk2.scalar
                def _(s):
                    if stage == 5 and sub == 20:
                        return
                    for sl in range(4):
                        for m in range(KC):
                            i = sl * 8 + m
                            s.wait_ge(qp, i + 1)
                            if i >= 1:
                                s.wait_ge(qe, i)
                            s.activation(sc2[:, 0:512], ps_q[:, m % 2, :], AF.Relu)
                            s.activation(sc3[:, 0:512], ps_q[:, m % 2, :], AF.Relu,
                                         scale=neg1[:, 0:1])
                            s.activation(sc1[:, 0:512], sc3[:, 0:512], AF.Exp,
                                         scale=neg1[:, 0:1]).then_inc(s2v2, 1)
                        for tt in range(4):
                            if stage == 5 and sub == 21:
                                break
                            t = sl * 4 + tt
                            s.wait_ge(v2s2, t + 1)
                            s.activation(gtl[:, t % 2, :], gsc[:],
                                         AF.Sigmoid).then_inc(g2s2, 1)

                @blk2.tensor
                def _(tn):
                    if stage == 5 and sub == 20:
                        return
                    for sl in range(4):
                        for m in range(KC):
                            if sl * 8 + m >= 2:
                                tn.wait_ge(qe, sl * 8 + m - 1)
                            for kc in range(KC):
                                st_, sp_ = kc == 0, kc == KC - 1
                                mm = tn.matmul(ps_q[:, m % 2, :],
                                               wa[:, 2, kc, m * P:(m + 1) * P],
                                               acta[:, kc, sl * 512:(sl + 1) * 512],
                                               start=st_, stop=sp_)
                            mm.then_inc(qp, 1)
                        if stage == 5 and sub == 21:
                            continue
                        for tt in range(4):
                            t = sl * 4 + tt
                            if t >= 2:
                                tn.wait_ge(g2s2, t - 1)
                            for kc in range(KC):
                                st_, sp_ = kc == 0, kc == KC - 1
                                lhs = acta[:, kc, t * P:(t + 1) * P]
                                tn.matmul(ps_g[:, t % 2, 0:512], lhs,
                                          wa[:, 3, kc, 0:512], start=st_, stop=sp_)
                                mm = tn.matmul(ps_g[:, t % 2, 512:1024], lhs,
                                               wa[:, 3, kc, 512:1024], start=st_, stop=sp_)
                            mm.then_inc(gp, 1)
                            if stage == 5 and sub == 22:
                                continue
                            if t == 0:
                                tn.wait_ge(kvxs, 1)
                            tn.wait_ge(qe, sl * 8 + 8)
                            for part in range(2):
                                if t > 0 or part > 0:
                                    tn.wait_ge(gat, 4 * t + 2 * part - 1)
                                for par in range(2):
                                    pst = ps_oA if par == 0 else ps_oB
                                    for j in range(4):
                                        h = part * 8 + 2 * j + par
                                        mm = tn.matmul(pst[:, j, 0:65],
                                                       qfm[par * 64:par * 64 + 64,
                                                           sl % 2, h // 2, tt * P:(tt + 1) * P],
                                                       kvx[par * 64:par * 64 + 64, h // 2, :],
                                                       start=True, stop=True)
                                mm.then_inc(oms, 1)

                @blk2.sync
                def _(sy):
                    if stage == 5 and sub in (20, 21, 22):
                        return
                    for t in range(NT):
                        sy.wait_ge(gat, 4 * t + 4)
                        sy.dma_start(out=attn_dram[t * P:(t + 1) * P, :],
                                     in_=ats[:, t % 2, :]).then_inc(asem[t % 2], 16)
                    sy.wait_ge(asem[0], 16 * (NT // 2))
                    sy.wait_ge(asem[1], 16 * (NT // 2))

        if stage == 5:
            with ExitStack() as s5:
                E5 = s5.enter_context
                d5 = E5(nc.semaphore("s5_d"))
                blk5d = E5(nc.Block())

                @blk5d.sync
                def _(sy):
                    sy.dma_start(out=out_d[0:P, 0:520],
                                 in_=bass.AP(kvsum, 0, [[520, P], [1, 520]])).then_inc(d5, 16)
                    sy.wait_ge(d5, 16)
            return nc

        sPre = ExitStack()
        wga = sPre.enter_context(nc.sbuf_tensor("wga", [P, 2, KC, D], BF16))
        wgq = [sPre.enter_context(nc.semaphore(f"wgq{i}")) for i in range(2)]

        def wge_quarter_ap(e, q):
            return bass.AP(wge_h, e * D * FF + q * D,
                           [[FF, P], [P * FF, KC], [1, D]])

        with ExitStack() as sB:
            EB = sB.enter_context
            wo_sb = EB(nc.sbuf_tensor("wo_sb", [P, KC, D], BF16))
            xin2 = EB(nc.sbuf_tensor("xin2", [P, NT, D], BF16))
            x2fm = acta
            xms = EB(nc.sbuf_tensor("xms", [P, 2, D], F32))
            ps_wo = EB(nc.psum_tensor("ps_wo", [P, 2, D], F32))
            ps_lg = EB(nc.psum_tensor("ps_lg", [P, 2, 512], F32))
            ps_pre = EB(nc.psum_tensor("ps_pre", [P, 256], F32))
            ps_off = EB(nc.psum_tensor("ps_off", [P, 256], F32))
            t3sem = EB(nc.semaphore("b3_t"))
            wosem = EB(nc.semaphore("b3_wo"))
            xr0 = EB(nc.semaphore("b3_xr0"))
            xr1 = EB(nc.semaphore("b3_xr1"))
            xrsem = [xr0, xr1]
            wops = EB(nc.semaphore("b3_wop"))
            xmr = EB(nc.semaphore("b3_xmr"))
            xms0 = EB(nc.semaphore("b3_xms0"))
            xms1 = EB(nc.semaphore("b3_xms1"))
            xmst = [xms0, xms1]
            v2s3 = EB(nc.semaphore("b3_v2s"))
            s2v3 = EB(nc.semaphore("b3_s2v"))
            x2r = EB(nc.semaphore("b3_x2r"))
            x2st = EB(nc.semaphore("b3_x2s"))
            t2sem = EB(nc.semaphore("b3_t2"))
            lgp = EB(nc.semaphore("b3_lgp"))
            lgc = EB(nc.semaphore("b3_lgc"))
            mdone = EB(nc.semaphore("b3_md"))
            prep = EB(nc.semaphore("b3_pre"))
            cpd = EB(nc.semaphore("b3_cpd"))
            offr = EB(nc.semaphore("b3_offr"))
            offbs = EB(nc.semaphore("b3_offb"))
            sltr = EB(nc.semaphore("b3_slt"))
            cntr = EB(nc.semaphore("b3_cnt"))
            scat = EB(nc.semaphore("b3_sca"))
            cis = EB(nc.semaphore("b3_cis"))
            ostg1 = EB(nc.semaphore("b3_o1"))
            vdr3 = EB(nc.semaphore("b3_vdr"))
            blk3 = EB(nc.Block())

            @blk3.sync
            def _(sy):
                for kc in range(KC):
                    sy.dma_start(out=acta[:, kc, :],
                                 in_=attn_dram[0:T, kc * P:(kc + 1) * P],
                                 transpose=True).then_inc(t3sem, 16)
                for t in range(NT):
                    sy.wait_ge(xmr, t + 1)
                    sy.dma_start(out=xmid_dram[t * P:(t + 1) * P, :],
                                 in_=xms[:, t % 2, :]).then_inc(xmst[t % 2], 16)
                    if stage == 1:
                        sy.dma_start(out=out_d[t * P:(t + 1) * P, :],
                                     in_=xms[:, t % 2, :]).then_inc(xmst[t % 2], 16)
                if stage == 1:
                    sy.wait_ge(xmst[0], 32 * (NT // 2))
                    sy.wait_ge(xmst[1], 32 * (NT // 2))
                    return
                for t in range(NT):
                    sy.wait_ge(x2r, t + 1)
                    sy.dma_start(out=xin2_dram[t * P:(t + 1) * P, :],
                                 in_=xin2[:, t, :]).then_inc(x2st, 16)
                sy.wait_ge(x2st, 16 * NT)
                for kc in range(KC):
                    sy.dma_start(out=x2fm[:, kc, :],
                                 in_=xin2_dram[0:T, kc * P:(kc + 1) * P],
                                 transpose=True).then_inc(t2sem, 16)

            @blk3.gpsimd
            def _(g):
                for kc in range(KC):
                    g.dma_start(out=wo_sb[:, kc, :],
                                in_=wo_d[kc * P:(kc + 1) * P, :]).then_inc(wosem, 16)
                for t in range(NT):
                    if t >= 2:
                        g.wait_ge(xmr, t - 1)
                    g.dma_start(out=xt[:, t % 2, :],
                                in_=x_d[t * P:(t + 1) * P, :]).then_inc(xrsem[t % 2], 16)
                if stage == 1:
                    return
                g.dma_start(out=eo_dram[SLOTS:DISP_ROWS, :], in_=zb[:]).then_inc(cis, 16)
                for i in range(2):
                    g.dma_start(out=wga[:, i, :, :],
                                in_=wge_quarter_ap(0, i)).then_inc(wgq[i], 16)
                for j in range(2):
                    g.wait_ge(sltr, j + 1)
                    for t in range(NT):
                        g.indirect_dma_start(
                            out=disp_dram.ap(),
                            out_offset=IOA(slotu[:, j, t:t + 1], 0),
                            in_=xin2[:, t, :], in_offset=None,
                        ).then_inc(scat, 16)
                g.wait_ge(cntr, 1)
                g.dma_start(out=cnt_in.ap(), in_=cntp[0:1, :]).then_inc(cis, 16)
                g.wait_ge(cis, 32)
                g.wait_ge(scat, 16 * 32)

            @blk3.vector
            def _(v):
                DR = mkdrain(v, vdr3)
                xmul = 32 if stage == 1 else 16
                for t in range(NT):
                    v.wait_ge(wops, t + 1)
                    v.wait_ge(xrsem[t % 2], 16 * (t // 2 + 1))
                    if t >= 2:
                        v.wait_ge(xmst[t % 2], xmul * (t // 2))
                    v.tensor_tensor(xms[:, t % 2, :], xt[:, t % 2, :],
                                    ps_wo[:, t % 2, :], ALU.add).then_inc(xmr, 1)
                    v.wait_ge(xmr, t + 1)
                    if stage == 1:
                        continue
                    LNSTATS(v, DR, xms[:, t % 2, :], t % 2, sc1[:])
                    v.tensor_copy(sdv[:, t % 2:t % 2 + 1],
                                  mv[:, t % 2, 1:2]).then_inc(v2s3, 1)
                    v.wait_ge(s2v3, t + 1)
                    DR(v.reciprocal(rvv[:, t % 2:t % 2 + 1], sds[:, t % 2:t % 2 + 1]))
                    v.tensor_scalar(xin2[:, t, :], xms[:, t % 2, :],
                                    mv[:, t % 2, 0:1], rvv[:, t % 2:t % 2 + 1],
                                    ALU.subtract, ALU.mult).then_inc(x2r, 1)
                if stage == 1:
                    return
                for t in range(NT):
                    v.wait_ge(lgp, t + 1)
                    v.tensor_copy(lgf[:, t, :],
                                  ps_lg[:, t % 2, 0:E]).then_inc(lgc, 1)
                    if t == NT - 1:
                        v.wait_ge(lgc, NT)
                for t in range(NT):
                    DR(v.max(tmaxs[:, t, :], lgf[:, t, :]))
                for t in range(NT):
                    DR(v.max_index(tidxs[:, t, :], tmaxs[:, t, :], lgf[:, t, :]))
                DR(v.tensor_scalar(i1fa[:], bass.AP(tidxs, 0, [[NT * E, P], [E, NT]]),
                                   0.0, None, ALU.add))
                DR(v.tensor_scalar(i2fa[:], bass.AP(tidxs, 1, [[NT * E, P], [E, NT]]),
                                   0.0, None, ALU.add))
                v.tensor_tensor(lda[:], bass.AP(tmaxs, 0, [[NT * E, P], [E, NT]]),
                                bass.AP(tmaxs, 1, [[NT * E, P], [E, NT]]),
                                ALU.subtract).then_inc(v2s3, 1)
                v.wait_ge(s2v3, NT + 1)
                v.tensor_scalar(g2a[:], g1a[:], -1.0, 1.0, ALU.mult, ALU.add)
                for t in range(NT):
                    v.tensor_scalar(m_all[:, 0, t, :], iot8, i1fa[:, t:t + 1],
                                    None, ALU.is_equal)
                    mi = v.tensor_scalar(m_all[:, 1, t, :], iot8, i2fa[:, t:t + 1],
                                         None, ALU.is_equal)
                    if t == NT - 1:
                        DR(mi)
                v.sem_inc(mdone, 1)
                v.wait_ge(prep, 1)
                v.tensor_copy(incl[:], ps_pre[:]).then_inc(cpd, 1)
                v.wait_ge(prep, 2)
                DR(v.tensor_copy(offa[0:1, :], ps_pre[0:1, :]))
                cur, nxt = offa, offb
                for sh in [8, 16, 32, 64]:
                    for j in range(2):
                        b = j * 128
                        v.tensor_copy(nxt[0:1, b:b + sh], cur[0:1, b:b + sh])
                        DR(v.tensor_tensor(nxt[0:1, b + sh:b + 128],
                                           cur[0:1, b + sh:b + 128],
                                           cur[0:1, b:b + 128 - sh], ALU.add))
                    cur, nxt = nxt, cur
                assert cur is offa
                v.tensor_copy(cntp[0:1, 0:8], cur[0:1, 120:128])
                v.tensor_copy(cntp[0:1, 8:16], cur[0:1, 248:256]).then_inc(cntr, 1)
                v.wait_ge(cntr, 1)
                DR(v.tensor_tensor(cur[0:1, :], cur[0:1, :], ps_pre[0:1, 0:256],
                                   ALU.subtract))
                v.tensor_tensor(bass.AP(cur, 128, [[2 * NT * E, 1], [8, NT], [1, 8]]),
                                bass.AP(cur, 128, [[2 * NT * E, 1], [8, NT], [1, 8]]),
                                bass.AP(cntp, 0, [[16, 1], [0, NT], [1, 8]]),
                                ALU.add).then_inc(offr, 1)
                v.wait_ge(offr, 1)
                v.wait_ge(offbs, 1)
                mflat = bass.AP(m_all, 0, [[256, P], [1, 256]])
                DR(v.tensor_tensor(rsc1[:], incl[:], mflat, ALU.subtract))
                DR(v.tensor_tensor(rsc1[:], rsc1[:], ps_off[:], ALU.add))
                DR(v.tensor_tensor(rsc1[:], rsc1[:], mflat, ALU.mult))
                DR(v.tensor_reduce(bass.AP(s_all, 0, [[2 * NT, P], [1, 2 * NT]]),
                                   bass.AP(rsc1, 0, [[256, P], [8, 2 * NT], [1, 8]]),
                                   AX.X, ALU.add))
                for j in range(2):
                    ifa = i1fa if j == 0 else i2fa
                    DR(v.tensor_scalar(slotf[:, j, :], ifa[:], float(CL), None, ALU.mult))
                    DR(v.tensor_tensor(slotf[:, j, :], slotf[:, j, :], s_all[:, j, :], ALU.add))
                    DR(v.tensor_scalar(ov1[:, j, :], s_all[:, j, :], CL - 0.5, None, ALU.is_gt))
                    DR(v.tensor_scalar(slosc[:, j, :], ov1[:, j, :], -1.0, 1.0,
                                       ALU.mult, ALU.add))
                    DR(v.tensor_tensor(slotf[:, j, :], slotf[:, j, :], slosc[:, j, :], ALU.mult))
                    DR(v.scalar_tensor_tensor(slotf[:, j, :], ov1[:, j, :], trash,
                                              slotf[:, j, :], ALU.mult, ALU.add))
                    DR(v.tensor_scalar(slotu[:, j, :], slotf[:, j, :], 0.0, None, ALU.add))
                    v.sem_inc(sltr, 1)

            @blk3.scalar
            def _(s):
                for t in range(NT):
                    if stage == 1:
                        break
                    s.wait_ge(v2s3, t + 1)
                    s.activation(sds[:, t % 2:t % 2 + 1], sdv[:, t % 2:t % 2 + 1],
                                 AF.Sqrt).then_inc(s2v3, 1)
                if stage >= 2:
                    s.wait_ge(v2s3, NT + 1)
                    s.activation(g1a[:], lda[:], AF.Sigmoid).then_inc(s2v3, 1)

            @blk3.tensor
            def _(tn):
                for i in range(KC):
                    tn.wait_ge(t3sem, 16 * (i + 1))
                    for _ in range(3):
                        tn.matmul(ps_pre[0:128, 0:128], tri[:, :], tri[:, :],
                                  start=True, stop=True)
                tn.wait_ge(wosem, 16 * KC)
                for t in range(NT):
                    if t >= 2:
                        tn.wait_ge(xmr, t - 1)
                    for kc in range(KC):
                        st_, sp_ = kc == 0, kc == KC - 1
                        lhs = acta[:, kc, t * P:(t + 1) * P]
                        tn.matmul(ps_wo[:, t % 2, 0:512], lhs, wo_sb[:, kc, 0:512],
                                  start=st_, stop=sp_)
                        mm = tn.matmul(ps_wo[:, t % 2, 512:1024], lhs, wo_sb[:, kc, 512:1024],
                                       start=st_, stop=sp_)
                    mm.then_inc(wops, 1)
                if stage == 1:
                    return
                for t in range(NT):
                    tn.wait_ge(x2st, 16 * (t + 1))
                    for _ in range(3):
                        tn.matmul(ps_pre[0:128, 0:128], tri[:, :], tri[:, :],
                                  start=True, stop=True)
                tn.wait_ge(t2sem, 16 * KC)
                for t in range(NT):
                    if t >= 2:
                        tn.wait_ge(lgc, t - 1)
                    for kc in range(KC):
                        mm = tn.matmul(ps_lg[:, t % 2, 0:E], x2fm[:, kc, t * P:(t + 1) * P],
                                       wgs[:, kc, :], start=(kc == 0), stop=(kc == KC - 1))
                    mm.then_inc(lgp, 1)
                tn.wait_ge(mdone, 1)
                tn.matmul(ps_pre[:], tri[:], bass.AP(m_all, 0, [[256, P], [1, 256]]),
                          start=True, stop=True).then_inc(prep, 1)
                tn.wait_ge(cpd, 1)
                tn.matmul(ps_pre[0:1, :], tri[:, 127:128],
                          bass.AP(m_all, 0, [[256, P], [1, 256]]),
                          start=True, stop=True).then_inc(prep, 1)
                tn.wait_ge(offr, 1)
                tn.matmul(ps_off[:], onesf[0:1, :], offa[0:1, :],
                          start=True, stop=True).then_inc(offbs, 1)
                for i in range(32):
                    tn.wait_ge(scat, 16 * (i + 1))
                    for _ in range(3):
                        tn.matmul(ps_pre[0:128, 0:128], tri[:, :], tri[:, :],
                                  start=True, stop=True)

        if stage == 1:
            return nc

        with ExitStack() as sC:
            EC = sC.enter_context
            wdb = EC(nc.sbuf_tensor("wdb", [P, 4, KC, D], BF16))
            h_sb = EC(nc.sbuf_tensor("h_sb", [P, 32, CL], BF16))
            eosb = EC(nc.sbuf_tensor("eosb", [P, 2, D], BF16))
            ps_h = EC(nc.psum_tensor("ps_h", [P, 2, D], F32))
            ps_eo = EC(nc.psum_tensor("ps_eo", [P, 4, 512], F32))
            ccs2 = EC(nc.semaphore("b4_cc"))
            cntl = EC(nc.semaphore("b4_cl"))
            dspt0 = EC(nc.semaphore("b4_dt0"))
            dspt1 = EC(nc.semaphore("b4_dt1"))
            dspt = [dspt0, dspt1]
            wdqa = EC(nc.semaphore("b4_wda"))
            wdqb = EC(nc.semaphore("b4_wdb"))
            wdqc = EC(nc.semaphore("b4_wdc"))
            wdqd_ = EC(nc.semaphore("b4_wdd"))
            wdq = [wdqa, wdqb, wdqc, wdqd_]
            hps = EC(nc.semaphore("b4_hp"))
            hev = EC(nc.semaphore("b4_he"))
            eops = EC(nc.semaphore("b4_eop"))
            eoev = EC(nc.semaphore("b4_eoe"))
            eost0 = EC(nc.semaphore("b4_eos0"))
            eost1 = EC(nc.semaphore("b4_eos1"))
            eost = [eost0, eost1]
            blk4 = EC(nc.Block())

            @blk4.sync
            def _(sy):
                def eo_stores(e):
                    nonlocal scount
                    for s in range(5):
                        scount += 1
                        sy.wait_ge(eoev, scount)
                        sy.dma_start(
                            out=eo_dram[e * CL + s * P:e * CL + (s + 1) * P, :],
                            in_=eosb[:, scount % 2, :]).then_inc(eost[scount % 2], 16)
                scount = 0
                for e in range(E):
                    if e >= 2:
                        sy.wait_ge(hps, 32 * (e - 1))
                    for kc in range(KC):
                        sy.dma_start(out=acta[:, kc, (e % 2) * CL:(e % 2) * CL + CL],
                                     in_=disp_dram[e * CL:(e + 1) * CL, kc * P:(kc + 1) * P],
                                     transpose=True).then_inc(dspt[e % 2], 16)
                    if e >= 1:
                        eo_stores(e - 1)
                eo_stores(E - 1)

            @blk4.gpsimd
            def _(g):
                g.collective_compute(
                    "AllGather", ALU.bypass,
                    replica_groups=[list(range(NCORE))],
                    ins=[cnt_in.ap().opt()], outs=[cnt_out.ap().opt()],
                ).then_inc(ccs2, 1)
                g.wait_ge(ccs2, 1)
                g.dma_start(out=cnts[:, :], in_=cnt_out.ap()).then_inc(cntl, 16)
                for e in range(E):
                    for kind, idx in [("wg", 0), ("wg", 1), ("wd", 0), ("wd", 1),
                                      ("wg", 2), ("wg", 3), ("wd", 2), ("wd", 3)]:
                        i = 4 * e + idx
                        if kind == "wg":
                            if i < 2:
                                continue
                            g.wait_ge(hps, 8 * (i - 1))
                            g.dma_start(
                                out=wga[:, i % 2, :, :],
                                in_=wge_quarter_ap(e, idx),
                            ).then_inc(wgq[i % 2], 16)
                        else:
                            if i >= 4:
                                g.wait_ge(eops, 3 * ((i - 4) // 4 + 1))
                            g.dma_start(
                                out=wdb[:, i % 4, :, :],
                                in_=bass.AP(wdn_h, e * FF * D + idx * KC * P * D,
                                            [[D, P], [P * D, KC], [1, D]]),
                            ).then_inc(wdq[i % 4], 16)

            @blk4.scalar
            def _(s):
                for e in range(E):
                    for fq in range(4):
                        for m in range(KC):
                            i = e * 32 + fq * 8 + m
                            s.wait_ge(hps, i + 1)
                            s.activation(h_sb[:, fq * 8 + m, :],
                                         ps_h[:, m % 2, 0:CL], AF.Gelu).then_inc(hev, 1)

            @blk4.vector
            def _(v):
                scount = 0
                for e in range(E):
                    for sgi, group in enumerate([[0, 1], [2, 3], [4]]):
                        v.wait_ge(eops, e * 3 + sgi + 1)
                        for s_ in group:
                            scount += 1
                            if scount >= 3:
                                v.wait_ge(eost[scount % 2], 16 * ((scount - 2 + 1) // 2))
                            v.tensor_copy(eosb[:, scount % 2, :],
                                          bass.AP(ps_eo, (2 * s_) % 4 * 512,
                                                  [[2048, P], [1, 1024]])
                                          ).then_inc(eoev, 1)

            @blk4.tensor
            def _(tn):
                for e in range(E):
                    tn.wait_ge(dspt[e % 2], 16 * KC * (e // 2 + 1))
                    for fq in range(4):
                        i = 4 * e + fq
                        tn.wait_ge(wgq[i % 2], 16 * (i // 2 + 1))
                        for m in range(KC):
                            mi = e * 32 + fq * 8 + m
                            if mi >= 2:
                                tn.wait_ge(hev, mi - 1)
                            for kc in range(KC):
                                st_, sp_ = kc == 0, kc == KC - 1
                                lhs = wga[:, i % 2, kc, m * P:(m + 1) * P]
                                tn.matmul(ps_h[:, m % 2, 0:512], lhs,
                                          acta[:, kc, (e % 2) * CL:(e % 2) * CL + 512],
                                          start=st_, stop=sp_)
                                mm = tn.matmul(ps_h[:, m % 2, 512:640], lhs,
                                               acta[:, kc, (e % 2) * CL + 512:(e % 2) * CL + CL],
                                               start=st_, stop=sp_)
                            mm.then_inc(hps, 1)
                    tn.wait_ge(hev, e * 32 + 32)
                    for sgi, group in enumerate([[0, 1], [2, 3], [4]]):
                        gidx = e * 3 + sgi
                        if gidx >= 1:
                            prevfull = (gidx - 1) // 3 * 5 + [0, 2, 4][(gidx - 1) % 3]
                            tn.wait_ge(eoev, prevfull + len([[0, 1], [2, 3], [4]][(gidx - 1) % 3]))
                        for kq in range(4):
                            i = 4 * e + kq
                            tn.wait_ge(wdq[i % 4], 16 * (i // 4 + 1))
                            for kc in range(KC):
                                st_ = (kq == 0 and kc == 0)
                                sp_ = (kq == 3 and kc == KC - 1)
                                for s_ in group:
                                    lhs = h_sb[:, kq * KC + kc, s_ * P:(s_ + 1) * P]
                                    tn.matmul(ps_eo[:, (2 * s_) % 4, :], lhs,
                                              wdb[:, i % 4, kc, 0:512],
                                              start=st_, stop=sp_)
                                    mm = tn.matmul(ps_eo[:, (2 * s_ + 1) % 4, :], lhs,
                                                   wdb[:, i % 4, kc, 512:1024],
                                                   start=st_, stop=sp_)
                        mm.then_inc(eops, 1)

        sPre.close()

        with ExitStack() as sD:
            ED = sD.enter_context
            eog = ED(nc.sbuf_tensor("eog", [P, 4, D], BF16))
            outsb = ED(nc.sbuf_tensor("outsb", [P, 2, D], F32))
            ps_go = ED(nc.psum_tensor("ps_go", [1, 16], F32))
            ps_gb = ED(nc.psum_tensor("ps_gb", [P, 16], F32))
            sc2b = ED(nc.sbuf_tensor("sc2b", [P, 2, D], F32))
            s5sem = ED(nc.semaphore("b5_s5"))
            gos = ED(nc.semaphore("b5_go"))
            gbs = ED(nc.semaphore("b5_gb"))
            gev = ED(nc.semaphore("b5_ge"))
            wrd = ED(nc.semaphore("b5_wr"))
            vdr5 = ED(nc.semaphore("b5_vdr"))
            gth0 = ED(nc.semaphore("b5_gt0"))
            gth1 = ED(nc.semaphore("b5_gt1"))
            gth = [gth0, gth1]
            xml0 = ED(nc.semaphore("b5_xm0"))
            xml1 = ED(nc.semaphore("b5_xm1"))
            xml = [xml0, xml1]
            cmb = ED(nc.semaphore("b5_cm"))
            osem0 = ED(nc.semaphore("b5_os0"))
            osem1 = ED(nc.semaphore("b5_os1"))
            osem = [osem0, osem1]
            blk5 = ED(nc.Block())

            @blk5.tensor
            def _(tn):
                tn.matmul(ps_go[0:1, :], prefm[:, 0:1], cnts[:, :], start=True, stop=False)
                tn.matmul(ps_go[0:1, 8:16], prefm[:, 1:2], cnts[:, 0:8],
                          start=False, stop=False)
                tn.matmul(ps_go[0:1, 8:16], prefm[:, 2:3], cnts[:, 0:8],
                          start=False, stop=True).then_inc(gos, 1)
                tn.wait_ge(gev, 1)
                tn.matmul(ps_gb[:, :], onesf[0:1, :], goffr[0:1, :],
                          start=True, stop=True).then_inc(gbs, 1)

            @blk5.vector
            def _(v):
                DR = mkdrain(v, vdr5)
                v.wait_ge(gos, 1)
                v.tensor_copy(goffr[0:1, :], ps_go[0:1, :]).then_inc(gev, 1)
                v.wait_ge(gbs, 1)
                mflat = bass.AP(m_all, 0, [[256, P], [1, 256]])
                DR(v.tensor_tensor(rsc1[:], mflat,
                                   bass.AP(ps_gb, 0, [[16, P], [8, 2], [0, NT], [1, 8]]),
                                   ALU.mult))
                DR(v.tensor_reduce(bass.AP(kg, 0, [[2 * NT, P], [1, 2 * NT]]),
                                   bass.AP(rsc1, 0, [[256, P], [8, 2 * NT], [1, 8]]),
                                   AX.X, ALU.add))
                DR(v.tensor_tensor(kg[:, :, :], kg[:, :, :], s_all[:, :, :], ALU.add))
                DR(v.tensor_scalar(bass.AP(kg, 0, [[2 * NT, P], [1, 2 * NT]]),
                                   bass.AP(kg, 0, [[2 * NT, P], [1, 2 * NT]]),
                                   CAP - 0.5, None, ALU.is_lt))
                DR(v.tensor_tensor(w1a[:], kg[:, 0, :], g1a[:], ALU.mult))
                v.tensor_tensor(w2a[:], kg[:, 1, :], g2a[:], ALU.mult).then_inc(wrd, 1)
                v.wait_ge(wrd, 1)
                for t in range(NT):
                    v.wait_ge(gth[t % 2], 32 * (t // 2 + 1))
                    v.wait_ge(xml[t % 2], 16 * (t // 2 + 1))
                    if t >= 2:
                        v.wait_ge(osem[t % 2], 16 * (t // 2))
                    v.wait_ge(s5sem, t + 1)
                    DR(v.scalar_tensor_tensor(sc1[:], eog[:, (t % 2) * 2 + 1, :],
                                              w2a[:, t:t + 1], sc2b[:, t % 2, :],
                                              ALU.mult, ALU.add))
                    v.tensor_tensor(outsb[:, t % 2, :], sc1[:], xt[:, t % 2, :],
                                    ALU.add).then_inc(cmb, 1)

            @blk5.scalar
            def _(s):
                s.wait_ge(wrd, 1)
                for t in range(NT):
                    s.wait_ge(gth[t % 2], 32 * (t // 2 + 1))
                    if t >= 2:
                        s.wait_ge(cmb, t - 1)
                    s.activation(sc2b[:, t % 2, :], eog[:, (t % 2) * 2, :],
                                 AF.Copy, scale=w1a[:, t:t + 1]).then_inc(s5sem, 1)

            @blk5.gpsimd
            def _(g):
                g.wait_ge(wrd, 1)
                for t in range(NT):
                    if t >= 2:
                        g.wait_ge(cmb, t - 1)
                    g.indirect_dma_start(
                        out=eog[:, (t % 2) * 2, :], out_offset=None,
                        in_=eo_dram.ap(), in_offset=IOA(slotu[:, 0, t:t + 1], 0),
                    ).then_inc(gth[t % 2], 16)
                    g.indirect_dma_start(
                        out=eog[:, (t % 2) * 2 + 1, :], out_offset=None,
                        in_=eo_dram.ap(), in_offset=IOA(slotu[:, 1, t:t + 1], 0),
                    ).then_inc(gth[t % 2], 16)

            @blk5.sync
            def _(sy):
                for t in range(NT):
                    if t >= 2:
                        sy.wait_ge(cmb, t - 1)
                    sy.dma_start(out=xt[:, t % 2, :],
                                 in_=xmid_dram[t * P:(t + 1) * P, :]).then_inc(xml[t % 2], 16)
                    if t >= 1:
                        sy.wait_ge(cmb, t)
                        sy.dma_start(out=out_d[(t - 1) * P:t * P, :],
                                     in_=outsb[:, (t - 1) % 2, :]).then_inc(osem[(t - 1) % 2], 16)
                sy.wait_ge(cmb, NT)
                sy.dma_start(out=out_d[(NT - 1) * P:NT * P, :],
                             in_=outsb[:, (NT - 1) % 2, :]).then_inc(osem[(NT - 1) % 2], 16)
                sy.wait_ge(osem[0], 16 * (NT // 2))
                sy.wait_ge(osem[1], 16 * (NT // 2))

    return nc


_NC_CACHE = {}


def kernel(L0_attn_x, L0_attn_Wq, L0_attn_Wk, L0_attn_Wv, L0_attn_Wg, L0_attn_bg,
           L0_attn_Wo, L0_ffn_wg, L0_ffn_w_gate, L0_ffn_w_down, stage=3, sub=9):
    x = np.ascontiguousarray(np.asarray(L0_attn_x, dtype=np.float32).reshape(4 * 4096, D))
    wq = np.ascontiguousarray(np.asarray(L0_attn_Wq, dtype=np.float32))
    wk = np.ascontiguousarray(np.asarray(L0_attn_Wk, dtype=np.float32))
    wv = np.ascontiguousarray(np.asarray(L0_attn_Wv, dtype=np.float32))
    wgt = np.ascontiguousarray(np.asarray(L0_attn_Wg, dtype=np.float32))
    wo = np.ascontiguousarray(np.asarray(L0_attn_Wo, dtype=np.float32))
    bg = np.asarray(L0_attn_bg, dtype=np.float32)
    wgr = np.ascontiguousarray(np.asarray(L0_ffn_wg, dtype=np.float32))
    wge = np.ascontiguousarray(np.asarray(L0_ffn_w_gate, dtype=np.float32).reshape(E * D, FF))
    wdn = np.ascontiguousarray(np.asarray(L0_ffn_w_down, dtype=np.float32).reshape(E * FF, D))
    bgb = np.ascontiguousarray(np.broadcast_to(bg[None, :], (P, D)))
    tri = np.ascontiguousarray(
        (np.arange(P)[:, None] <= np.arange(P)[None, :]).astype(ml_dtypes.bfloat16))
    csts = np.zeros((P, 16), dtype=np.float32)
    csts[:, 0:8] = np.arange(8, dtype=np.float32)[None, :]
    csts[:, 8] = SLOTS + np.arange(P, dtype=np.float32)
    onesf = np.ones((P, P), dtype=np.float32)

    if (stage, sub) not in _NC_CACHE:
        _NC_CACHE[(stage, sub)] = build(stage, sub)
    nc = _NC_CACHE[(stage, sub)]
    in_maps = []
    for c in range(NCORE):
        prefm = np.zeros((E, 3), dtype=np.float32)
        prefm[:, 0] = (np.arange(E) < c).astype(np.float32)
        prefm[:, 1] = 1.0
        prefm[:, 2] = -((np.arange(E) == c).astype(np.float32))
        in_maps.append(dict(
            x=np.ascontiguousarray(x[c * T:(c + 1) * T, :]),
            wq=wq, wk=wk, wv=wv, wgt=wgt, wo=wo, bgb=bgb, wgr=wgr,
            wge=wge, wdn=wdn, tri=tri, csts=csts, onesf=onesf, prefm=prefm,
        ))
    res = run_bass_kernel_spmd(nc, in_maps, core_ids=list(range(NCORE)))
    out = np.concatenate([res.results[c]["out"] for c in range(NCORE)], axis=0)
    return out.reshape(4, 4096, D)


# version 11
# speedup vs baseline: 1.0713x; 1.0018x over previous
import numpy as np
import ml_dtypes
from contextlib import ExitStack
import concourse.bass as bass
import concourse.mybir as mybir
from concourse.bass_utils import run_bass_kernel_spmd

F32 = mybir.dt.float32
BF16 = mybir.dt.bfloat16
U32 = mybir.dt.uint32
AF = mybir.ActivationFunctionType
ALU = mybir.AluOpType
AX = mybir.AxisListType
IOA = bass.IndirectOffsetOnAxis

P = 128
T = 2048
NT = 16
D = 1024
KC = 8
H = 16
E = 8
CL = 640
SLOTS = E * CL
DISP_ROWS = SLOTS + P
CAP = 4096.0
FF = 4096
EPS = 1e-5
SCALE = 64 ** -0.5
NCORE = 8


def build(stage=3, sub=9):
    nc = bass.Bass("TRN2", target_bir_lowering=False, detect_race_conditions=False)
    x_d = nc.dram_tensor("x", [T, D], F32, kind="ExternalInput").ap()
    wq_d = nc.dram_tensor("wq", [D, D], F32, kind="ExternalInput").ap()
    wk_d = nc.dram_tensor("wk", [D, D], F32, kind="ExternalInput").ap()
    wv_d = nc.dram_tensor("wv", [D, D], F32, kind="ExternalInput").ap()
    wgt_d = nc.dram_tensor("wgt", [D, D], F32, kind="ExternalInput").ap()
    wo_d = nc.dram_tensor("wo", [D, D], F32, kind="ExternalInput").ap()
    bgb_d = nc.dram_tensor("bgb", [P, D], F32, kind="ExternalInput").ap()
    wgr_d = nc.dram_tensor("wgr", [D, E], F32, kind="ExternalInput").ap()
    wge_h = nc.dram_tensor("wge", [E * D, FF], F32, kind="ExternalInput")
    wdn_h = nc.dram_tensor("wdn", [E * FF, D], F32, kind="ExternalInput")
    tri_d = nc.dram_tensor("tri", [P, P], BF16, kind="ExternalInput").ap()
    csts_d = nc.dram_tensor("csts", [P, 16], F32, kind="ExternalInput").ap()
    onesf_d = nc.dram_tensor("onesf", [P, P], F32, kind="ExternalInput").ap()
    prefm_d = nc.dram_tensor("prefm", [E, 3], F32, kind="ExternalInput").ap()
    out_d = nc.dram_tensor("out", [T, D], F32, kind="ExternalOutput").ap()

    xin1_dram = nc.dram_tensor("xin1_dram", [T, D], BF16)
    attn_dram = nc.dram_tensor("attn_dram", [T, D], BF16)
    xin2_dram = nc.dram_tensor("xin2_dram", [T, D], BF16)
    xmid_dram = nc.dram_tensor("xmid_dram", [T, D], F32)
    disp_dram = nc.dram_tensor("disp_dram", [DISP_ROWS, D], BF16)
    eo_dram = nc.dram_tensor("eo_dram", [DISP_ROWS, D], BF16)
    kv_in = nc.dram_tensor("kv_in", [P, 8 * 65], F32)
    kv_out = nc.dram_tensor("kv_out", [P, 8 * 65], F32)
    cnt_in = nc.dram_tensor("cnt_in", [1, 16], F32)
    cnt_out = nc.dram_tensor("cnt_out", [E, 16], F32)

    with ExitStack() as st:
        Ez = st.enter_context
        tri = Ez(nc.sbuf_tensor("tri_s", [P, P], BF16))
        csts = Ez(nc.sbuf_tensor("csts_s", [P, 16], F32))
        onesf = Ez(nc.sbuf_tensor("onesf_s", [P, P], F32))
        prefm = Ez(nc.sbuf_tensor("prefm_s", [E, 3], F32))

        junk = Ez(nc.sbuf_tensor("junk", [P, 8], F32))
        neg1 = Ez(nc.sbuf_tensor("neg1", [P, 1], F32))
        zb = Ez(nc.sbuf_tensor("zb", [P, D], BF16))
        wgs = Ez(nc.sbuf_tensor("wgs", [P, KC, E], BF16))
        acta = Ez(nc.sbuf_tensor("acta", [P, KC, T], BF16))
        bst = Ez(nc.sbuf_tensor("bst", [P, 12], F32))
        mv = Ez(nc.sbuf_tensor("mv", [P, 2, 2], F32))
        lt1 = Ez(nc.sbuf_tensor("lt1", [P, 4], F32))
        lt2 = Ez(nc.sbuf_tensor("lt2", [P, 4], F32))
        lt3 = Ez(nc.sbuf_tensor("lt3", [P, 1], F32))
        sdv = Ez(nc.sbuf_tensor("sdv", [P, 2], F32))
        sds = Ez(nc.sbuf_tensor("sds", [P, 2], F32))
        rvv = Ez(nc.sbuf_tensor("rvv", [P, 2], F32))
        xt = Ez(nc.sbuf_tensor("xt", [P, 2, D], F32))
        sc1 = Ez(nc.sbuf_tensor("sc1", [P, D], F32))
        sc2 = Ez(nc.sbuf_tensor("sc2", [P, D], F32))
        sc3 = Ez(nc.sbuf_tensor("sc3", [P, D], F32))
        gsc = Ez(nc.sbuf_tensor("gsc", [P, D], F32))
        tsc = Ez(nc.sbuf_tensor("tsc", [P, 256], F32))
        zq = Ez(nc.sbuf_tensor("zq", [P, H], F32))
        zr = Ez(nc.sbuf_tensor("zr", [P, H], F32))
        lgf = Ez(nc.sbuf_tensor("lgf", [P, NT, E], F32))
        tmaxs = Ez(nc.sbuf_tensor("tmaxs", [P, NT, E], F32))
        tidxs = Ez(nc.sbuf_tensor("tidxs", [P, NT, E], U32))
        i1fa = Ez(nc.sbuf_tensor("i1fa", [P, NT], F32))
        i2fa = Ez(nc.sbuf_tensor("i2fa", [P, NT], F32))
        lda = Ez(nc.sbuf_tensor("lda", [P, NT], F32))
        g1a = Ez(nc.sbuf_tensor("g1a", [P, NT], F32))
        g2a = Ez(nc.sbuf_tensor("g2a", [P, NT], F32))
        m_all = Ez(nc.sbuf_tensor("m_all", [P, 2, NT, E], BF16))
        incl = Ez(nc.sbuf_tensor("incl", [P, 2 * NT * E], F32))
        offa = Ez(nc.sbuf_tensor("offa", [P, 2 * NT * E], F32))
        offb = Ez(nc.sbuf_tensor("offb", [P, 2 * NT * E], F32))
        rsc1 = Ez(nc.sbuf_tensor("rsc1", [P, 2 * NT * E], F32))
        s_all = Ez(nc.sbuf_tensor("s_all", [P, 2, NT], F32))
        kg = Ez(nc.sbuf_tensor("kg", [P, 2, NT], F32))
        ov1 = Ez(nc.sbuf_tensor("ov1", [P, 2, NT], F32))
        slotf = Ez(nc.sbuf_tensor("slotf", [P, 2, NT], F32))
        slosc = Ez(nc.sbuf_tensor("slosc", [P, 2, NT], F32))
        slotu = Ez(nc.sbuf_tensor("slotu", [P, 2, NT], U32))
        cntp = Ez(nc.sbuf_tensor("cntp", [P, 16], F32))
        cnts = Ez(nc.sbuf_tensor("cnts", [E, 16], F32))
        goffr = Ez(nc.sbuf_tensor("goffr", [1, 16], F32))
        w1a = Ez(nc.sbuf_tensor("w1a", [P, NT], F32))
        w2a = Ez(nc.sbuf_tensor("w2a", [P, NT], F32))

        iot8 = csts[:, 0:8]
        trash = csts[:, 8:9]

        def sep(v):
            v.tensor_scalar(junk[:, 0:1], junk[:, 0:1], 1.0, None, ALU.add)

        def mkdrain(eng, sem):
            state = {"n": 0}

            def D(inst):
                state["n"] += 1
                inst.then_inc(sem, 1)
                eng.wait_ge(sem, state["n"])
                return inst
            return D

        def LNSTATS(v, DR, src_ap, tpar, sqbuf, sqwait=None):
                        DR(v.tensor_reduce(lt3[:], src_ap, AX.X, ALU.add))
                        DR(v.tensor_scalar(mv[:, tpar, 0:1], lt3[:], 1.0 / D, None, ALU.mult))
                        if sqwait is not None:
                            v.wait_ge(*sqwait)
                        else:
                            DR(v.tensor_tensor(sqbuf, src_ap, src_ap, ALU.mult))
                        DR(v.tensor_reduce(lt3[:], sqbuf, AX.X, ALU.add))
                        DR(v.tensor_scalar(lt3[:], lt3[:], 1.0 / D, EPS, ALU.mult,
                                           ALU.add))
                        DR(v.tensor_tensor(lt2[:, 1:2], mv[:, tpar, 0:1],
                                           mv[:, tpar, 0:1], ALU.mult))
                        DR(v.tensor_tensor(mv[:, tpar, 1:2], lt3[:], lt2[:, 1:2],
                                           ALU.subtract))

        with ExitStack() as sA:
            EA = sA.enter_context
            wa = EA(nc.sbuf_tensor("wa", [P, 4, KC, D], BF16))
            ktl = EA(nc.sbuf_tensor("ktl", [P, 2, D], BF16))
            vxt = EA(nc.sbuf_tensor("vxt", [P, 2, D], BF16))
            onebf = EA(nc.sbuf_tensor("onebf", [P, 1], BF16))
            qfm = EA(nc.sbuf_tensor("qfm", [P, 2, KC, 512], BF16))
            gtl = EA(nc.sbuf_tensor("gtl", [P, 2, D], BF16))
            x1s = EA(nc.sbuf_tensor("x1s", [P, 2, D], BF16))
            ats = EA(nc.sbuf_tensor("ats", [P, 2, D], BF16))
            bgb = EA(nc.sbuf_tensor("bgb_s", [P, D], F32))
            kvpk = EA(nc.sbuf_tensor("kvpk", [P, 8, 65], F32))
            kvsum = EA(nc.sbuf_tensor("kvsum", [P, 8, 65], F32))
            kvx = EA(nc.sbuf_tensor("kvx", [P, 8, 65], BF16))

            with ExitStack() as s1:
                E1 = s1.enter_context
                ps_k = E1(nc.psum_tensor("ps_k", [P, 2, D], F32))
                ps_v = E1(nc.psum_tensor("ps_v", [P, D], F32))
                ps_kvA = E1(nc.psum_tensor("ps_kvA", [P, 4, 128], F32))
                ps_kvB = E1(nc.psum_tensor("ps_kvB", [P, 4, 128], F32))
                xsem0 = E1(nc.semaphore("b1_x0"))
                xsem1 = E1(nc.semaphore("b1_x1p"))
                xsem = [xsem0, xsem1]
                csem = E1(nc.semaphore("b1_c"))
                wkvs = E1(nc.semaphore("b1_wkv"))
                wsem = E1(nc.semaphore("b1_w"))
                x1sem = E1(nc.semaphore("b1_x1"))
                stp0 = E1(nc.semaphore("b1_st0"))
                stp1 = E1(nc.semaphore("b1_st1"))
                stp = [stp0, stp1]
                tsem = E1(nc.semaphore("b1_t"))
                v2s = E1(nc.semaphore("b1_v2s"))
                s2v = E1(nc.semaphore("b1_s2v"))
                ksem = E1(nc.semaphore("b1_k"))
                kvrdy = E1(nc.semaphore("b1_kvr"))
                vesem = E1(nc.semaphore("b1_ve"))
                kelsem = E1(nc.semaphore("b1_kel"))
                kvmmd = E1(nc.semaphore("b1_kvm"))
                kvadd = E1(nc.semaphore("b1_kva"))
                kvev = E1(nc.semaphore("b1_kve"))
                gsem = E1(nc.semaphore("b1_g"))
                vdr = E1(nc.semaphore("b1_vdr"))
                zdsem = E1(nc.semaphore("b1_zd"))
                kvin_s = E1(nc.semaphore("b1_kvi"))
                sqd1 = E1(nc.semaphore("b1_sqd"))
                LAG = 3
                blk = E1(nc.Block())

                @blk.sync
                def _(sy):
                    sy.dma_start(out=tri[:], in_=tri_d).then_inc(csem, 16)
                    sy.dma_start(out=csts[:], in_=csts_d).then_inc(csem, 16)
                    sy.dma_start(out=onesf[:], in_=onesf_d).then_inc(csem, 16)
                    sy.dma_start(out=prefm[:], in_=prefm_d).then_inc(csem, 16)
                    sy.dma_start(out=bgb[:], in_=bgb_d).then_inc(csem, 16)
                    def TD(tl):
                        sy.wait_ge(stp[tl % 2], 16 * (tl // 2 + 1))
                        if tl >= 1:
                            sy.wait_ge(tsem, 16 * KC * tl)
                        for kc in range(KC):
                            sy.dma_start(out=acta[:, kc, tl * P:(tl + 1) * P],
                                         in_=xin1_dram[tl * P:(tl + 1) * P,
                                                       kc * P:(kc + 1) * P],
                                         transpose=True).then_inc(tsem, 16)

                    for t in range(NT):
                        if t >= 2:
                            sy.wait_ge(x1sem, t - 1)
                        sy.dma_start(out=xt[:, t % 2, :],
                                     in_=x_d[t * P:(t + 1) * P, :]).then_inc(xsem[t % 2], 16)
                        if t >= 1:
                            sy.wait_ge(x1sem, t)
                            sy.dma_start(out=xin1_dram[(t - 1) * P:t * P, :],
                                         in_=x1s[:, (t - 1) % 2, :]).then_inc(stp[(t - 1) % 2], 16)
                        if t >= 2:
                            TD(t - 2)
                    sy.wait_ge(x1sem, NT)
                    sy.dma_start(out=xin1_dram[(NT - 1) * P:NT * P, :],
                                 in_=x1s[:, (NT - 1) % 2, :]).then_inc(stp[(NT - 1) % 2], 16)
                    TD(NT - 2)
                    TD(NT - 1)
                    sy.wait_ge(gsem, 1)
                    for i in range(DISP_ROWS // P):
                        sy.dma_start(out=disp_dram[i * P:(i + 1) * P, :],
                                     in_=zb[:]).then_inc(zdsem, 16)

                @blk.gpsimd
                def _(g):
                    g.memset(zb[:], 0.0)
                    g.sem_inc(gsem, 1)
                    for i, wsrc in enumerate([wk_d, wv_d, wq_d, wgt_d]):
                        for kc in range(KC):
                            g.dma_start(out=wa[:, i, kc, :],
                                        in_=wsrc[kc * P:(kc + 1) * P, :]
                                        ).then_inc(wkvs if i < 2 else wsem, 16)
                    for kc in range(KC):
                        g.dma_start(out=wgs[:, kc, :],
                                    in_=wgr_d[kc * P:(kc + 1) * P, :]).then_inc(wsem, 16)
                    g.wait_ge(kvev, 1)
                    g.dma_start(out=kv_in.ap(), in_=kvpk[:, :, :]).then_inc(kvin_s, 16)
                    g.wait_ge(kvin_s, 16)

                @blk.vector
                def _(v):
                    DR = mkdrain(v, vdr)
                    DR(v.memset(junk[:], 0.0))
                    DR(v.memset(neg1[:], -1.0))
                    DR(v.memset(onebf[:], 1.0))
                    DR(v.memset(kvpk[:, :, :], 0.0))
                    v.wait_ge(csem, 16 * 5)
                    vskip = stage == 0 and (sub < 1 or sub == 10)
                    kvskip = stage == 0 and (sub < 2 or sub >= 10)

                    def EPI(u):
                        v.wait_ge(ksem, u + 1)
                        v.wait_ge(kelsem, u + 1)
                        v.tensor_tensor(ktl[:, u % 2, :], sc2[:], sc1[:], ALU.add)
                        v.tensor_copy(vxt[:, u % 2, :],
                                      ps_v[:, :]).then_inc(vesem, 1)
                        v.sem_inc(kvrdy, 1)
                        if u >= 1 and not kvskip:
                            v.wait_ge(kvmmd, u)
                            DR(v.tensor_tensor(kvpk[:, 0:4, :], kvpk[:, 0:4, :],
                                               ps_kvA[:, :, 0:65], ALU.add))
                            v.tensor_tensor(kvpk[:, 4:8, :], kvpk[:, 4:8, :],
                                            ps_kvB[:, :, 0:65], ALU.add).then_inc(kvadd, 1)
                            v.wait_ge(kvadd, u)

                    for t in range(NT):
                        v.wait_ge(xsem[t % 2], 16 * (t // 2 + 1))
                        LNSTATS(v, DR, xt[:, t % 2, :], t % 2, gsc[:],
                                sqwait=(sqd1, t + 1))
                        v.tensor_copy(sdv[:, t % 2:t % 2 + 1],
                                      mv[:, t % 2, 1:2]).then_inc(v2s, 1)
                        v.wait_ge(s2v, t + 1)
                        DR(v.reciprocal(rvv[:, t % 2:t % 2 + 1], sds[:, t % 2:t % 2 + 1]))
                        if t >= 2:
                            v.wait_ge(stp[t % 2], 16 * (t // 2))
                        v.tensor_scalar(x1s[:, t % 2, :], xt[:, t % 2, :],
                                        mv[:, t % 2, 0:1], rvv[:, t % 2:t % 2 + 1],
                                        ALU.subtract, ALU.mult).then_inc(x1sem, 1)
                        if t >= LAG and not vskip:
                            EPI(t - LAG)
                    if not vskip:
                        for u in range(NT - LAG, NT):
                            EPI(u)
                    if stage == 0 and (sub < 2 or sub >= 10):
                        v.sem_inc(kvev, 1)
                        return
                    v.wait_ge(kvmmd, NT)
                    DR(v.tensor_tensor(kvpk[:, 0:4, :], kvpk[:, 0:4, :],
                                       ps_kvA[:, :, 0:65], ALU.add))
                    v.tensor_tensor(kvpk[:, 4:8, :], kvpk[:, 4:8, :],
                                    ps_kvB[:, :, 0:65], ALU.add).then_inc(kvev, 1)
                    v.wait_ge(kvev, 1)

                @blk.scalar
                def _(s):
                    kskip = stage == 0 and (sub < 1 or sub == 10 or sub >= 12)

                    def KELU(u):
                        s.wait_ge(ksem, u + 1)
                        if u >= 1:
                            s.wait_ge(kvrdy, u)
                        s.activation(sc2[:], ps_k[:, u % 2, :], AF.Relu)
                        s.activation(sc3[:], ps_k[:, u % 2, :], AF.Relu,
                                     scale=neg1[:, 0:1])
                        s.activation(sc1[:], sc3[:], AF.Exp,
                                     scale=neg1[:, 0:1]).then_inc(kelsem, 1)

                    for t in range(NT):
                        s.wait_ge(xsem[t % 2], 16 * (t // 2 + 1))
                        if t >= 1:
                            s.wait_ge(v2s, t)
                        s.activation(gsc[:], xt[:, t % 2, :],
                                     AF.Square).then_inc(sqd1, 1)
                        s.wait_ge(v2s, t + 1)
                        s.activation(sds[:, t % 2:t % 2 + 1], sdv[:, t % 2:t % 2 + 1],
                                     AF.Sqrt).then_inc(s2v, 1)
                        if t >= LAG and not kskip:
                            KELU(t - LAG)
                    if not kskip:
                        for u in range(NT - LAG, NT):
                            KELU(u)

                @blk.tensor
                def _(tn):
                    if stage == 0 and sub < 1:
                        return
                    for t in range(3):
                        tn.wait_ge(x1sem, t + 1)
                        for _ in range(4):
                            tn.matmul(ps_k[:, 0, 0:128], tri[:, :], tri[:, :],
                                      start=True, stop=True)
                    tn.wait_ge(wkvs, 16 * 16)
                    for t in range(NT):
                        tn.wait_ge(tsem, 16 * KC * (t + 1))
                        if t >= 2 and sub != 10:
                            tn.wait_ge(kvrdy, t - 1)
                        if t >= 1 and sub != 10:
                            tn.wait_ge(vesem, t)
                        for kc in range(KC):
                            st_, sp_ = kc == 0, kc == KC - 1
                            lhs = acta[:, kc, t * P:(t + 1) * P]
                            tn.matmul(ps_k[:, t % 2, 0:512], lhs, wa[:, 0, kc, 0:512],
                                      start=st_, stop=sp_)
                            tn.matmul(ps_k[:, t % 2, 512:1024], lhs, wa[:, 0, kc, 512:1024],
                                      start=st_, stop=sp_)
                            tn.matmul(ps_v[:, 0:512], lhs, wa[:, 1, kc, 0:512],
                                      start=st_, stop=sp_)
                            mm = tn.matmul(ps_v[:, 512:1024], lhs, wa[:, 1, kc, 512:1024],
                                           start=st_, stop=sp_)
                        mm.then_inc(ksem, 1)
                        if t >= 1 and not (stage == 0 and (sub < 2 or sub >= 10)):
                            tn.wait_ge(kvrdy, t)
                            if t >= 2:
                                tn.wait_ge(kvadd, t - 1)
                            for h in range(H):
                                pst = ps_kvA if (h // 2) < 4 else ps_kvB
                                tp = (0, 64) if (h % 2) else None
                                po = (h % 2) * 64
                                blk_i = (h // 2) % 4
                                tn.matmul(pst[po:po + 64, blk_i, 0:64],
                                          ktl[:, (t - 1) % 2, h * 64:(h + 1) * 64],
                                          vxt[:, (t - 1) % 2, h * 64:(h + 1) * 64],
                                          start=True, stop=True, tile_position=tp)
                                mm = tn.matmul(pst[po:po + 64, blk_i, 64:65],
                                               ktl[:, (t - 1) % 2, h * 64:(h + 1) * 64],
                                               onebf[:, :],
                                               start=True, stop=True, tile_position=tp)
                            mm.then_inc(kvmmd, 1)
                    if stage == 0 and (sub < 2 or sub >= 10):
                        return
                    tn.wait_ge(kvrdy, NT)
                    tn.wait_ge(kvadd, NT - 1)
                    for h in range(H):
                        pst = ps_kvA if (h // 2) < 4 else ps_kvB
                        tp = (0, 64) if (h % 2) else None
                        po = (h % 2) * 64
                        blk_i = (h // 2) % 4
                        tn.matmul(pst[po:po + 64, blk_i, 0:64],
                                  ktl[:, (NT - 1) % 2, h * 64:(h + 1) * 64],
                                  vxt[:, (NT - 1) % 2, h * 64:(h + 1) * 64],
                                  start=True, stop=True, tile_position=tp)
                        mm = tn.matmul(pst[po:po + 64, blk_i, 64:65],
                                       ktl[:, (NT - 1) % 2, h * 64:(h + 1) * 64],
                                       onebf[:, :],
                                       start=True, stop=True, tile_position=tp)
                    mm.then_inc(kvmmd, 1)

            if stage == 5:
                pass
            if stage == 0:
                with ExitStack() as s0:
                    E0 = s0.enter_context
                    x1f = E0(nc.sbuf_tensor("x1f", [P, D], F32))
                    ktf = E0(nc.sbuf_tensor("ktf", [P, D], F32))
                    d0 = E0(nc.semaphore("s0_d"))
                    g0 = E0(nc.semaphore("s0_g"))
                    blk0 = E0(nc.Block())

                    @blk0.gpsimd
                    def _(g):
                        g.dma_start(out=ktf[:], in_=xin1_dram[0:P, :]).then_inc(g0, 16)

                    @blk0.vector
                    def _(v):
                        v.tensor_copy(ktf[:], ktl[:, (NT - 1) % 2, :])
                        v.tensor_copy(x1f[:, 0:12], bst[:])
                        v.tensor_copy(x1f[:, 16:20], bass.AP(mv, 0, [[4, P], [1, 4]]))
                        v.tensor_copy(x1f[:, 20:22], sdv[:])
                        v.tensor_copy(x1f[:, 22:24], rvv[:]).then_inc(g0, 1)

                    @blk0.sync
                    def _(sy):
                        sy.wait_ge(g0, 17)
                        sy.dma_start(out=out_d[P:2 * P, :], in_=x1f[:]).then_inc(d0, 16)
                        sy.dma_start(out=out_d[2 * P:3 * P, :], in_=ktf[:]).then_inc(d0, 16)
                        sy.wait_ge(d0, 32)
                return nc

            with ExitStack() as s2:
                E2 = s2.enter_context
                ps_q = E2(nc.psum_tensor("ps_q", [P, 2, 512], F32))
                ps_g = E2(nc.psum_tensor("ps_g", [P, 2, D], F32))
                ps_oA = E2(nc.psum_tensor("ps_oA", [P, 4, 128], F32))
                ps_oB = E2(nc.psum_tensor("ps_oB", [P, 4, 128], F32))
                osb = E2(nc.sbuf_tensor("osb", [P, 8, 128], F32))
                ccsem = E2(nc.semaphore("b2_cc"))
                kvls = E2(nc.semaphore("b2_kvl"))
                kvxs = E2(nc.semaphore("b2_kvx"))
                qp = E2(nc.semaphore("b2_qp"))
                qe = E2(nc.semaphore("b2_qe"))
                gp = E2(nc.semaphore("b2_gp"))
                ge = E2(nc.semaphore("b2_ge"))
                oms = E2(nc.semaphore("b2_om"))
                gat = E2(nc.semaphore("b2_gat"))
                asem0 = E2(nc.semaphore("b2_as0"))
                asem1 = E2(nc.semaphore("b2_as1"))
                asem = [asem0, asem1]
                v2s2 = E2(nc.semaphore("b2_v2s"))
                s2v2 = E2(nc.semaphore("b2_s2v"))
                g2s2 = E2(nc.semaphore("b2_g2s"))
                vdr2 = E2(nc.semaphore("b2_vdr"))
                blk2 = E2(nc.Block())

                @blk2.gpsimd
                def _(g):
                    g.collective_compute(
                        "AllReduce", ALU.add,
                        replica_groups=[[0, 1], [2, 3], [4, 5], [6, 7]],
                        ins=[kv_in.ap().opt()], outs=[kv_out.ap().opt()],
                    ).then_inc(ccsem, 1)
                    g.wait_ge(ccsem, 1)
                    g.dma_start(out=kvsum[:, :, :], in_=kv_out.ap()).then_inc(kvls, 16)

                @blk2.vector
                def _(v):
                    DR = mkdrain(v, vdr2)
                    if stage == 5 and sub == 20:
                        v.wait_ge(kvls, 16)
                        v.tensor_copy(kvx[:, :, :], kvsum[:, :, :]).then_inc(kvxs, 1)
                        return
                    for sl in range(4):
                        if sl == 1:
                            pass
                        for m in range(KC):
                            v.wait_ge(s2v2, sl * 8 + m + 1)
                            v.tensor_tensor(qfm[:, sl % 2, m, :], sc2[:, 0:512],
                                            sc1[:, 0:512], ALU.add).then_inc(qe, 1)
                        if sl == 0:
                            v.wait_ge(kvls, 16)
                            v.tensor_copy(kvx[:, :, :], kvsum[:, :, :]).then_inc(kvxs, 1)
                        if stage == 5 and sub == 21:
                            continue
                        for tt in range(4):
                            t = sl * 4 + tt
                            v.wait_ge(gp, t + 1)
                            if t >= 2 and not (stage == 5 and sub == 22):
                                v.wait_ge(asem[t % 2], 16 * (t // 2))
                            v.tensor_tensor(gsc[:], ps_g[:, t % 2, :], bgb[:],
                                            ALU.add).then_inc(v2s2, 1)
                            if stage == 5 and sub == 22:
                                continue
                            for part in range(2):
                                v.wait_ge(oms, 2 * t + part + 1)
                                if part == 0:
                                    v.wait_ge(g2s2, t + 1)
                                DR(v.tensor_copy(osb[:, 0:4, 0:65], ps_oA[:, :, 0:65]))
                                v.tensor_copy(osb[:, 4:8, 0:65],
                                              ps_oB[:, :, 0:65]).then_inc(gat, 1)
                                v.wait_ge(gat, 4 * t + 2 * part + 1)
                                DR(v.tensor_scalar(zq[:, part * 8:part * 8 + 8],
                                                   bass.AP(osb, 64, [[1024, P], [128, 8]]),
                                                   1e-6, None, ALU.max))
                                DR(v.reciprocal(zr[:, part * 8:part * 8 + 8],
                                                zq[:, part * 8:part * 8 + 8]))
                                DR(v.tensor_scalar(zr[:, part * 8:part * 8 + 8],
                                                   zr[:, part * 8:part * 8 + 8],
                                                   SCALE, None, ALU.mult))
                                for par in range(2):
                                    base = (t % 2) * D + part * 512 + par * 64
                                    DR(v.tensor_tensor(
                                        tsc[:, 0:256],
                                        bass.AP(osb, par * 4 * 128, [[1024, P], [128, 4], [1, 64]]),
                                        bass.AP(zr, part * 8 + par * 4,
                                                [[16, P], [1, 4], [0, 64]]),
                                        ALU.mult))
                                    mmv = v.tensor_tensor(
                                        bass.AP(ats, base, [[2 * D, P], [128, 4], [1, 64]]),
                                        tsc[:, 0:256],
                                        bass.AP(gtl, base, [[2 * D, P], [128, 4], [1, 64]]),
                                        ALU.mult)
                                mmv.then_inc(gat, 1)

                @blk2.scalar
                def _(s):
                    if stage == 5 and sub == 20:
                        return
                    for sl in range(4):
                        for m in range(KC):
                            i = sl * 8 + m
                            s.wait_ge(qp, i + 1)
                            if i >= 1:
                                s.wait_ge(qe, i)
                            s.activation(sc2[:, 0:512], ps_q[:, m % 2, :], AF.Relu)
                            s.activation(sc3[:, 0:512], ps_q[:, m % 2, :], AF.Relu,
                                         scale=neg1[:, 0:1])
                            s.activation(sc1[:, 0:512], sc3[:, 0:512], AF.Exp,
                                         scale=neg1[:, 0:1]).then_inc(s2v2, 1)
                        for tt in range(4):
                            if stage == 5 and sub == 21:
                                break
                            t = sl * 4 + tt
                            s.wait_ge(v2s2, t + 1)
                            s.activation(gtl[:, t % 2, :], gsc[:],
                                         AF.Sigmoid).then_inc(g2s2, 1)

                @blk2.tensor
                def _(tn):
                    if stage == 5 and sub == 20:
                        return
                    for sl in range(4):
                        for m in range(KC):
                            if sl * 8 + m >= 2:
                                tn.wait_ge(qe, sl * 8 + m - 1)
                            for kc in range(KC):
                                st_, sp_ = kc == 0, kc == KC - 1
                                mm = tn.matmul(ps_q[:, m % 2, :],
                                               wa[:, 2, kc, m * P:(m + 1) * P],
                                               acta[:, kc, sl * 512:(sl + 1) * 512],
                                               start=st_, stop=sp_)
                            mm.then_inc(qp, 1)
                        if stage == 5 and sub == 21:
                            continue
                        for tt in range(4):
                            t = sl * 4 + tt
                            if t >= 2:
                                tn.wait_ge(g2s2, t - 1)
                            for kc in range(KC):
                                st_, sp_ = kc == 0, kc == KC - 1
                                lhs = acta[:, kc, t * P:(t + 1) * P]
                                tn.matmul(ps_g[:, t % 2, 0:512], lhs,
                                          wa[:, 3, kc, 0:512], start=st_, stop=sp_)
                                mm = tn.matmul(ps_g[:, t % 2, 512:1024], lhs,
                                               wa[:, 3, kc, 512:1024], start=st_, stop=sp_)
                            mm.then_inc(gp, 1)
                            if stage == 5 and sub == 22:
                                continue
                            if t == 0:
                                tn.wait_ge(kvxs, 1)
                            tn.wait_ge(qe, sl * 8 + 8)
                            for part in range(2):
                                if t > 0 or part > 0:
                                    tn.wait_ge(gat, 4 * t + 2 * part - 1)
                                for par in range(2):
                                    pst = ps_oA if par == 0 else ps_oB
                                    for j in range(4):
                                        h = part * 8 + 2 * j + par
                                        mm = tn.matmul(pst[:, j, 0:65],
                                                       qfm[par * 64:par * 64 + 64,
                                                           sl % 2, h // 2, tt * P:(tt + 1) * P],
                                                       kvx[par * 64:par * 64 + 64, h // 2, :],
                                                       start=True, stop=True)
                                mm.then_inc(oms, 1)

                @blk2.sync
                def _(sy):
                    if stage == 5 and sub in (20, 21, 22):
                        return
                    for t in range(NT):
                        sy.wait_ge(gat, 4 * t + 4)
                        sy.dma_start(out=attn_dram[t * P:(t + 1) * P, :],
                                     in_=ats[:, t % 2, :]).then_inc(asem[t % 2], 16)
                    sy.wait_ge(asem[0], 16 * (NT // 2))
                    sy.wait_ge(asem[1], 16 * (NT // 2))

        if stage == 5:
            with ExitStack() as s5:
                E5 = s5.enter_context
                d5 = E5(nc.semaphore("s5_d"))
                blk5d = E5(nc.Block())

                @blk5d.sync
                def _(sy):
                    sy.dma_start(out=out_d[0:P, 0:520],
                                 in_=bass.AP(kvsum, 0, [[520, P], [1, 520]])).then_inc(d5, 16)
                    sy.wait_ge(d5, 16)
            return nc

        sPre = ExitStack()
        wga = sPre.enter_context(nc.sbuf_tensor("wga", [P, 2, KC, D], BF16))
        wgq = [sPre.enter_context(nc.semaphore(f"wgq{i}")) for i in range(2)]

        def wge_quarter_ap(e, q):
            return bass.AP(wge_h, e * D * FF + q * D,
                           [[FF, P], [P * FF, KC], [1, D]])

        with ExitStack() as sB:
            EB = sB.enter_context
            wo_sb = EB(nc.sbuf_tensor("wo_sb", [P, KC, D], BF16))
            xin2 = EB(nc.sbuf_tensor("xin2", [P, NT, D], BF16))
            x2fm = acta
            xms = EB(nc.sbuf_tensor("xms", [P, 2, D], F32))
            ps_wo = EB(nc.psum_tensor("ps_wo", [P, 2, D], F32))
            ps_lg = EB(nc.psum_tensor("ps_lg", [P, 2, 512], F32))
            ps_pre = EB(nc.psum_tensor("ps_pre", [P, 256], F32))
            ps_off = EB(nc.psum_tensor("ps_off", [P, 256], F32))
            t3sem = EB(nc.semaphore("b3_t"))
            wosem = EB(nc.semaphore("b3_wo"))
            xr0 = EB(nc.semaphore("b3_xr0"))
            xr1 = EB(nc.semaphore("b3_xr1"))
            xrsem = [xr0, xr1]
            wops = EB(nc.semaphore("b3_wop"))
            xmr = EB(nc.semaphore("b3_xmr"))
            xms0 = EB(nc.semaphore("b3_xms0"))
            xms1 = EB(nc.semaphore("b3_xms1"))
            xmst = [xms0, xms1]
            v2s3 = EB(nc.semaphore("b3_v2s"))
            s2v3 = EB(nc.semaphore("b3_s2v"))
            x2r = EB(nc.semaphore("b3_x2r"))
            x2st = EB(nc.semaphore("b3_x2s"))
            t2sem = EB(nc.semaphore("b3_t2"))
            lgp = EB(nc.semaphore("b3_lgp"))
            lgc = EB(nc.semaphore("b3_lgc"))
            mdone = EB(nc.semaphore("b3_md"))
            prep = EB(nc.semaphore("b3_pre"))
            cpd = EB(nc.semaphore("b3_cpd"))
            offr = EB(nc.semaphore("b3_offr"))
            offbs = EB(nc.semaphore("b3_offb"))
            sltr = EB(nc.semaphore("b3_slt"))
            cntr = EB(nc.semaphore("b3_cnt"))
            scat = EB(nc.semaphore("b3_sca"))
            cis = EB(nc.semaphore("b3_cis"))
            ostg1 = EB(nc.semaphore("b3_o1"))
            vdr3 = EB(nc.semaphore("b3_vdr"))
            blk3 = EB(nc.Block())

            @blk3.sync
            def _(sy):
                for kc in range(KC):
                    sy.dma_start(out=acta[:, kc, :],
                                 in_=attn_dram[0:T, kc * P:(kc + 1) * P],
                                 transpose=True).then_inc(t3sem, 16)
                for t in range(NT):
                    sy.wait_ge(xmr, t + 1)
                    sy.dma_start(out=xmid_dram[t * P:(t + 1) * P, :],
                                 in_=xms[:, t % 2, :]).then_inc(xmst[t % 2], 16)
                    if stage == 1:
                        sy.dma_start(out=out_d[t * P:(t + 1) * P, :],
                                     in_=xms[:, t % 2, :]).then_inc(xmst[t % 2], 16)
                if stage == 1:
                    sy.wait_ge(xmst[0], 32 * (NT // 2))
                    sy.wait_ge(xmst[1], 32 * (NT // 2))
                    return
                for t in range(NT):
                    sy.wait_ge(x2r, t + 1)
                    sy.dma_start(out=xin2_dram[t * P:(t + 1) * P, :],
                                 in_=xin2[:, t, :]).then_inc(x2st, 16)
                sy.wait_ge(x2st, 16 * NT)
                for kc in range(KC):
                    sy.dma_start(out=x2fm[:, kc, :],
                                 in_=xin2_dram[0:T, kc * P:(kc + 1) * P],
                                 transpose=True).then_inc(t2sem, 16)

            @blk3.gpsimd
            def _(g):
                for kc in range(KC):
                    g.dma_start(out=wo_sb[:, kc, :],
                                in_=wo_d[kc * P:(kc + 1) * P, :]).then_inc(wosem, 16)
                for t in range(NT):
                    if t >= 2:
                        g.wait_ge(xmr, t - 1)
                    g.dma_start(out=xt[:, t % 2, :],
                                in_=x_d[t * P:(t + 1) * P, :]).then_inc(xrsem[t % 2], 16)
                if stage == 1:
                    return
                g.dma_start(out=eo_dram[SLOTS:DISP_ROWS, :], in_=zb[:]).then_inc(cis, 16)
                for i in range(2):
                    g.dma_start(out=wga[:, i, :, :],
                                in_=wge_quarter_ap(0, i)).then_inc(wgq[i], 16)
                for j in range(2):
                    g.wait_ge(sltr, j + 1)
                    for t in range(NT):
                        g.indirect_dma_start(
                            out=disp_dram.ap(),
                            out_offset=IOA(slotu[:, j, t:t + 1], 0),
                            in_=xin2[:, t, :], in_offset=None,
                        ).then_inc(scat, 16)
                g.wait_ge(cntr, 1)
                g.dma_start(out=cnt_in.ap(), in_=cntp[0:1, :]).then_inc(cis, 16)
                g.wait_ge(cis, 32)
                g.wait_ge(scat, 16 * 32)

            @blk3.vector
            def _(v):
                DR = mkdrain(v, vdr3)
                xmul = 32 if stage == 1 else 16
                for t in range(NT):
                    v.wait_ge(wops, t + 1)
                    v.wait_ge(xrsem[t % 2], 16 * (t // 2 + 1))
                    if t >= 2:
                        v.wait_ge(xmst[t % 2], xmul * (t // 2))
                    v.tensor_tensor(xms[:, t % 2, :], xt[:, t % 2, :],
                                    ps_wo[:, t % 2, :], ALU.add).then_inc(xmr, 1)
                    v.wait_ge(xmr, t + 1)
                    if stage == 1:
                        continue
                    LNSTATS(v, DR, xms[:, t % 2, :], t % 2, sc1[:])
                    v.tensor_copy(sdv[:, t % 2:t % 2 + 1],
                                  mv[:, t % 2, 1:2]).then_inc(v2s3, 1)
                    v.wait_ge(s2v3, t + 1)
                    DR(v.reciprocal(rvv[:, t % 2:t % 2 + 1], sds[:, t % 2:t % 2 + 1]))
                    v.tensor_scalar(xin2[:, t, :], xms[:, t % 2, :],
                                    mv[:, t % 2, 0:1], rvv[:, t % 2:t % 2 + 1],
                                    ALU.subtract, ALU.mult).then_inc(x2r, 1)
                if stage == 1:
                    return
                for t in range(NT):
                    v.wait_ge(lgp, t + 1)
                    v.tensor_copy(lgf[:, t, :],
                                  ps_lg[:, t % 2, 0:E]).then_inc(lgc, 1)
                    if t == NT - 1:
                        v.wait_ge(lgc, NT)
                for t in range(NT):
                    DR(v.max(tmaxs[:, t, :], lgf[:, t, :]))
                for t in range(NT):
                    DR(v.max_index(tidxs[:, t, :], tmaxs[:, t, :], lgf[:, t, :]))
                DR(v.tensor_scalar(i1fa[:], bass.AP(tidxs, 0, [[NT * E, P], [E, NT]]),
                                   0.0, None, ALU.add))
                DR(v.tensor_scalar(i2fa[:], bass.AP(tidxs, 1, [[NT * E, P], [E, NT]]),
                                   0.0, None, ALU.add))
                v.tensor_tensor(lda[:], bass.AP(tmaxs, 0, [[NT * E, P], [E, NT]]),
                                bass.AP(tmaxs, 1, [[NT * E, P], [E, NT]]),
                                ALU.subtract).then_inc(v2s3, 1)
                v.wait_ge(s2v3, NT + 1)
                v.tensor_scalar(g2a[:], g1a[:], -1.0, 1.0, ALU.mult, ALU.add)
                for t in range(NT):
                    v.tensor_scalar(m_all[:, 0, t, :], iot8, i1fa[:, t:t + 1],
                                    None, ALU.is_equal)
                    mi = v.tensor_scalar(m_all[:, 1, t, :], iot8, i2fa[:, t:t + 1],
                                         None, ALU.is_equal)
                    if t == NT - 1:
                        DR(mi)
                v.sem_inc(mdone, 1)
                v.wait_ge(prep, 1)
                v.tensor_copy(incl[:], ps_pre[:]).then_inc(cpd, 1)
                v.wait_ge(prep, 2)
                DR(v.tensor_copy(offa[0:1, :], ps_pre[0:1, :]))
                cur, nxt = offa, offb
                for sh in [8, 16, 32, 64]:
                    for j in range(2):
                        b = j * 128
                        v.tensor_copy(nxt[0:1, b:b + sh], cur[0:1, b:b + sh])
                        DR(v.tensor_tensor(nxt[0:1, b + sh:b + 128],
                                           cur[0:1, b + sh:b + 128],
                                           cur[0:1, b:b + 128 - sh], ALU.add))
                    cur, nxt = nxt, cur
                assert cur is offa
                v.tensor_copy(cntp[0:1, 0:8], cur[0:1, 120:128])
                v.tensor_copy(cntp[0:1, 8:16], cur[0:1, 248:256]).then_inc(cntr, 1)
                v.wait_ge(cntr, 1)
                DR(v.tensor_tensor(cur[0:1, :], cur[0:1, :], ps_pre[0:1, 0:256],
                                   ALU.subtract))
                v.tensor_tensor(bass.AP(cur, 128, [[2 * NT * E, 1], [8, NT], [1, 8]]),
                                bass.AP(cur, 128, [[2 * NT * E, 1], [8, NT], [1, 8]]),
                                bass.AP(cntp, 0, [[16, 1], [0, NT], [1, 8]]),
                                ALU.add).then_inc(offr, 1)
                v.wait_ge(offr, 1)
                v.wait_ge(offbs, 1)
                mflat = bass.AP(m_all, 0, [[256, P], [1, 256]])
                DR(v.tensor_tensor(rsc1[:], incl[:], mflat, ALU.subtract))
                DR(v.tensor_tensor(rsc1[:], rsc1[:], ps_off[:], ALU.add))
                DR(v.tensor_tensor(rsc1[:], rsc1[:], mflat, ALU.mult))
                DR(v.tensor_reduce(bass.AP(s_all, 0, [[2 * NT, P], [1, 2 * NT]]),
                                   bass.AP(rsc1, 0, [[256, P], [8, 2 * NT], [1, 8]]),
                                   AX.X, ALU.add))
                for j in range(2):
                    ifa = i1fa if j == 0 else i2fa
                    DR(v.tensor_scalar(slotf[:, j, :], ifa[:], float(CL), None, ALU.mult))
                    DR(v.tensor_tensor(slotf[:, j, :], slotf[:, j, :], s_all[:, j, :], ALU.add))
                    DR(v.tensor_scalar(ov1[:, j, :], s_all[:, j, :], CL - 0.5, None, ALU.is_gt))
                    DR(v.tensor_scalar(slosc[:, j, :], ov1[:, j, :], -1.0, 1.0,
                                       ALU.mult, ALU.add))
                    DR(v.tensor_tensor(slotf[:, j, :], slotf[:, j, :], slosc[:, j, :], ALU.mult))
                    DR(v.scalar_tensor_tensor(slotf[:, j, :], ov1[:, j, :], trash,
                                              slotf[:, j, :], ALU.mult, ALU.add))
                    DR(v.tensor_scalar(slotu[:, j, :], slotf[:, j, :], 0.0, None, ALU.add))
                    v.sem_inc(sltr, 1)

            @blk3.scalar
            def _(s):
                for t in range(NT):
                    if stage == 1:
                        break
                    s.wait_ge(v2s3, t + 1)
                    s.activation(sds[:, t % 2:t % 2 + 1], sdv[:, t % 2:t % 2 + 1],
                                 AF.Sqrt).then_inc(s2v3, 1)
                if stage >= 2:
                    s.wait_ge(v2s3, NT + 1)
                    s.activation(g1a[:], lda[:], AF.Sigmoid).then_inc(s2v3, 1)

            @blk3.tensor
            def _(tn):
                for i in range(KC):
                    tn.wait_ge(t3sem, 16 * (i + 1))
                    for _ in range(3):
                        tn.matmul(ps_pre[0:128, 0:128], tri[:, :], tri[:, :],
                                  start=True, stop=True)
                tn.wait_ge(wosem, 16 * KC)
                for t in range(NT):
                    if t >= 2:
                        tn.wait_ge(xmr, t - 1)
                    for kc in range(KC):
                        st_, sp_ = kc == 0, kc == KC - 1
                        lhs = acta[:, kc, t * P:(t + 1) * P]
                        tn.matmul(ps_wo[:, t % 2, 0:512], lhs, wo_sb[:, kc, 0:512],
                                  start=st_, stop=sp_)
                        mm = tn.matmul(ps_wo[:, t % 2, 512:1024], lhs, wo_sb[:, kc, 512:1024],
                                       start=st_, stop=sp_)
                    mm.then_inc(wops, 1)
                if stage == 1:
                    return
                for t in range(NT):
                    tn.wait_ge(x2st, 16 * (t + 1))
                    for _ in range(3):
                        tn.matmul(ps_pre[0:128, 0:128], tri[:, :], tri[:, :],
                                  start=True, stop=True)
                tn.wait_ge(t2sem, 16 * KC)
                for t in range(NT):
                    if t >= 2:
                        tn.wait_ge(lgc, t - 1)
                    for kc in range(KC):
                        mm = tn.matmul(ps_lg[:, t % 2, 0:E], x2fm[:, kc, t * P:(t + 1) * P],
                                       wgs[:, kc, :], start=(kc == 0), stop=(kc == KC - 1))
                    mm.then_inc(lgp, 1)
                tn.wait_ge(mdone, 1)
                tn.matmul(ps_pre[:], tri[:], bass.AP(m_all, 0, [[256, P], [1, 256]]),
                          start=True, stop=True).then_inc(prep, 1)
                tn.wait_ge(cpd, 1)
                tn.matmul(ps_pre[0:1, :], tri[:, 127:128],
                          bass.AP(m_all, 0, [[256, P], [1, 256]]),
                          start=True, stop=True).then_inc(prep, 1)
                tn.wait_ge(offr, 1)
                tn.matmul(ps_off[:], onesf[0:1, :], offa[0:1, :],
                          start=True, stop=True).then_inc(offbs, 1)
                for i in range(32):
                    tn.wait_ge(scat, 16 * (i + 1))
                    for _ in range(3):
                        tn.matmul(ps_pre[0:128, 0:128], tri[:, :], tri[:, :],
                                  start=True, stop=True)

        if stage == 1:
            return nc

        with ExitStack() as sC:
            EC = sC.enter_context
            wdb = EC(nc.sbuf_tensor("wdb", [P, 4, KC, D], BF16))
            h_sb = EC(nc.sbuf_tensor("h_sb", [P, 32, CL], BF16))
            eosb = EC(nc.sbuf_tensor("eosb", [P, 2, D], BF16))
            ps_h = EC(nc.psum_tensor("ps_h", [P, 2, D], F32))
            ps_eo = EC(nc.psum_tensor("ps_eo", [P, 4, 512], F32))
            ccs2 = EC(nc.semaphore("b4_cc"))
            cntl = EC(nc.semaphore("b4_cl"))
            dspt0 = EC(nc.semaphore("b4_dt0"))
            dspt1 = EC(nc.semaphore("b4_dt1"))
            dspt = [dspt0, dspt1]
            wdqa = EC(nc.semaphore("b4_wda"))
            wdqb = EC(nc.semaphore("b4_wdb"))
            wdqc = EC(nc.semaphore("b4_wdc"))
            wdqd_ = EC(nc.semaphore("b4_wdd"))
            wdq = [wdqa, wdqb, wdqc, wdqd_]
            hps = EC(nc.semaphore("b4_hp"))
            hev = EC(nc.semaphore("b4_he"))
            eops = EC(nc.semaphore("b4_eop"))
            eoev = EC(nc.semaphore("b4_eoe"))
            eost0 = EC(nc.semaphore("b4_eos0"))
            eost1 = EC(nc.semaphore("b4_eos1"))
            eost = [eost0, eost1]
            blk4 = EC(nc.Block())

            @blk4.sync
            def _(sy):
                def eo_stores(e):
                    nonlocal scount
                    for s in range(5):
                        scount += 1
                        sy.wait_ge(eoev, scount)
                        sy.dma_start(
                            out=eo_dram[e * CL + s * P:e * CL + (s + 1) * P, :],
                            in_=eosb[:, scount % 2, :]).then_inc(eost[scount % 2], 16)
                scount = 0
                for e in range(E):
                    if e >= 2:
                        sy.wait_ge(hps, 32 * (e - 1))
                    for kc in range(KC):
                        sy.dma_start(out=acta[:, kc, (e % 2) * CL:(e % 2) * CL + CL],
                                     in_=disp_dram[e * CL:(e + 1) * CL, kc * P:(kc + 1) * P],
                                     transpose=True).then_inc(dspt[e % 2], 16)
                    if e >= 1:
                        eo_stores(e - 1)
                eo_stores(E - 1)

            @blk4.gpsimd
            def _(g):
                g.collective_compute(
                    "AllGather", ALU.bypass,
                    replica_groups=[list(range(NCORE))],
                    ins=[cnt_in.ap().opt()], outs=[cnt_out.ap().opt()],
                ).then_inc(ccs2, 1)
                g.wait_ge(ccs2, 1)
                g.dma_start(out=cnts[:, :], in_=cnt_out.ap()).then_inc(cntl, 16)
                for e in range(E):
                    for kind, idx in [("wg", 0), ("wg", 1), ("wd", 0), ("wd", 1),
                                      ("wg", 2), ("wg", 3), ("wd", 2), ("wd", 3)]:
                        i = 4 * e + idx
                        if kind == "wg":
                            if i < 2:
                                continue
                            g.wait_ge(hps, 8 * (i - 1))
                            g.dma_start(
                                out=wga[:, i % 2, :, :],
                                in_=wge_quarter_ap(e, idx),
                            ).then_inc(wgq[i % 2], 16)
                        else:
                            if i >= 4:
                                g.wait_ge(eops, 3 * ((i - 4) // 4 + 1))
                            g.dma_start(
                                out=wdb[:, i % 4, :, :],
                                in_=bass.AP(wdn_h, e * FF * D + idx * KC * P * D,
                                            [[D, P], [P * D, KC], [1, D]]),
                            ).then_inc(wdq[i % 4], 16)

            @blk4.scalar
            def _(s):
                for e in range(E):
                    for fq in range(4):
                        for m in range(KC):
                            i = e * 32 + fq * 8 + m
                            s.wait_ge(hps, i + 1)
                            s.activation(h_sb[:, fq * 8 + m, :],
                                         ps_h[:, m % 2, 0:CL], AF.Gelu).then_inc(hev, 1)

            @blk4.vector
            def _(v):
                scount = 0
                for e in range(E):
                    for sgi, group in enumerate([[0, 1], [2, 3], [4]]):
                        v.wait_ge(eops, e * 3 + sgi + 1)
                        for s_ in group:
                            scount += 1
                            if scount >= 3:
                                v.wait_ge(eost[scount % 2], 16 * ((scount - 2 + 1) // 2))
                            v.tensor_copy(eosb[:, scount % 2, :],
                                          bass.AP(ps_eo, (2 * s_) % 4 * 512,
                                                  [[2048, P], [1, 1024]])
                                          ).then_inc(eoev, 1)

            @blk4.tensor
            def _(tn):
                for e in range(E):
                    tn.wait_ge(dspt[e % 2], 16 * KC * (e // 2 + 1))
                    for fq in range(4):
                        i = 4 * e + fq
                        tn.wait_ge(wgq[i % 2], 16 * (i // 2 + 1))
                        for m in range(KC):
                            mi = e * 32 + fq * 8 + m
                            if mi >= 2:
                                tn.wait_ge(hev, mi - 1)
                            for kc in range(KC):
                                st_, sp_ = kc == 0, kc == KC - 1
                                lhs = wga[:, i % 2, kc, m * P:(m + 1) * P]
                                tn.matmul(ps_h[:, m % 2, 0:512], lhs,
                                          acta[:, kc, (e % 2) * CL:(e % 2) * CL + 512],
                                          start=st_, stop=sp_)
                                mm = tn.matmul(ps_h[:, m % 2, 512:640], lhs,
                                               acta[:, kc, (e % 2) * CL + 512:(e % 2) * CL + CL],
                                               start=st_, stop=sp_)
                            mm.then_inc(hps, 1)
                    tn.wait_ge(hev, e * 32 + 32)
                    for sgi, group in enumerate([[0, 1], [2, 3], [4]]):
                        gidx = e * 3 + sgi
                        if gidx >= 1:
                            prevfull = (gidx - 1) // 3 * 5 + [0, 2, 4][(gidx - 1) % 3]
                            tn.wait_ge(eoev, prevfull + len([[0, 1], [2, 3], [4]][(gidx - 1) % 3]))
                        for kq in range(4):
                            i = 4 * e + kq
                            tn.wait_ge(wdq[i % 4], 16 * (i // 4 + 1))
                            for kc in range(KC):
                                st_ = (kq == 0 and kc == 0)
                                sp_ = (kq == 3 and kc == KC - 1)
                                for s_ in group:
                                    lhs = h_sb[:, kq * KC + kc, s_ * P:(s_ + 1) * P]
                                    tn.matmul(ps_eo[:, (2 * s_) % 4, :], lhs,
                                              wdb[:, i % 4, kc, 0:512],
                                              start=st_, stop=sp_)
                                    mm = tn.matmul(ps_eo[:, (2 * s_ + 1) % 4, :], lhs,
                                                   wdb[:, i % 4, kc, 512:1024],
                                                   start=st_, stop=sp_)
                        mm.then_inc(eops, 1)

        sPre.close()

        with ExitStack() as sD:
            ED = sD.enter_context
            eog = ED(nc.sbuf_tensor("eog", [P, 4, D], BF16))
            outsb = ED(nc.sbuf_tensor("outsb", [P, 2, D], F32))
            ps_go = ED(nc.psum_tensor("ps_go", [1, 16], F32))
            ps_gb = ED(nc.psum_tensor("ps_gb", [P, 16], F32))
            sc2b = ED(nc.sbuf_tensor("sc2b", [P, 2, D], F32))
            s5sem = ED(nc.semaphore("b5_s5"))
            gos = ED(nc.semaphore("b5_go"))
            gbs = ED(nc.semaphore("b5_gb"))
            gev = ED(nc.semaphore("b5_ge"))
            wrd = ED(nc.semaphore("b5_wr"))
            vdr5 = ED(nc.semaphore("b5_vdr"))
            gth0 = ED(nc.semaphore("b5_gt0"))
            gth1 = ED(nc.semaphore("b5_gt1"))
            gth = [gth0, gth1]
            xml0 = ED(nc.semaphore("b5_xm0"))
            xml1 = ED(nc.semaphore("b5_xm1"))
            xml = [xml0, xml1]
            cmb = ED(nc.semaphore("b5_cm"))
            osem0 = ED(nc.semaphore("b5_os0"))
            osem1 = ED(nc.semaphore("b5_os1"))
            osem = [osem0, osem1]
            blk5 = ED(nc.Block())

            @blk5.tensor
            def _(tn):
                tn.matmul(ps_go[0:1, :], prefm[:, 0:1], cnts[:, :], start=True, stop=False)
                tn.matmul(ps_go[0:1, 8:16], prefm[:, 1:2], cnts[:, 0:8],
                          start=False, stop=False)
                tn.matmul(ps_go[0:1, 8:16], prefm[:, 2:3], cnts[:, 0:8],
                          start=False, stop=True).then_inc(gos, 1)
                tn.wait_ge(gev, 1)
                tn.matmul(ps_gb[:, :], onesf[0:1, :], goffr[0:1, :],
                          start=True, stop=True).then_inc(gbs, 1)

            @blk5.vector
            def _(v):
                DR = mkdrain(v, vdr5)
                v.wait_ge(gos, 1)
                v.tensor_copy(goffr[0:1, :], ps_go[0:1, :]).then_inc(gev, 1)
                v.wait_ge(gbs, 1)
                mflat = bass.AP(m_all, 0, [[256, P], [1, 256]])
                DR(v.tensor_tensor(rsc1[:], mflat,
                                   bass.AP(ps_gb, 0, [[16, P], [8, 2], [0, NT], [1, 8]]),
                                   ALU.mult))
                DR(v.tensor_reduce(bass.AP(kg, 0, [[2 * NT, P], [1, 2 * NT]]),
                                   bass.AP(rsc1, 0, [[256, P], [8, 2 * NT], [1, 8]]),
                                   AX.X, ALU.add))
                DR(v.tensor_tensor(kg[:, :, :], kg[:, :, :], s_all[:, :, :], ALU.add))
                DR(v.tensor_scalar(bass.AP(kg, 0, [[2 * NT, P], [1, 2 * NT]]),
                                   bass.AP(kg, 0, [[2 * NT, P], [1, 2 * NT]]),
                                   CAP - 0.5, None, ALU.is_lt))
                DR(v.tensor_tensor(w1a[:], kg[:, 0, :], g1a[:], ALU.mult))
                v.tensor_tensor(w2a[:], kg[:, 1, :], g2a[:], ALU.mult).then_inc(wrd, 1)
                v.wait_ge(wrd, 1)
                for t in range(NT):
                    v.wait_ge(gth[t % 2], 32 * (t // 2 + 1))
                    v.wait_ge(xml[t % 2], 16 * (t // 2 + 1))
                    if t >= 2:
                        v.wait_ge(osem[t % 2], 16 * (t // 2))
                    v.wait_ge(s5sem, t + 1)
                    DR(v.scalar_tensor_tensor(sc1[:], eog[:, (t % 2) * 2 + 1, :],
                                              w2a[:, t:t + 1], sc2b[:, t % 2, :],
                                              ALU.mult, ALU.add))
                    v.tensor_tensor(outsb[:, t % 2, :], sc1[:], xt[:, t % 2, :],
                                    ALU.add).then_inc(cmb, 1)

            @blk5.scalar
            def _(s):
                s.wait_ge(wrd, 1)
                for t in range(NT):
                    s.wait_ge(gth[t % 2], 32 * (t // 2 + 1))
                    if t >= 2:
                        s.wait_ge(cmb, t - 1)
                    s.activation(sc2b[:, t % 2, :], eog[:, (t % 2) * 2, :],
                                 AF.Copy, scale=w1a[:, t:t + 1]).then_inc(s5sem, 1)

            @blk5.gpsimd
            def _(g):
                g.wait_ge(wrd, 1)
                for t in range(NT):
                    if t >= 2:
                        g.wait_ge(cmb, t - 1)
                    g.indirect_dma_start(
                        out=eog[:, (t % 2) * 2, :], out_offset=None,
                        in_=eo_dram.ap(), in_offset=IOA(slotu[:, 0, t:t + 1], 0),
                    ).then_inc(gth[t % 2], 16)
                    g.indirect_dma_start(
                        out=eog[:, (t % 2) * 2 + 1, :], out_offset=None,
                        in_=eo_dram.ap(), in_offset=IOA(slotu[:, 1, t:t + 1], 0),
                    ).then_inc(gth[t % 2], 16)

            @blk5.sync
            def _(sy):
                for t in range(NT):
                    if t >= 2:
                        sy.wait_ge(cmb, t - 1)
                    sy.dma_start(out=xt[:, t % 2, :],
                                 in_=xmid_dram[t * P:(t + 1) * P, :]).then_inc(xml[t % 2], 16)
                    if t >= 1:
                        sy.wait_ge(cmb, t)
                        sy.dma_start(out=out_d[(t - 1) * P:t * P, :],
                                     in_=outsb[:, (t - 1) % 2, :]).then_inc(osem[(t - 1) % 2], 16)
                sy.wait_ge(cmb, NT)
                sy.dma_start(out=out_d[(NT - 1) * P:NT * P, :],
                             in_=outsb[:, (NT - 1) % 2, :]).then_inc(osem[(NT - 1) % 2], 16)
                sy.wait_ge(osem[0], 16 * (NT // 2))
                sy.wait_ge(osem[1], 16 * (NT // 2))

    return nc


_NC_CACHE = {}


def kernel(L0_attn_x, L0_attn_Wq, L0_attn_Wk, L0_attn_Wv, L0_attn_Wg, L0_attn_bg,
           L0_attn_Wo, L0_ffn_wg, L0_ffn_w_gate, L0_ffn_w_down, stage=3, sub=9):
    x = np.ascontiguousarray(np.asarray(L0_attn_x, dtype=np.float32).reshape(4 * 4096, D))
    wq = np.ascontiguousarray(np.asarray(L0_attn_Wq, dtype=np.float32))
    wk = np.ascontiguousarray(np.asarray(L0_attn_Wk, dtype=np.float32))
    wv = np.ascontiguousarray(np.asarray(L0_attn_Wv, dtype=np.float32))
    wgt = np.ascontiguousarray(np.asarray(L0_attn_Wg, dtype=np.float32))
    wo = np.ascontiguousarray(np.asarray(L0_attn_Wo, dtype=np.float32))
    bg = np.asarray(L0_attn_bg, dtype=np.float32)
    wgr = np.ascontiguousarray(np.asarray(L0_ffn_wg, dtype=np.float32))
    wge = np.ascontiguousarray(np.asarray(L0_ffn_w_gate, dtype=np.float32).reshape(E * D, FF))
    wdn = np.ascontiguousarray(np.asarray(L0_ffn_w_down, dtype=np.float32).reshape(E * FF, D))
    bgb = np.ascontiguousarray(np.broadcast_to(bg[None, :], (P, D)))
    tri = np.ascontiguousarray(
        (np.arange(P)[:, None] <= np.arange(P)[None, :]).astype(ml_dtypes.bfloat16))
    csts = np.zeros((P, 16), dtype=np.float32)
    csts[:, 0:8] = np.arange(8, dtype=np.float32)[None, :]
    csts[:, 8] = SLOTS + np.arange(P, dtype=np.float32)
    onesf = np.ones((P, P), dtype=np.float32)

    if (stage, sub) not in _NC_CACHE:
        _NC_CACHE[(stage, sub)] = build(stage, sub)
    nc = _NC_CACHE[(stage, sub)]
    in_maps = []
    for c in range(NCORE):
        prefm = np.zeros((E, 3), dtype=np.float32)
        prefm[:, 0] = (np.arange(E) < c).astype(np.float32)
        prefm[:, 1] = 1.0
        prefm[:, 2] = -((np.arange(E) == c).astype(np.float32))
        in_maps.append(dict(
            x=np.ascontiguousarray(x[c * T:(c + 1) * T, :]),
            wq=wq, wk=wk, wv=wv, wgt=wgt, wo=wo, bgb=bgb, wgr=wgr,
            wge=wge, wdn=wdn, tri=tri, csts=csts, onesf=onesf, prefm=prefm,
        ))
    res = run_bass_kernel_spmd(nc, in_maps, core_ids=list(range(NCORE)))
    out = np.concatenate([res.results[c]["out"] for c in range(NCORE)], axis=0)
    return out.reshape(4, 4096, D)


# version 12 (final)
# speedup vs baseline: 1.0818x; 1.0099x over previous
import numpy as np
import ml_dtypes
from contextlib import ExitStack
import concourse.bass as bass
import concourse.mybir as mybir
from concourse.bass_utils import run_bass_kernel_spmd

F32 = mybir.dt.float32
BF16 = mybir.dt.bfloat16
U32 = mybir.dt.uint32
AF = mybir.ActivationFunctionType
ALU = mybir.AluOpType
AX = mybir.AxisListType
IOA = bass.IndirectOffsetOnAxis

P = 128
T = 2048
NT = 16
D = 1024
KC = 8
H = 16
E = 8
CL = 640
SLOTS = E * CL
DISP_ROWS = SLOTS + P
CAP = 4096.0
FF = 4096
EPS = 1e-5
SCALE = 64 ** -0.5
NCORE = 8


def build(stage=3, sub=9):
    nc = bass.Bass("TRN2", target_bir_lowering=False, detect_race_conditions=False)
    x_d = nc.dram_tensor("x", [T, D], F32, kind="ExternalInput").ap()
    wq_d = nc.dram_tensor("wq", [D, D], F32, kind="ExternalInput").ap()
    wk_d = nc.dram_tensor("wk", [D, D], F32, kind="ExternalInput").ap()
    wv_d = nc.dram_tensor("wv", [D, D], F32, kind="ExternalInput").ap()
    wgt_d = nc.dram_tensor("wgt", [D, D], F32, kind="ExternalInput").ap()
    wo_d = nc.dram_tensor("wo", [D, D], F32, kind="ExternalInput").ap()
    bgb_d = nc.dram_tensor("bgb", [P, D], F32, kind="ExternalInput").ap()
    wgr_d = nc.dram_tensor("wgr", [D, E], F32, kind="ExternalInput").ap()
    wge_h = nc.dram_tensor("wge", [E * D, FF], F32, kind="ExternalInput")
    wdn_h = nc.dram_tensor("wdn", [E * FF, D], F32, kind="ExternalInput")
    tri_d = nc.dram_tensor("tri", [P, P], BF16, kind="ExternalInput").ap()
    csts_d = nc.dram_tensor("csts", [P, 16], F32, kind="ExternalInput").ap()
    onesf_d = nc.dram_tensor("onesf", [P, P], F32, kind="ExternalInput").ap()
    prefm_d = nc.dram_tensor("prefm", [E, 3], F32, kind="ExternalInput").ap()
    out_d = nc.dram_tensor("out", [T, D], F32, kind="ExternalOutput").ap()

    xin1_dram = nc.dram_tensor("xin1_dram", [T, D], BF16)
    attn_dram = nc.dram_tensor("attn_dram", [T, D], BF16)
    xin2_dram = nc.dram_tensor("xin2_dram", [T, D], BF16)
    xmid_dram = nc.dram_tensor("xmid_dram", [T, D], F32)
    disp_dram = nc.dram_tensor("disp_dram", [DISP_ROWS, D], BF16)
    eo_dram = nc.dram_tensor("eo_dram", [DISP_ROWS, D], BF16)
    kv_in = nc.dram_tensor("kv_in", [P, 8 * 65], F32)
    kv_out = nc.dram_tensor("kv_out", [P, 8 * 65], F32)
    cnt_in = nc.dram_tensor("cnt_in", [1, 16], F32)
    cnt_out = nc.dram_tensor("cnt_out", [E, 16], F32)

    with ExitStack() as st:
        Ez = st.enter_context
        tri = Ez(nc.sbuf_tensor("tri_s", [P, P], BF16))
        csts = Ez(nc.sbuf_tensor("csts_s", [P, 16], F32))
        onesf = Ez(nc.sbuf_tensor("onesf_s", [P, P], F32))
        prefm = Ez(nc.sbuf_tensor("prefm_s", [E, 3], F32))

        junk = Ez(nc.sbuf_tensor("junk", [P, 8], F32))
        neg1 = Ez(nc.sbuf_tensor("neg1", [P, 1], F32))
        zb = Ez(nc.sbuf_tensor("zb", [P, D], BF16))
        wgs = Ez(nc.sbuf_tensor("wgs", [P, KC, E], BF16))
        acta = Ez(nc.sbuf_tensor("acta", [P, KC, T], BF16))
        bst = Ez(nc.sbuf_tensor("bst", [P, 12], F32))
        mv = Ez(nc.sbuf_tensor("mv", [P, 2, 2], F32))
        lt1 = Ez(nc.sbuf_tensor("lt1", [P, 4], F32))
        lt2 = Ez(nc.sbuf_tensor("lt2", [P, 4], F32))
        lt3 = Ez(nc.sbuf_tensor("lt3", [P, 1], F32))
        sdv = Ez(nc.sbuf_tensor("sdv", [P, 2], F32))
        sds = Ez(nc.sbuf_tensor("sds", [P, 2], F32))
        rvv = Ez(nc.sbuf_tensor("rvv", [P, 2], F32))
        xt = Ez(nc.sbuf_tensor("xt", [P, 2, D], F32))
        sc1 = Ez(nc.sbuf_tensor("sc1", [P, D], F32))
        sc2 = Ez(nc.sbuf_tensor("sc2", [P, D], F32))
        sc3 = Ez(nc.sbuf_tensor("sc3", [P, D], F32))
        gsc = Ez(nc.sbuf_tensor("gsc", [P, D], F32))
        tsc = Ez(nc.sbuf_tensor("tsc", [P, 256], F32))
        zq = Ez(nc.sbuf_tensor("zq", [P, H], F32))
        zr = Ez(nc.sbuf_tensor("zr", [P, H], F32))
        lgf = Ez(nc.sbuf_tensor("lgf", [P, NT, E], F32))
        tmaxs = Ez(nc.sbuf_tensor("tmaxs", [P, NT, E], F32))
        tidxs = Ez(nc.sbuf_tensor("tidxs", [P, NT, E], U32))
        i1fa = Ez(nc.sbuf_tensor("i1fa", [P, NT], F32))
        i2fa = Ez(nc.sbuf_tensor("i2fa", [P, NT], F32))
        lda = Ez(nc.sbuf_tensor("lda", [P, NT], F32))
        g1a = Ez(nc.sbuf_tensor("g1a", [P, NT], F32))
        g2a = Ez(nc.sbuf_tensor("g2a", [P, NT], F32))
        m_all = Ez(nc.sbuf_tensor("m_all", [P, 2, NT, E], BF16))
        incl = Ez(nc.sbuf_tensor("incl", [P, 2 * NT * E], F32))
        offa = Ez(nc.sbuf_tensor("offa", [P, 2 * NT * E], F32))
        offb = Ez(nc.sbuf_tensor("offb", [P, 2 * NT * E], F32))
        rsc1 = Ez(nc.sbuf_tensor("rsc1", [P, 2 * NT * E], F32))
        s_all = Ez(nc.sbuf_tensor("s_all", [P, 2, NT], F32))
        kg = Ez(nc.sbuf_tensor("kg", [P, 2, NT], F32))
        ov1 = Ez(nc.sbuf_tensor("ov1", [P, 2, NT], F32))
        slotf = Ez(nc.sbuf_tensor("slotf", [P, 2, NT], F32))
        slosc = Ez(nc.sbuf_tensor("slosc", [P, 2, NT], F32))
        slotu = Ez(nc.sbuf_tensor("slotu", [P, 2, NT], U32))
        cntp = Ez(nc.sbuf_tensor("cntp", [P, 16], F32))
        cnts = Ez(nc.sbuf_tensor("cnts", [E, 16], F32))
        goffr = Ez(nc.sbuf_tensor("goffr", [1, 16], F32))
        w1a = Ez(nc.sbuf_tensor("w1a", [P, NT], F32))
        w2a = Ez(nc.sbuf_tensor("w2a", [P, NT], F32))

        iot8 = csts[:, 0:8]
        trash = csts[:, 8:9]

        def sep(v):
            v.tensor_scalar(junk[:, 0:1], junk[:, 0:1], 1.0, None, ALU.add)

        def mkdrain(eng, sem):
            state = {"n": 0}

            def D(inst):
                state["n"] += 1
                inst.then_inc(sem, 1)
                eng.wait_ge(sem, state["n"])
                return inst
            return D

        def LNSTATS(v, DR, src_ap, tpar, sqbuf, sqwait=None):
                        DR(v.tensor_reduce(lt3[:], src_ap, AX.X, ALU.add))
                        DR(v.tensor_scalar(mv[:, tpar, 0:1], lt3[:], 1.0 / D, None, ALU.mult))
                        if sqwait is not None:
                            v.wait_ge(*sqwait)
                        else:
                            DR(v.tensor_tensor(sqbuf, src_ap, src_ap, ALU.mult))
                        DR(v.tensor_reduce(lt3[:], sqbuf, AX.X, ALU.add))
                        DR(v.tensor_scalar(lt3[:], lt3[:], 1.0 / D, EPS, ALU.mult,
                                           ALU.add))
                        DR(v.tensor_tensor(lt2[:, 1:2], mv[:, tpar, 0:1],
                                           mv[:, tpar, 0:1], ALU.mult))
                        DR(v.tensor_tensor(mv[:, tpar, 1:2], lt3[:], lt2[:, 1:2],
                                           ALU.subtract))

        with ExitStack() as sA:
            EA = sA.enter_context
            wa = EA(nc.sbuf_tensor("wa", [P, 4, KC, D], BF16))
            ktl = EA(nc.sbuf_tensor("ktl", [P, 2, D], BF16))
            vxt = EA(nc.sbuf_tensor("vxt", [P, 2, D], BF16))
            onebf = EA(nc.sbuf_tensor("onebf", [P, 1], BF16))
            qfm = EA(nc.sbuf_tensor("qfm", [P, 2, KC, 512], BF16))
            gtl = EA(nc.sbuf_tensor("gtl", [P, 2, D], BF16))
            x1s = EA(nc.sbuf_tensor("x1s", [P, 2, D], BF16))
            ats = EA(nc.sbuf_tensor("ats", [P, 2, D], BF16))
            bgb = EA(nc.sbuf_tensor("bgb_s", [P, D], F32))
            kvpk = EA(nc.sbuf_tensor("kvpk", [P, 8, 65], F32))
            kvsum = EA(nc.sbuf_tensor("kvsum", [P, 8, 65], F32))
            kvx = EA(nc.sbuf_tensor("kvx", [P, 8, 65], BF16))

            with ExitStack() as s1:
                E1 = s1.enter_context
                ps_k = E1(nc.psum_tensor("ps_k", [P, 2, D], F32))
                ps_v = E1(nc.psum_tensor("ps_v", [P, D], F32))
                ps_kvA = E1(nc.psum_tensor("ps_kvA", [P, 4, 128], F32))
                ps_kvB = E1(nc.psum_tensor("ps_kvB", [P, 4, 128], F32))
                xsem0 = E1(nc.semaphore("b1_x0"))
                xsem1 = E1(nc.semaphore("b1_x1p"))
                xsem = [xsem0, xsem1]
                csem = E1(nc.semaphore("b1_c"))
                wkvs = E1(nc.semaphore("b1_wkv"))
                wsem = E1(nc.semaphore("b1_w"))
                x1sem = E1(nc.semaphore("b1_x1"))
                stp0 = E1(nc.semaphore("b1_st0"))
                stp1 = E1(nc.semaphore("b1_st1"))
                stp = [stp0, stp1]
                tsem = E1(nc.semaphore("b1_t"))
                v2s = E1(nc.semaphore("b1_v2s"))
                s2v = E1(nc.semaphore("b1_s2v"))
                ksem = E1(nc.semaphore("b1_k"))
                kvrdy = E1(nc.semaphore("b1_kvr"))
                vesem = E1(nc.semaphore("b1_ve"))
                kelsem = E1(nc.semaphore("b1_kel"))
                kvmmd = E1(nc.semaphore("b1_kvm"))
                kvadd = E1(nc.semaphore("b1_kva"))
                kvev = E1(nc.semaphore("b1_kve"))
                gsem = E1(nc.semaphore("b1_g"))
                vdr = E1(nc.semaphore("b1_vdr"))
                zdsem = E1(nc.semaphore("b1_zd"))
                kvin_s = E1(nc.semaphore("b1_kvi"))
                sqd1 = E1(nc.semaphore("b1_sqd"))
                LAG = 3
                blk = E1(nc.Block())

                @blk.sync
                def _(sy):
                    sy.dma_start(out=tri[:], in_=tri_d).then_inc(csem, 16)
                    sy.dma_start(out=csts[:], in_=csts_d).then_inc(csem, 16)
                    sy.dma_start(out=onesf[:], in_=onesf_d).then_inc(csem, 16)
                    sy.dma_start(out=prefm[:], in_=prefm_d).then_inc(csem, 16)
                    sy.dma_start(out=bgb[:], in_=bgb_d).then_inc(csem, 16)
                    def TD(tl):
                        sy.wait_ge(stp[tl % 2], 16 * (tl // 2 + 1))
                        if tl >= 1:
                            sy.wait_ge(tsem, 16 * KC * tl)
                        for kc in range(KC):
                            sy.dma_start(out=acta[:, kc, tl * P:(tl + 1) * P],
                                         in_=xin1_dram[tl * P:(tl + 1) * P,
                                                       kc * P:(kc + 1) * P],
                                         transpose=True).then_inc(tsem, 16)

                    for t in range(NT):
                        if t >= 2:
                            sy.wait_ge(x1sem, t - 1)
                        sy.dma_start(out=xt[:, t % 2, :],
                                     in_=x_d[t * P:(t + 1) * P, :]).then_inc(xsem[t % 2], 16)
                        if t >= 1:
                            sy.wait_ge(x1sem, t)
                            sy.dma_start(out=xin1_dram[(t - 1) * P:t * P, :],
                                         in_=x1s[:, (t - 1) % 2, :]).then_inc(stp[(t - 1) % 2], 16)
                        if t >= 2:
                            TD(t - 2)
                    sy.wait_ge(x1sem, NT)
                    sy.dma_start(out=xin1_dram[(NT - 1) * P:NT * P, :],
                                 in_=x1s[:, (NT - 1) % 2, :]).then_inc(stp[(NT - 1) % 2], 16)
                    TD(NT - 2)
                    TD(NT - 1)
                    sy.wait_ge(gsem, 1)
                    for i in range(DISP_ROWS // P):
                        sy.dma_start(out=disp_dram[i * P:(i + 1) * P, :],
                                     in_=zb[:]).then_inc(zdsem, 16)

                @blk.gpsimd
                def _(g):
                    g.memset(zb[:], 0.0)
                    g.sem_inc(gsem, 1)
                    for i, wsrc in enumerate([wk_d, wv_d, wq_d, wgt_d]):
                        for kc in range(KC):
                            g.dma_start(out=wa[:, i, kc, :],
                                        in_=wsrc[kc * P:(kc + 1) * P, :]
                                        ).then_inc(wkvs if i < 2 else wsem, 16)
                    for kc in range(KC):
                        g.dma_start(out=wgs[:, kc, :],
                                    in_=wgr_d[kc * P:(kc + 1) * P, :]).then_inc(wsem, 16)
                    g.wait_ge(kvev, 1)
                    g.dma_start(out=kv_in.ap(), in_=kvpk[:, :, :]).then_inc(kvin_s, 16)
                    g.wait_ge(kvin_s, 16)

                @blk.vector
                def _(v):
                    DR = mkdrain(v, vdr)
                    DR(v.memset(junk[:], 0.0))
                    DR(v.memset(neg1[:], -1.0))
                    DR(v.memset(onebf[:], 1.0))
                    DR(v.memset(kvpk[:, :, :], 0.0))
                    v.wait_ge(csem, 16 * 5)
                    vskip = stage == 0 and (sub < 1 or sub == 10)
                    kvskip = stage == 0 and (sub < 2 or sub >= 10)

                    def EPI(u):
                        v.wait_ge(ksem, u + 1)
                        v.wait_ge(kelsem, u + 1)
                        v.tensor_tensor(ktl[:, u % 2, :], sc2[:], sc1[:], ALU.add)
                        v.tensor_copy(vxt[:, u % 2, :],
                                      ps_v[:, :]).then_inc(vesem, 1)
                        v.sem_inc(kvrdy, 1)
                        if u >= 1 and not kvskip:
                            v.wait_ge(kvmmd, u)
                            DR(v.tensor_tensor(kvpk[:, 0:4, :], kvpk[:, 0:4, :],
                                               ps_kvA[:, :, 0:65], ALU.add))
                            v.tensor_tensor(kvpk[:, 4:8, :], kvpk[:, 4:8, :],
                                            ps_kvB[:, :, 0:65], ALU.add).then_inc(kvadd, 1)
                            v.wait_ge(kvadd, u)

                    for t in range(NT):
                        v.wait_ge(xsem[t % 2], 16 * (t // 2 + 1))
                        LNSTATS(v, DR, xt[:, t % 2, :], t % 2, gsc[:],
                                sqwait=(sqd1, t + 1))
                        v.tensor_copy(sdv[:, t % 2:t % 2 + 1],
                                      mv[:, t % 2, 1:2]).then_inc(v2s, 1)
                        v.wait_ge(s2v, t + 1)
                        DR(v.reciprocal(rvv[:, t % 2:t % 2 + 1], sds[:, t % 2:t % 2 + 1]))
                        if t >= 2:
                            v.wait_ge(stp[t % 2], 16 * (t // 2))
                        v.tensor_scalar(x1s[:, t % 2, :], xt[:, t % 2, :],
                                        mv[:, t % 2, 0:1], rvv[:, t % 2:t % 2 + 1],
                                        ALU.subtract, ALU.mult).then_inc(x1sem, 1)
                        if t >= LAG and not vskip:
                            EPI(t - LAG)
                    if not vskip:
                        for u in range(NT - LAG, NT):
                            EPI(u)
                    if stage == 0 and (sub < 2 or sub >= 10):
                        v.sem_inc(kvev, 1)
                        return
                    v.wait_ge(kvmmd, NT)
                    DR(v.tensor_tensor(kvpk[:, 0:4, :], kvpk[:, 0:4, :],
                                       ps_kvA[:, :, 0:65], ALU.add))
                    v.tensor_tensor(kvpk[:, 4:8, :], kvpk[:, 4:8, :],
                                    ps_kvB[:, :, 0:65], ALU.add).then_inc(kvev, 1)
                    v.wait_ge(kvev, 1)

                @blk.scalar
                def _(s):
                    kskip = stage == 0 and (sub < 1 or sub == 10 or sub >= 12)

                    def KELU(u):
                        s.wait_ge(ksem, u + 1)
                        if u >= 1:
                            s.wait_ge(kvrdy, u)
                        s.activation(sc2[:], ps_k[:, u % 2, :], AF.Relu)
                        s.activation(sc3[:], ps_k[:, u % 2, :], AF.Relu,
                                     scale=neg1[:, 0:1])
                        s.activation(sc1[:], sc3[:], AF.Exp,
                                     scale=neg1[:, 0:1]).then_inc(kelsem, 1)

                    for t in range(NT):
                        s.wait_ge(xsem[t % 2], 16 * (t // 2 + 1))
                        if t >= 1:
                            s.wait_ge(v2s, t)
                        s.activation(gsc[:], xt[:, t % 2, :],
                                     AF.Square).then_inc(sqd1, 1)
                        s.wait_ge(v2s, t + 1)
                        s.activation(sds[:, t % 2:t % 2 + 1], sdv[:, t % 2:t % 2 + 1],
                                     AF.Sqrt).then_inc(s2v, 1)
                        if t >= LAG and not kskip:
                            KELU(t - LAG)
                    if not kskip:
                        for u in range(NT - LAG, NT):
                            KELU(u)

                @blk.tensor
                def _(tn):
                    if stage == 0 and sub < 1:
                        return
                    for t in range(3):
                        tn.wait_ge(x1sem, t + 1)
                        for _ in range(4):
                            tn.matmul(ps_k[:, 0, 0:128], tri[:, :], tri[:, :],
                                      start=True, stop=True)
                    tn.wait_ge(wkvs, 16 * 16)
                    for t in range(NT):
                        tn.wait_ge(tsem, 16 * KC * (t + 1))
                        if t >= 2 and sub != 10:
                            tn.wait_ge(kvrdy, t - 1)
                        if t >= 1 and sub != 10:
                            tn.wait_ge(vesem, t)
                        for kc in range(KC):
                            st_, sp_ = kc == 0, kc == KC - 1
                            lhs = acta[:, kc, t * P:(t + 1) * P]
                            tn.matmul(ps_k[:, t % 2, 0:512], lhs, wa[:, 0, kc, 0:512],
                                      start=st_, stop=sp_)
                            tn.matmul(ps_k[:, t % 2, 512:1024], lhs, wa[:, 0, kc, 512:1024],
                                      start=st_, stop=sp_)
                            tn.matmul(ps_v[:, 0:512], lhs, wa[:, 1, kc, 0:512],
                                      start=st_, stop=sp_)
                            mm = tn.matmul(ps_v[:, 512:1024], lhs, wa[:, 1, kc, 512:1024],
                                           start=st_, stop=sp_)
                        mm.then_inc(ksem, 1)
                        if t >= 1 and not (stage == 0 and (sub < 2 or sub >= 10)):
                            tn.wait_ge(kvrdy, t)
                            if t >= 2:
                                tn.wait_ge(kvadd, t - 1)
                            for h in range(H):
                                pst = ps_kvA if (h // 2) < 4 else ps_kvB
                                tp = (0, 64) if (h % 2) else None
                                po = (h % 2) * 64
                                blk_i = (h // 2) % 4
                                tn.matmul(pst[po:po + 64, blk_i, 0:64],
                                          ktl[:, (t - 1) % 2, h * 64:(h + 1) * 64],
                                          vxt[:, (t - 1) % 2, h * 64:(h + 1) * 64],
                                          start=True, stop=True, tile_position=tp)
                                mm = tn.matmul(pst[po:po + 64, blk_i, 64:65],
                                               ktl[:, (t - 1) % 2, h * 64:(h + 1) * 64],
                                               onebf[:, :],
                                               start=True, stop=True, tile_position=tp)
                            mm.then_inc(kvmmd, 1)
                    if stage == 0 and (sub < 2 or sub >= 10):
                        return
                    tn.wait_ge(kvrdy, NT)
                    tn.wait_ge(kvadd, NT - 1)
                    for h in range(H):
                        pst = ps_kvA if (h // 2) < 4 else ps_kvB
                        tp = (0, 64) if (h % 2) else None
                        po = (h % 2) * 64
                        blk_i = (h // 2) % 4
                        tn.matmul(pst[po:po + 64, blk_i, 0:64],
                                  ktl[:, (NT - 1) % 2, h * 64:(h + 1) * 64],
                                  vxt[:, (NT - 1) % 2, h * 64:(h + 1) * 64],
                                  start=True, stop=True, tile_position=tp)
                        mm = tn.matmul(pst[po:po + 64, blk_i, 64:65],
                                       ktl[:, (NT - 1) % 2, h * 64:(h + 1) * 64],
                                       onebf[:, :],
                                       start=True, stop=True, tile_position=tp)
                    mm.then_inc(kvmmd, 1)

            if stage == 5:
                pass
            if stage == 0:
                with ExitStack() as s0:
                    E0 = s0.enter_context
                    x1f = E0(nc.sbuf_tensor("x1f", [P, D], F32))
                    ktf = E0(nc.sbuf_tensor("ktf", [P, D], F32))
                    d0 = E0(nc.semaphore("s0_d"))
                    g0 = E0(nc.semaphore("s0_g"))
                    blk0 = E0(nc.Block())

                    @blk0.gpsimd
                    def _(g):
                        g.dma_start(out=ktf[:], in_=xin1_dram[0:P, :]).then_inc(g0, 16)

                    @blk0.vector
                    def _(v):
                        v.tensor_copy(ktf[:], ktl[:, (NT - 1) % 2, :])
                        v.tensor_copy(x1f[:, 0:12], bst[:])
                        v.tensor_copy(x1f[:, 16:20], bass.AP(mv, 0, [[4, P], [1, 4]]))
                        v.tensor_copy(x1f[:, 20:22], sdv[:])
                        v.tensor_copy(x1f[:, 22:24], rvv[:]).then_inc(g0, 1)

                    @blk0.sync
                    def _(sy):
                        sy.wait_ge(g0, 17)
                        sy.dma_start(out=out_d[P:2 * P, :], in_=x1f[:]).then_inc(d0, 16)
                        sy.dma_start(out=out_d[2 * P:3 * P, :], in_=ktf[:]).then_inc(d0, 16)
                        sy.wait_ge(d0, 32)
                return nc

            with ExitStack() as s2:
                E2 = s2.enter_context
                ps_q = E2(nc.psum_tensor("ps_q", [P, 2, 512], F32))
                ps_g = E2(nc.psum_tensor("ps_g", [P, 2, D], F32))
                ps_oA = E2(nc.psum_tensor("ps_oA", [P, 4, 128], F32))
                ps_oB = E2(nc.psum_tensor("ps_oB", [P, 4, 128], F32))
                osb = E2(nc.sbuf_tensor("osb", [P, 8, 128], F32))
                ccsem = E2(nc.semaphore("b2_cc"))
                kvls = E2(nc.semaphore("b2_kvl"))
                kvxs = E2(nc.semaphore("b2_kvx"))
                qp = E2(nc.semaphore("b2_qp"))
                qe = E2(nc.semaphore("b2_qe"))
                gp = E2(nc.semaphore("b2_gp"))
                ge = E2(nc.semaphore("b2_ge"))
                oms = E2(nc.semaphore("b2_om"))
                gat = E2(nc.semaphore("b2_gat"))
                asem0 = E2(nc.semaphore("b2_as0"))
                asem1 = E2(nc.semaphore("b2_as1"))
                asem = [asem0, asem1]
                v2s2 = E2(nc.semaphore("b2_v2s"))
                s2v2 = E2(nc.semaphore("b2_s2v"))
                g2s2 = E2(nc.semaphore("b2_g2s"))
                vdr2 = E2(nc.semaphore("b2_vdr"))
                blk2 = E2(nc.Block())

                @blk2.gpsimd
                def _(g):
                    g.collective_compute(
                        "AllReduce", ALU.add,
                        replica_groups=[[0, 1], [2, 3], [4, 5], [6, 7]],
                        ins=[kv_in.ap().opt()], outs=[kv_out.ap().opt()],
                    ).then_inc(ccsem, 1)
                    g.wait_ge(ccsem, 1)
                    g.dma_start(out=kvsum[:, :, :], in_=kv_out.ap()).then_inc(kvls, 16)

                @blk2.vector
                def _(v):
                    DR = mkdrain(v, vdr2)
                    if stage == 5 and sub == 20:
                        v.wait_ge(kvls, 16)
                        v.tensor_copy(kvx[:, :, :], kvsum[:, :, :]).then_inc(kvxs, 1)
                        return
                    for sl in range(4):
                        if sl == 1:
                            pass
                        for m in range(KC):
                            v.wait_ge(s2v2, sl * 8 + m + 1)
                            v.tensor_tensor(qfm[:, sl % 2, m, :], sc2[:, 0:512],
                                            sc1[:, 0:512], ALU.add).then_inc(qe, 1)
                        if sl == 0:
                            v.wait_ge(kvls, 16)
                            v.tensor_copy(kvx[:, :, :], kvsum[:, :, :]).then_inc(kvxs, 1)
                        if stage == 5 and sub == 21:
                            continue
                        for tt in range(4):
                            t = sl * 4 + tt
                            v.wait_ge(gp, t + 1)
                            if t >= 2 and not (stage == 5 and sub == 22):
                                v.wait_ge(asem[t % 2], 16 * (t // 2))
                            v.tensor_tensor(gsc[:], ps_g[:, t % 2, :], bgb[:],
                                            ALU.add).then_inc(v2s2, 1)
                            if stage == 5 and sub == 22:
                                continue
                            for part in range(2):
                                v.wait_ge(oms, 2 * t + part + 1)
                                if part == 0:
                                    v.wait_ge(g2s2, t + 1)
                                DR(v.tensor_copy(osb[:, 0:4, 0:65], ps_oA[:, :, 0:65]))
                                v.tensor_copy(osb[:, 4:8, 0:65],
                                              ps_oB[:, :, 0:65]).then_inc(gat, 1)
                                v.wait_ge(gat, 4 * t + 2 * part + 1)
                                DR(v.tensor_scalar(zq[:, part * 8:part * 8 + 8],
                                                   bass.AP(osb, 64, [[1024, P], [128, 8]]),
                                                   1e-6, None, ALU.max))
                                DR(v.reciprocal(zr[:, part * 8:part * 8 + 8],
                                                zq[:, part * 8:part * 8 + 8]))
                                DR(v.tensor_scalar(zr[:, part * 8:part * 8 + 8],
                                                   zr[:, part * 8:part * 8 + 8],
                                                   SCALE, None, ALU.mult))
                                for par in range(2):
                                    base = (t % 2) * D + part * 512 + par * 64
                                    DR(v.tensor_tensor(
                                        tsc[:, 0:256],
                                        bass.AP(osb, par * 4 * 128, [[1024, P], [128, 4], [1, 64]]),
                                        bass.AP(zr, part * 8 + par * 4,
                                                [[16, P], [1, 4], [0, 64]]),
                                        ALU.mult))
                                    mmv = v.tensor_tensor(
                                        bass.AP(ats, base, [[2 * D, P], [128, 4], [1, 64]]),
                                        tsc[:, 0:256],
                                        bass.AP(gtl, base, [[2 * D, P], [128, 4], [1, 64]]),
                                        ALU.mult)
                                mmv.then_inc(gat, 1)

                @blk2.scalar
                def _(s):
                    if stage == 5 and sub == 20:
                        return
                    for sl in range(4):
                        for m in range(KC):
                            i = sl * 8 + m
                            s.wait_ge(qp, i + 1)
                            if i >= 1:
                                s.wait_ge(qe, i)
                            s.activation(sc2[:, 0:512], ps_q[:, m % 2, :], AF.Relu)
                            s.activation(sc3[:, 0:512], ps_q[:, m % 2, :], AF.Relu,
                                         scale=neg1[:, 0:1])
                            s.activation(sc1[:, 0:512], sc3[:, 0:512], AF.Exp,
                                         scale=neg1[:, 0:1]).then_inc(s2v2, 1)
                        for tt in range(4):
                            if stage == 5 and sub == 21:
                                break
                            t = sl * 4 + tt
                            s.wait_ge(v2s2, t + 1)
                            s.activation(gtl[:, t % 2, :], gsc[:],
                                         AF.Sigmoid).then_inc(g2s2, 1)

                @blk2.tensor
                def _(tn):
                    if stage == 5 and sub == 20:
                        return
                    for sl in range(4):
                        for m in range(KC):
                            if sl * 8 + m >= 2:
                                tn.wait_ge(qe, sl * 8 + m - 1)
                            for kc in range(KC):
                                st_, sp_ = kc == 0, kc == KC - 1
                                mm = tn.matmul(ps_q[:, m % 2, :],
                                               wa[:, 2, kc, m * P:(m + 1) * P],
                                               acta[:, kc, sl * 512:(sl + 1) * 512],
                                               start=st_, stop=sp_)
                            mm.then_inc(qp, 1)
                        if stage == 5 and sub == 21:
                            continue
                        for tt in range(4):
                            t = sl * 4 + tt
                            if t >= 2:
                                tn.wait_ge(g2s2, t - 1)
                            for kc in range(KC):
                                st_, sp_ = kc == 0, kc == KC - 1
                                lhs = acta[:, kc, t * P:(t + 1) * P]
                                tn.matmul(ps_g[:, t % 2, 0:512], lhs,
                                          wa[:, 3, kc, 0:512], start=st_, stop=sp_)
                                mm = tn.matmul(ps_g[:, t % 2, 512:1024], lhs,
                                               wa[:, 3, kc, 512:1024], start=st_, stop=sp_)
                            mm.then_inc(gp, 1)
                            if stage == 5 and sub == 22:
                                continue
                            if t == 0:
                                tn.wait_ge(kvxs, 1)
                            tn.wait_ge(qe, sl * 8 + 8)
                            for part in range(2):
                                if t > 0 or part > 0:
                                    tn.wait_ge(gat, 4 * t + 2 * part - 1)
                                for par in range(2):
                                    pst = ps_oA if par == 0 else ps_oB
                                    for j in range(4):
                                        h = part * 8 + 2 * j + par
                                        mm = tn.matmul(pst[:, j, 0:65],
                                                       qfm[par * 64:par * 64 + 64,
                                                           sl % 2, h // 2, tt * P:(tt + 1) * P],
                                                       kvx[par * 64:par * 64 + 64, h // 2, :],
                                                       start=True, stop=True)
                                mm.then_inc(oms, 1)

                @blk2.sync
                def _(sy):
                    if stage == 5 and sub in (20, 21, 22):
                        return
                    for t in range(NT):
                        sy.wait_ge(gat, 4 * t + 4)
                        sy.dma_start(out=attn_dram[t * P:(t + 1) * P, :],
                                     in_=ats[:, t % 2, :]).then_inc(asem[t % 2], 16)
                    sy.wait_ge(asem[0], 16 * (NT // 2))
                    sy.wait_ge(asem[1], 16 * (NT // 2))

        if stage == 5:
            with ExitStack() as s5:
                E5 = s5.enter_context
                d5 = E5(nc.semaphore("s5_d"))
                blk5d = E5(nc.Block())

                @blk5d.sync
                def _(sy):
                    sy.dma_start(out=out_d[0:P, 0:520],
                                 in_=bass.AP(kvsum, 0, [[520, P], [1, 520]])).then_inc(d5, 16)
                    sy.wait_ge(d5, 16)
            return nc

        sPre = ExitStack()
        wga = sPre.enter_context(nc.sbuf_tensor("wga", [P, 2, KC, D], BF16))
        wgq = [sPre.enter_context(nc.semaphore(f"wgq{i}")) for i in range(2)]

        def wge_quarter_ap(e, q):
            return bass.AP(wge_h, e * D * FF + q * D,
                           [[FF, P], [P * FF, KC], [1, D]])

        with ExitStack() as sB:
            EB = sB.enter_context
            wo_sb = EB(nc.sbuf_tensor("wo_sb", [P, KC, D], BF16))
            xin2 = EB(nc.sbuf_tensor("xin2", [P, NT, D], BF16))
            x2fm = acta
            xms = EB(nc.sbuf_tensor("xms", [P, 2, D], F32))
            ps_wo = EB(nc.psum_tensor("ps_wo", [P, 2, D], F32))
            ps_lg = EB(nc.psum_tensor("ps_lg", [P, 2, 512], F32))
            ps_pre = EB(nc.psum_tensor("ps_pre", [P, 256], F32))
            ps_off = EB(nc.psum_tensor("ps_off", [P, 256], F32))
            t3sem = EB(nc.semaphore("b3_t"))
            wosem = EB(nc.semaphore("b3_wo"))
            xr0 = EB(nc.semaphore("b3_xr0"))
            xr1 = EB(nc.semaphore("b3_xr1"))
            xrsem = [xr0, xr1]
            wops = EB(nc.semaphore("b3_wop"))
            xmr = EB(nc.semaphore("b3_xmr"))
            xms0 = EB(nc.semaphore("b3_xms0"))
            xms1 = EB(nc.semaphore("b3_xms1"))
            xmst = [xms0, xms1]
            v2s3 = EB(nc.semaphore("b3_v2s"))
            s2v3 = EB(nc.semaphore("b3_s2v"))
            x2r = EB(nc.semaphore("b3_x2r"))
            x2st = EB(nc.semaphore("b3_x2s"))
            t2sem = EB(nc.semaphore("b3_t2"))
            lgp = EB(nc.semaphore("b3_lgp"))
            lgc = EB(nc.semaphore("b3_lgc"))
            mdone = EB(nc.semaphore("b3_md"))
            prep = EB(nc.semaphore("b3_pre"))
            cpd = EB(nc.semaphore("b3_cpd"))
            offr = EB(nc.semaphore("b3_offr"))
            offbs = EB(nc.semaphore("b3_offb"))
            sltr = EB(nc.semaphore("b3_slt"))
            cntr = EB(nc.semaphore("b3_cnt"))
            scat = EB(nc.semaphore("b3_sca"))
            cis = EB(nc.semaphore("b3_cis"))
            ostg1 = EB(nc.semaphore("b3_o1"))
            vdr3 = EB(nc.semaphore("b3_vdr"))
            sqd3 = EB(nc.semaphore("b3_sqd"))
            blk3 = EB(nc.Block())

            @blk3.sync
            def _(sy):
                for kc in range(KC):
                    sy.dma_start(out=acta[:, kc, :],
                                 in_=attn_dram[0:T, kc * P:(kc + 1) * P],
                                 transpose=True).then_inc(t3sem, 16)
                for t in range(NT):
                    sy.wait_ge(xmr, t + 1)
                    sy.dma_start(out=xmid_dram[t * P:(t + 1) * P, :],
                                 in_=xms[:, t % 2, :]).then_inc(xmst[t % 2], 16)
                    if stage == 1:
                        sy.dma_start(out=out_d[t * P:(t + 1) * P, :],
                                     in_=xms[:, t % 2, :]).then_inc(xmst[t % 2], 16)
                if stage == 1:
                    sy.wait_ge(xmst[0], 32 * (NT // 2))
                    sy.wait_ge(xmst[1], 32 * (NT // 2))
                    return
                for t in range(NT):
                    sy.wait_ge(x2r, t + 1)
                    sy.dma_start(out=xin2_dram[t * P:(t + 1) * P, :],
                                 in_=xin2[:, t, :]).then_inc(x2st, 16)
                sy.wait_ge(x2st, 16 * NT)
                for kc in range(KC):
                    sy.dma_start(out=x2fm[:, kc, :],
                                 in_=xin2_dram[0:T, kc * P:(kc + 1) * P],
                                 transpose=True).then_inc(t2sem, 16)

            @blk3.gpsimd
            def _(g):
                for kc in range(KC):
                    g.dma_start(out=wo_sb[:, kc, :],
                                in_=wo_d[kc * P:(kc + 1) * P, :]).then_inc(wosem, 16)
                for t in range(NT):
                    if t >= 2:
                        g.wait_ge(xmr, t - 1)
                    g.dma_start(out=xt[:, t % 2, :],
                                in_=x_d[t * P:(t + 1) * P, :]).then_inc(xrsem[t % 2], 16)
                if stage == 1:
                    return
                g.dma_start(out=eo_dram[SLOTS:DISP_ROWS, :], in_=zb[:]).then_inc(cis, 16)
                for i in range(2):
                    g.dma_start(out=wga[:, i, :, :],
                                in_=wge_quarter_ap(0, i)).then_inc(wgq[i], 16)
                for j in range(2):
                    g.wait_ge(sltr, j + 1)
                    for t in range(NT):
                        g.indirect_dma_start(
                            out=disp_dram.ap(),
                            out_offset=IOA(slotu[:, j, t:t + 1], 0),
                            in_=xin2[:, t, :], in_offset=None,
                        ).then_inc(scat, 16)
                g.wait_ge(cntr, 1)
                g.dma_start(out=cnt_in.ap(), in_=cntp[0:1, :]).then_inc(cis, 16)
                g.wait_ge(cis, 32)
                g.wait_ge(scat, 16 * 32)

            @blk3.vector
            def _(v):
                DR = mkdrain(v, vdr3)
                xmul = 32 if stage == 1 else 16
                for t in range(NT):
                    v.wait_ge(wops, t + 1)
                    v.wait_ge(xrsem[t % 2], 16 * (t // 2 + 1))
                    if t >= 2:
                        v.wait_ge(xmst[t % 2], xmul * (t // 2))
                    v.tensor_tensor(xms[:, t % 2, :], xt[:, t % 2, :],
                                    ps_wo[:, t % 2, :], ALU.add).then_inc(xmr, 1)
                    v.wait_ge(xmr, t + 1)
                    if stage == 1:
                        continue
                    LNSTATS(v, DR, xms[:, t % 2, :], t % 2, sc1[:],
                            sqwait=(sqd3, t + 1))
                    v.tensor_copy(sdv[:, t % 2:t % 2 + 1],
                                  mv[:, t % 2, 1:2]).then_inc(v2s3, 1)
                    v.wait_ge(s2v3, t + 1)
                    DR(v.reciprocal(rvv[:, t % 2:t % 2 + 1], sds[:, t % 2:t % 2 + 1]))
                    v.tensor_scalar(xin2[:, t, :], xms[:, t % 2, :],
                                    mv[:, t % 2, 0:1], rvv[:, t % 2:t % 2 + 1],
                                    ALU.subtract, ALU.mult).then_inc(x2r, 1)
                if stage == 1:
                    return
                for t in range(NT):
                    v.wait_ge(lgp, t + 1)
                    v.tensor_copy(lgf[:, t, :],
                                  ps_lg[:, t % 2, 0:E]).then_inc(lgc, 1)
                    if t == NT - 1:
                        v.wait_ge(lgc, NT)
                for t in range(NT):
                    DR(v.max(tmaxs[:, t, :], lgf[:, t, :]))
                for t in range(NT):
                    DR(v.max_index(tidxs[:, t, :], tmaxs[:, t, :], lgf[:, t, :]))
                DR(v.tensor_scalar(i1fa[:], bass.AP(tidxs, 0, [[NT * E, P], [E, NT]]),
                                   0.0, None, ALU.add))
                DR(v.tensor_scalar(i2fa[:], bass.AP(tidxs, 1, [[NT * E, P], [E, NT]]),
                                   0.0, None, ALU.add))
                v.tensor_tensor(lda[:], bass.AP(tmaxs, 0, [[NT * E, P], [E, NT]]),
                                bass.AP(tmaxs, 1, [[NT * E, P], [E, NT]]),
                                ALU.subtract).then_inc(v2s3, 1)
                v.wait_ge(s2v3, NT + 1)
                v.tensor_scalar(g2a[:], g1a[:], -1.0, 1.0, ALU.mult, ALU.add)
                for t in range(NT):
                    v.tensor_scalar(m_all[:, 0, t, :], iot8, i1fa[:, t:t + 1],
                                    None, ALU.is_equal)
                    mi = v.tensor_scalar(m_all[:, 1, t, :], iot8, i2fa[:, t:t + 1],
                                         None, ALU.is_equal)
                    if t == NT - 1:
                        DR(mi)
                v.sem_inc(mdone, 1)
                v.wait_ge(prep, 1)
                v.tensor_copy(incl[:], ps_pre[:]).then_inc(cpd, 1)
                v.wait_ge(prep, 2)
                DR(v.tensor_copy(offa[0:1, :], ps_pre[0:1, :]))
                cur, nxt = offa, offb
                for sh in [8, 16, 32, 64]:
                    for j in range(2):
                        b = j * 128
                        v.tensor_copy(nxt[0:1, b:b + sh], cur[0:1, b:b + sh])
                        DR(v.tensor_tensor(nxt[0:1, b + sh:b + 128],
                                           cur[0:1, b + sh:b + 128],
                                           cur[0:1, b:b + 128 - sh], ALU.add))
                    cur, nxt = nxt, cur
                assert cur is offa
                v.tensor_copy(cntp[0:1, 0:8], cur[0:1, 120:128])
                v.tensor_copy(cntp[0:1, 8:16], cur[0:1, 248:256]).then_inc(cntr, 1)
                v.wait_ge(cntr, 1)
                DR(v.tensor_tensor(cur[0:1, :], cur[0:1, :], ps_pre[0:1, 0:256],
                                   ALU.subtract))
                v.tensor_tensor(bass.AP(cur, 128, [[2 * NT * E, 1], [8, NT], [1, 8]]),
                                bass.AP(cur, 128, [[2 * NT * E, 1], [8, NT], [1, 8]]),
                                bass.AP(cntp, 0, [[16, 1], [0, NT], [1, 8]]),
                                ALU.add).then_inc(offr, 1)
                v.wait_ge(offr, 1)
                v.wait_ge(offbs, 1)
                mflat = bass.AP(m_all, 0, [[256, P], [1, 256]])
                DR(v.tensor_tensor(rsc1[:], incl[:], mflat, ALU.subtract))
                DR(v.tensor_tensor(rsc1[:], rsc1[:], ps_off[:], ALU.add))
                DR(v.tensor_tensor(rsc1[:], rsc1[:], mflat, ALU.mult))
                DR(v.tensor_reduce(bass.AP(s_all, 0, [[2 * NT, P], [1, 2 * NT]]),
                                   bass.AP(rsc1, 0, [[256, P], [8, 2 * NT], [1, 8]]),
                                   AX.X, ALU.add))
                for j in range(2):
                    ifa = i1fa if j == 0 else i2fa
                    DR(v.tensor_scalar(slotf[:, j, :], ifa[:], float(CL), None, ALU.mult))
                    DR(v.tensor_tensor(slotf[:, j, :], slotf[:, j, :], s_all[:, j, :], ALU.add))
                    DR(v.tensor_scalar(ov1[:, j, :], s_all[:, j, :], CL - 0.5, None, ALU.is_gt))
                    DR(v.tensor_scalar(slosc[:, j, :], ov1[:, j, :], -1.0, 1.0,
                                       ALU.mult, ALU.add))
                    DR(v.tensor_tensor(slotf[:, j, :], slotf[:, j, :], slosc[:, j, :], ALU.mult))
                    DR(v.scalar_tensor_tensor(slotf[:, j, :], ov1[:, j, :], trash,
                                              slotf[:, j, :], ALU.mult, ALU.add))
                    DR(v.tensor_scalar(slotu[:, j, :], slotf[:, j, :], 0.0, None, ALU.add))
                    v.sem_inc(sltr, 1)

            @blk3.scalar
            def _(s):
                for t in range(NT):
                    if stage == 1:
                        break
                    s.wait_ge(xmr, t + 1)
                    if t >= 1:
                        s.wait_ge(v2s3, t)
                    s.activation(sc1[:], xms[:, t % 2, :],
                                 AF.Square).then_inc(sqd3, 1)
                    s.wait_ge(v2s3, t + 1)
                    s.activation(sds[:, t % 2:t % 2 + 1], sdv[:, t % 2:t % 2 + 1],
                                 AF.Sqrt).then_inc(s2v3, 1)
                if stage >= 2:
                    s.wait_ge(v2s3, NT + 1)
                    s.activation(g1a[:], lda[:], AF.Sigmoid).then_inc(s2v3, 1)

            @blk3.tensor
            def _(tn):
                for i in range(KC):
                    tn.wait_ge(t3sem, 16 * (i + 1))
                    for _ in range(3):
                        tn.matmul(ps_pre[0:128, 0:128], tri[:, :], tri[:, :],
                                  start=True, stop=True)
                tn.wait_ge(wosem, 16 * KC)
                for t in range(NT):
                    if t >= 2:
                        tn.wait_ge(xmr, t - 1)
                    for kc in range(KC):
                        st_, sp_ = kc == 0, kc == KC - 1
                        lhs = acta[:, kc, t * P:(t + 1) * P]
                        tn.matmul(ps_wo[:, t % 2, 0:512], lhs, wo_sb[:, kc, 0:512],
                                  start=st_, stop=sp_)
                        mm = tn.matmul(ps_wo[:, t % 2, 512:1024], lhs, wo_sb[:, kc, 512:1024],
                                       start=st_, stop=sp_)
                    mm.then_inc(wops, 1)
                if stage == 1:
                    return
                for t in range(NT):
                    tn.wait_ge(x2st, 16 * (t + 1))
                    for _ in range(3):
                        tn.matmul(ps_pre[0:128, 0:128], tri[:, :], tri[:, :],
                                  start=True, stop=True)
                tn.wait_ge(t2sem, 16 * KC)
                for t in range(NT):
                    if t >= 2:
                        tn.wait_ge(lgc, t - 1)
                    for kc in range(KC):
                        mm = tn.matmul(ps_lg[:, t % 2, 0:E], x2fm[:, kc, t * P:(t + 1) * P],
                                       wgs[:, kc, :], start=(kc == 0), stop=(kc == KC - 1))
                    mm.then_inc(lgp, 1)
                tn.wait_ge(mdone, 1)
                tn.matmul(ps_pre[:], tri[:], bass.AP(m_all, 0, [[256, P], [1, 256]]),
                          start=True, stop=True).then_inc(prep, 1)
                tn.wait_ge(cpd, 1)
                tn.matmul(ps_pre[0:1, :], tri[:, 127:128],
                          bass.AP(m_all, 0, [[256, P], [1, 256]]),
                          start=True, stop=True).then_inc(prep, 1)
                tn.wait_ge(offr, 1)
                tn.matmul(ps_off[:], onesf[0:1, :], offa[0:1, :],
                          start=True, stop=True).then_inc(offbs, 1)
                for i in range(32):
                    tn.wait_ge(scat, 16 * (i + 1))
                    for _ in range(3):
                        tn.matmul(ps_pre[0:128, 0:128], tri[:, :], tri[:, :],
                                  start=True, stop=True)

        if stage == 1:
            return nc

        with ExitStack() as sC:
            EC = sC.enter_context
            wdb = EC(nc.sbuf_tensor("wdb", [P, 4, KC, D], BF16))
            h_sb = EC(nc.sbuf_tensor("h_sb", [P, 32, CL], BF16))
            eosb = EC(nc.sbuf_tensor("eosb", [P, 2, D], BF16))
            ps_h = EC(nc.psum_tensor("ps_h", [P, 2, D], F32))
            ps_eo = EC(nc.psum_tensor("ps_eo", [P, 4, 512], F32))
            ccs2 = EC(nc.semaphore("b4_cc"))
            cntl = EC(nc.semaphore("b4_cl"))
            dspt0 = EC(nc.semaphore("b4_dt0"))
            dspt1 = EC(nc.semaphore("b4_dt1"))
            dspt = [dspt0, dspt1]
            wdqa = EC(nc.semaphore("b4_wda"))
            wdqb = EC(nc.semaphore("b4_wdb"))
            wdqc = EC(nc.semaphore("b4_wdc"))
            wdqd_ = EC(nc.semaphore("b4_wdd"))
            wdq = [wdqa, wdqb, wdqc, wdqd_]
            hps = EC(nc.semaphore("b4_hp"))
            hev = EC(nc.semaphore("b4_he"))
            eops = EC(nc.semaphore("b4_eop"))
            eoev = EC(nc.semaphore("b4_eoe"))
            eost0 = EC(nc.semaphore("b4_eos0"))
            eost1 = EC(nc.semaphore("b4_eos1"))
            eost = [eost0, eost1]
            blk4 = EC(nc.Block())

            @blk4.sync
            def _(sy):
                def eo_stores(e):
                    nonlocal scount
                    for s in range(5):
                        scount += 1
                        sy.wait_ge(eoev, scount)
                        sy.dma_start(
                            out=eo_dram[e * CL + s * P:e * CL + (s + 1) * P, :],
                            in_=eosb[:, scount % 2, :]).then_inc(eost[scount % 2], 16)
                scount = 0
                for e in range(E):
                    if e >= 2:
                        sy.wait_ge(hps, 32 * (e - 1))
                    for kc in range(KC):
                        sy.dma_start(out=acta[:, kc, (e % 2) * CL:(e % 2) * CL + CL],
                                     in_=disp_dram[e * CL:(e + 1) * CL, kc * P:(kc + 1) * P],
                                     transpose=True).then_inc(dspt[e % 2], 16)
                    if e >= 1:
                        eo_stores(e - 1)
                eo_stores(E - 1)

            @blk4.gpsimd
            def _(g):
                g.collective_compute(
                    "AllGather", ALU.bypass,
                    replica_groups=[list(range(NCORE))],
                    ins=[cnt_in.ap().opt()], outs=[cnt_out.ap().opt()],
                ).then_inc(ccs2, 1)
                g.wait_ge(ccs2, 1)
                g.dma_start(out=cnts[:, :], in_=cnt_out.ap()).then_inc(cntl, 16)
                for e in range(E):
                    for kind, idx in [("wg", 0), ("wg", 1), ("wd", 0), ("wd", 1),
                                      ("wg", 2), ("wg", 3), ("wd", 2), ("wd", 3)]:
                        i = 4 * e + idx
                        if kind == "wg":
                            if i < 2:
                                continue
                            g.wait_ge(hps, 8 * (i - 1))
                            g.dma_start(
                                out=wga[:, i % 2, :, :],
                                in_=wge_quarter_ap(e, idx),
                            ).then_inc(wgq[i % 2], 16)
                        else:
                            if i >= 4:
                                g.wait_ge(eops, 3 * ((i - 4) // 4 + 1))
                            g.dma_start(
                                out=wdb[:, i % 4, :, :],
                                in_=bass.AP(wdn_h, e * FF * D + idx * KC * P * D,
                                            [[D, P], [P * D, KC], [1, D]]),
                            ).then_inc(wdq[i % 4], 16)

            @blk4.scalar
            def _(s):
                for e in range(E):
                    for fq in range(4):
                        for m in range(KC):
                            i = e * 32 + fq * 8 + m
                            s.wait_ge(hps, i + 1)
                            s.activation(h_sb[:, fq * 8 + m, :],
                                         ps_h[:, m % 2, 0:CL], AF.Gelu).then_inc(hev, 1)

            @blk4.vector
            def _(v):
                scount = 0
                for e in range(E):
                    for sgi, group in enumerate([[0, 1], [2, 3], [4]]):
                        v.wait_ge(eops, e * 3 + sgi + 1)
                        for s_ in group:
                            scount += 1
                            if scount >= 3:
                                v.wait_ge(eost[scount % 2], 16 * ((scount - 2 + 1) // 2))
                            v.tensor_copy(eosb[:, scount % 2, :],
                                          bass.AP(ps_eo, (2 * s_) % 4 * 512,
                                                  [[2048, P], [1, 1024]])
                                          ).then_inc(eoev, 1)

            @blk4.tensor
            def _(tn):
                for e in range(E):
                    tn.wait_ge(dspt[e % 2], 16 * KC * (e // 2 + 1))
                    for fq in range(4):
                        i = 4 * e + fq
                        tn.wait_ge(wgq[i % 2], 16 * (i // 2 + 1))
                        for m in range(KC):
                            mi = e * 32 + fq * 8 + m
                            if mi >= 2:
                                tn.wait_ge(hev, mi - 1)
                            for kc in range(KC):
                                st_, sp_ = kc == 0, kc == KC - 1
                                lhs = wga[:, i % 2, kc, m * P:(m + 1) * P]
                                tn.matmul(ps_h[:, m % 2, 0:512], lhs,
                                          acta[:, kc, (e % 2) * CL:(e % 2) * CL + 512],
                                          start=st_, stop=sp_)
                                mm = tn.matmul(ps_h[:, m % 2, 512:640], lhs,
                                               acta[:, kc, (e % 2) * CL + 512:(e % 2) * CL + CL],
                                               start=st_, stop=sp_)
                            mm.then_inc(hps, 1)
                    tn.wait_ge(hev, e * 32 + 32)
                    for sgi, group in enumerate([[0, 1], [2, 3], [4]]):
                        gidx = e * 3 + sgi
                        if gidx >= 1:
                            prevfull = (gidx - 1) // 3 * 5 + [0, 2, 4][(gidx - 1) % 3]
                            tn.wait_ge(eoev, prevfull + len([[0, 1], [2, 3], [4]][(gidx - 1) % 3]))
                        for kq in range(4):
                            i = 4 * e + kq
                            tn.wait_ge(wdq[i % 4], 16 * (i // 4 + 1))
                            for kc in range(KC):
                                st_ = (kq == 0 and kc == 0)
                                sp_ = (kq == 3 and kc == KC - 1)
                                for s_ in group:
                                    lhs = h_sb[:, kq * KC + kc, s_ * P:(s_ + 1) * P]
                                    tn.matmul(ps_eo[:, (2 * s_) % 4, :], lhs,
                                              wdb[:, i % 4, kc, 0:512],
                                              start=st_, stop=sp_)
                                    mm = tn.matmul(ps_eo[:, (2 * s_ + 1) % 4, :], lhs,
                                                   wdb[:, i % 4, kc, 512:1024],
                                                   start=st_, stop=sp_)
                        mm.then_inc(eops, 1)

        sPre.close()

        with ExitStack() as sD:
            ED = sD.enter_context
            eog = ED(nc.sbuf_tensor("eog", [P, 4, D], BF16))
            outsb = ED(nc.sbuf_tensor("outsb", [P, 2, D], F32))
            ps_go = ED(nc.psum_tensor("ps_go", [1, 16], F32))
            ps_gb = ED(nc.psum_tensor("ps_gb", [P, 16], F32))
            sc2b = ED(nc.sbuf_tensor("sc2b", [P, 2, D], F32))
            s5sem = ED(nc.semaphore("b5_s5"))
            gos = ED(nc.semaphore("b5_go"))
            gbs = ED(nc.semaphore("b5_gb"))
            gev = ED(nc.semaphore("b5_ge"))
            wrd = ED(nc.semaphore("b5_wr"))
            vdr5 = ED(nc.semaphore("b5_vdr"))
            gth0 = ED(nc.semaphore("b5_gt0"))
            gth1 = ED(nc.semaphore("b5_gt1"))
            gth = [gth0, gth1]
            xml0 = ED(nc.semaphore("b5_xm0"))
            xml1 = ED(nc.semaphore("b5_xm1"))
            xml = [xml0, xml1]
            cmb = ED(nc.semaphore("b5_cm"))
            osem0 = ED(nc.semaphore("b5_os0"))
            osem1 = ED(nc.semaphore("b5_os1"))
            osem = [osem0, osem1]
            blk5 = ED(nc.Block())

            @blk5.tensor
            def _(tn):
                tn.matmul(ps_go[0:1, :], prefm[:, 0:1], cnts[:, :], start=True, stop=False)
                tn.matmul(ps_go[0:1, 8:16], prefm[:, 1:2], cnts[:, 0:8],
                          start=False, stop=False)
                tn.matmul(ps_go[0:1, 8:16], prefm[:, 2:3], cnts[:, 0:8],
                          start=False, stop=True).then_inc(gos, 1)
                tn.wait_ge(gev, 1)
                tn.matmul(ps_gb[:, :], onesf[0:1, :], goffr[0:1, :],
                          start=True, stop=True).then_inc(gbs, 1)

            @blk5.vector
            def _(v):
                DR = mkdrain(v, vdr5)
                v.wait_ge(gos, 1)
                v.tensor_copy(goffr[0:1, :], ps_go[0:1, :]).then_inc(gev, 1)
                v.wait_ge(gbs, 1)
                mflat = bass.AP(m_all, 0, [[256, P], [1, 256]])
                DR(v.tensor_tensor(rsc1[:], mflat,
                                   bass.AP(ps_gb, 0, [[16, P], [8, 2], [0, NT], [1, 8]]),
                                   ALU.mult))
                DR(v.tensor_reduce(bass.AP(kg, 0, [[2 * NT, P], [1, 2 * NT]]),
                                   bass.AP(rsc1, 0, [[256, P], [8, 2 * NT], [1, 8]]),
                                   AX.X, ALU.add))
                DR(v.tensor_tensor(kg[:, :, :], kg[:, :, :], s_all[:, :, :], ALU.add))
                DR(v.tensor_scalar(bass.AP(kg, 0, [[2 * NT, P], [1, 2 * NT]]),
                                   bass.AP(kg, 0, [[2 * NT, P], [1, 2 * NT]]),
                                   CAP - 0.5, None, ALU.is_lt))
                DR(v.tensor_tensor(w1a[:], kg[:, 0, :], g1a[:], ALU.mult))
                v.tensor_tensor(w2a[:], kg[:, 1, :], g2a[:], ALU.mult).then_inc(wrd, 1)
                v.wait_ge(wrd, 1)
                for t in range(NT):
                    v.wait_ge(gth[t % 2], 32 * (t // 2 + 1))
                    v.wait_ge(xml[t % 2], 16 * (t // 2 + 1))
                    if t >= 2:
                        v.wait_ge(osem[t % 2], 16 * (t // 2))
                    v.wait_ge(s5sem, t + 1)
                    DR(v.scalar_tensor_tensor(sc1[:], eog[:, (t % 2) * 2 + 1, :],
                                              w2a[:, t:t + 1], sc2b[:, t % 2, :],
                                              ALU.mult, ALU.add))
                    v.tensor_tensor(outsb[:, t % 2, :], sc1[:], xt[:, t % 2, :],
                                    ALU.add).then_inc(cmb, 1)

            @blk5.scalar
            def _(s):
                s.wait_ge(wrd, 1)
                for t in range(NT):
                    s.wait_ge(gth[t % 2], 32 * (t // 2 + 1))
                    if t >= 2:
                        s.wait_ge(cmb, t - 1)
                    s.activation(sc2b[:, t % 2, :], eog[:, (t % 2) * 2, :],
                                 AF.Copy, scale=w1a[:, t:t + 1]).then_inc(s5sem, 1)

            @blk5.gpsimd
            def _(g):
                g.wait_ge(wrd, 1)
                for t in range(NT):
                    if t >= 2:
                        g.wait_ge(cmb, t - 1)
                    g.indirect_dma_start(
                        out=eog[:, (t % 2) * 2, :], out_offset=None,
                        in_=eo_dram.ap(), in_offset=IOA(slotu[:, 0, t:t + 1], 0),
                    ).then_inc(gth[t % 2], 16)
                    g.indirect_dma_start(
                        out=eog[:, (t % 2) * 2 + 1, :], out_offset=None,
                        in_=eo_dram.ap(), in_offset=IOA(slotu[:, 1, t:t + 1], 0),
                    ).then_inc(gth[t % 2], 16)

            @blk5.sync
            def _(sy):
                for t in range(NT):
                    if t >= 2:
                        sy.wait_ge(cmb, t - 1)
                    sy.dma_start(out=xt[:, t % 2, :],
                                 in_=xmid_dram[t * P:(t + 1) * P, :]).then_inc(xml[t % 2], 16)
                    if t >= 1:
                        sy.wait_ge(cmb, t)
                        sy.dma_start(out=out_d[(t - 1) * P:t * P, :],
                                     in_=outsb[:, (t - 1) % 2, :]).then_inc(osem[(t - 1) % 2], 16)
                sy.wait_ge(cmb, NT)
                sy.dma_start(out=out_d[(NT - 1) * P:NT * P, :],
                             in_=outsb[:, (NT - 1) % 2, :]).then_inc(osem[(NT - 1) % 2], 16)
                sy.wait_ge(osem[0], 16 * (NT // 2))
                sy.wait_ge(osem[1], 16 * (NT // 2))

    return nc


_NC_CACHE = {}


def kernel(L0_attn_x, L0_attn_Wq, L0_attn_Wk, L0_attn_Wv, L0_attn_Wg, L0_attn_bg,
           L0_attn_Wo, L0_ffn_wg, L0_ffn_w_gate, L0_ffn_w_down, stage=3, sub=9):
    x = np.ascontiguousarray(np.asarray(L0_attn_x, dtype=np.float32).reshape(4 * 4096, D))
    wq = np.ascontiguousarray(np.asarray(L0_attn_Wq, dtype=np.float32))
    wk = np.ascontiguousarray(np.asarray(L0_attn_Wk, dtype=np.float32))
    wv = np.ascontiguousarray(np.asarray(L0_attn_Wv, dtype=np.float32))
    wgt = np.ascontiguousarray(np.asarray(L0_attn_Wg, dtype=np.float32))
    wo = np.ascontiguousarray(np.asarray(L0_attn_Wo, dtype=np.float32))
    bg = np.asarray(L0_attn_bg, dtype=np.float32)
    wgr = np.ascontiguousarray(np.asarray(L0_ffn_wg, dtype=np.float32))
    wge = np.ascontiguousarray(np.asarray(L0_ffn_w_gate, dtype=np.float32).reshape(E * D, FF))
    wdn = np.ascontiguousarray(np.asarray(L0_ffn_w_down, dtype=np.float32).reshape(E * FF, D))
    bgb = np.ascontiguousarray(np.broadcast_to(bg[None, :], (P, D)))
    tri = np.ascontiguousarray(
        (np.arange(P)[:, None] <= np.arange(P)[None, :]).astype(ml_dtypes.bfloat16))
    csts = np.zeros((P, 16), dtype=np.float32)
    csts[:, 0:8] = np.arange(8, dtype=np.float32)[None, :]
    csts[:, 8] = SLOTS + np.arange(P, dtype=np.float32)
    onesf = np.ones((P, P), dtype=np.float32)

    if (stage, sub) not in _NC_CACHE:
        _NC_CACHE[(stage, sub)] = build(stage, sub)
    nc = _NC_CACHE[(stage, sub)]
    in_maps = []
    for c in range(NCORE):
        prefm = np.zeros((E, 3), dtype=np.float32)
        prefm[:, 0] = (np.arange(E) < c).astype(np.float32)
        prefm[:, 1] = 1.0
        prefm[:, 2] = -((np.arange(E) == c).astype(np.float32))
        in_maps.append(dict(
            x=np.ascontiguousarray(x[c * T:(c + 1) * T, :]),
            wq=wq, wk=wk, wv=wv, wgt=wgt, wo=wo, bgb=bgb, wgr=wgr,
            wge=wge, wdn=wdn, tri=tri, csts=csts, onesf=onesf, prefm=prefm,
        ))
    res = run_bass_kernel_spmd(nc, in_maps, core_ids=list(range(NCORE)))
    out = np.concatenate([res.results[c]["out"] for c in range(NCORE)], axis=0)
    return out.reshape(4, 4096, D)
